# version 03
# baseline (speedup 1.0000x reference)
import numpy as np
import concourse.bass as bass
import concourse.mybir as mybir
from concourse.bass_utils import run_bass_kernel_spmd

F32 = mybir.dt.float32
F32R = mybir.dt.float32r
BF16 = mybir.dt.bfloat16
ALU = mybir.AluOpType
AF = mybir.ActivationFunctionType
AX = mybir.AxisListType

T = 4096
D = 1024
FF = 4096
E = 8
G = 8
NCORE = 8
TAU1 = 256
N0 = T // 512
N1 = T // TAU1
N2 = T // 512
NTILE = T // 128

ENGINES = ("sync", "vector", "scalar", "tensor", "gpsimd")
SEMS = ["LD", "LG", "LS", "MM", "EV", "AC", "PL", "OD"]


class Prog:
    def __init__(self):
        self.counts = {s: 0 for s in SEMS}
        self.items = []
        self.tok_info = {}
        self.phase_bar = ()
        self.phase_pending = set()

    def start_phase(self):
        self.phase_bar = tuple((s, v) for s, v in self.counts.items() if v > 0)
        self.phase_pending = set(ENGINES)

    def emit(self, engine, fn, wait=(), inc=None, inc_n=1, hazard=False):
        wl = [w for w in wait if w is not None]
        if engine in self.phase_pending:
            self.phase_pending.discard(engine)
            wl = list(self.phase_bar) + wl
        tok = None
        if inc is not None:
            self.counts[inc] += inc_n
            tok = (inc, self.counts[inc])
            self.tok_info[tok] = (engine, hazard)
        self.items.append((engine, fn, tuple(wl), (inc, inc_n) if inc else None))
        return tok

    def replay(self, block, sems):
        for eng_name in ENGINES:
            items = [(fn, waits, inc) for (e, fn, waits, inc) in self.items if e == eng_name]
            if not items:
                continue

            def body(eng, _items=items, _eng=eng_name):
                hi = {}
                for fn, waits, inc in _items:
                    for (s, v) in waits:
                        if hi.get(s, -1) >= v:
                            continue
                        eng.wait_ge(sems[s], v)
                        hi[s] = v
                    instr = fn(eng)
                    if inc is not None:
                        instr.then_inc(sems[inc[0]], inc[1])
            getattr(block, eng_name)(body)


def build_program(taps=(), upto="all"):
    nc = bass.Bass("TRN2", target_bir_lowering=False)
    P = Prog()

    def kind(name):
        return "ExternalOutput" if name in taps else "Internal"

    x_d = nc.dram_tensor("x", [T, D], F32, kind="ExternalInput").ap()
    out_d = nc.dram_tensor("out", [T, D], F32, kind="ExternalOutput").ap()
    w_d = {}
    for nm, shape in [("wlnz", [D, D]), ("wdt", [D, D]), ("wy", [D, D]),
                      ("wyact", [D, D]), ("wfc", [FF, D]), ("wfca", [FF, D]),
                      ("wout", [D, FF])]:
        w_d[nm] = nc.dram_tensor(nm, shape, F32, kind="ExternalInput").ap()
    exps_d = nc.dram_tensor("exps", [E, D, D], F32R, kind="ExternalInput").ap()
    gw_d = nc.dram_tensor("gw", [D, E], F32R, kind="ExternalInput").ap()
    vec_d = {}
    for nm, n in [("blnz", D), ("bdt", D), ("by", D), ("byact", D), ("bfc", FF),
                  ("bfca", FF), ("bout", D), ("gsio", D), ("gffn", D), ("h0", D)]:
        vec_d[nm] = nc.dram_tensor(nm, [n], F32, kind="ExternalInput").ap()
    boutr_d = nc.dram_tensor("boutr", [1, D], F32R, kind="ExternalInput").ap()
    ident_d = nc.dram_tensor("ident", [128, 128], F32, kind="ExternalInput").ap()
    identr_d = nc.dram_tensor("identr", [128, 128], F32R, kind="ExternalInput").ap()
    ones1_d = nc.dram_tensor("ones1", [1, 128], F32R, kind="ExternalInput").ap()
    onesc_d = nc.dram_tensor("onesc", [128, 1], F32R, kind="ExternalInput").ap()

    xnF_d = nc.dram_tensor("xnF", [G, 128, T], F32R, kind=kind("xnF")).ap()
    xnB_d = nc.dram_tensor("xnB", [G, 128, T], BF16, kind=kind("xnB")).ap()
    x2F_d = nc.dram_tensor("x2F", [G, 128, T], F32R, kind=kind("x2F")).ap()
    xaT_d = nc.dram_tensor("xaT", [NTILE, 128, D], F32, kind=kind("xaT")).ap()
    sT_d = nc.dram_tensor("sT", [NTILE, 128, D], F32, kind=kind("sT")).ap()
    wfcT_d = nc.dram_tensor("wfcT", [G, 128, FF], F32R, kind=kind("wfcT")).ap()
    wfcaT_d = nc.dram_tensor("wfcaT", [G, 128, FF], F32R, kind=kind("wfcaT")).ap()
    woutT_d = nc.dram_tensor("woutT", [FF // 128, 128, D], F32R, kind=kind("woutT")).ap()
    expB_d = nc.dram_tensor("expB", [E, G, 128, D], BF16, kind=kind("expB")).ap()
    dbg_d = {}
    for nm, shape, dt_ in [("gdbg", [8, T], F32), ("windbg", [128, E, NTILE], F32),
                           ("hdbg", [G, 128, T], F32), ("seldbg", [128, E, NTILE], F32),
                           ("taudbg", [128, 8], F32), ("gtokdbg", [128, 256], F32),
                           ("cntdbg", [128, 16], F32), ("yaccdbg", [128, 1024], F32),
                           ("expbdbg", [128, 8192], BF16)]:
        if nm in taps:
            dbg_d[nm] = nc.dram_tensor(nm, shape, dt_, kind="ExternalOutput").ap()

    psums = [nc.psum_tensor(f"PS{i}", [128, 512], F32).__enter__() for i in range(8)]
    semh = {s: nc.semaphore(s).__enter__() for s in SEMS}

    tens = {}
    dead = {}
    seq = [0]

    def sb(name, cols, dtype=F32, rows=128):
        cm = nc.sbuf_tensor("sb_" + name, [rows, cols], dtype)
        h = cm.__enter__()
        seq[0] += 1
        tens[name] = (h, cm, dtype, rows, seq[0])

    def drop(*names):
        for name in sorted(names, key=lambda n: -tens[n][4]):
            rec = tens.pop(name)
            dead[name] = rec
            rec[1].__exit__(None, None, None)

    def sbv(name, rows=None, row0=0):
        h, _, dt_, r, _s = tens[name] if name in tens else dead[name]
        if rows is None and row0 == 0:
            return h[:]
        return h[row0:(row0 + rows) if rows is not None else r, :]

    def slc(name, c0, c1, rows=None, row0=0):
        h, _, dt_, r, _s = tens[name] if name in tens else dead[name]
        r1 = (row0 + rows) if rows is not None else r
        return h[row0:r1, c0:c1]

    sb("ident", 128)
    sb("identr", 128, F32R)
    sb("ones1", 128, F32R, rows=1)
    sb("onesc", 1, F32R)
    sb("gwsb", 64, F32R)
    colspec = {"blnz_c": ("blnz", 8), "bdt_c": ("bdt", 8), "nbdt_c": (None, 8),
               "by_c": ("by", 8), "byact_c": ("byact", 8), "bfc_c": ("bfc", 32),
               "bfca_c": ("bfca", 32), "gsio_c": ("gsio", 8), "gffn_c": ("gffn", 8),
               "h0_c": ("h0", 8)}
    for nm, (_, k) in colspec.items():
        sb(nm, k)
    sb("win", 256)

    P.emit("sync", lambda s: s.dma_start(out=sbv("ident"), in_=ident_d), inc="LD", inc_n=16)
    P.emit("sync", lambda s: s.dma_start(out=sbv("identr"), in_=identr_d), inc="LD", inc_n=16)
    P.emit("sync", lambda s: s.dma_start(out=sbv("ones1"), in_=ones1_d), inc="LD", inc_n=16)
    P.emit("sync", lambda s: s.dma_start(out=sbv("onesc"), in_=onesc_d), inc="LD", inc_n=16)
    for g in range(G):
        P.emit("sync", lambda s, g=g: s.dma_start(
            out=slc("gwsb", g * 8, g * 8 + 8), in_=gw_d[g * 128:(g + 1) * 128, :]),
            inc="LD", inc_n=16)
    for nm, (src, k) in colspec.items():
        if src is None:
            continue
        for j in range(k):
            P.emit("sync", lambda s, nm=nm, src=src, j=j: s.dma_start(
                out=slc(nm, j, j + 1),
                in_=vec_d[src][j * 128:(j + 1) * 128].rearrange("(p o) -> p o", o=1)),
                inc="LD", inc_n=16)
    tok_consts = ("LD", P.counts["LD"])
    tok_nbdt = P.emit("vector", lambda v: v.tensor_scalar(
        out=sbv("nbdt_c"), in0=sbv("bdt_c"), scalar1=-1.0, scalar2=None, op0=ALU.mult),
        wait=[tok_consts], inc="EV")

    P.start_phase()
    sb("wstage", 8192)
    sb("wbnc0", 512, F32R)
    sb("wbnc1", 512, F32R)
    bank_free = {}
    stage_free = None

    stg_free = {0: None, 1: None}
    for src, dstT in (("wfc", wfcT_d), ("wfca", wfcaT_d)):
        for fb in range(FF // 128):
            par = fb % 2
            tok_stg = P.emit("sync", lambda s, src=src, fb=fb, par=par: s.dma_start(
                out=slc("wstage", par * 1024, par * 1024 + 1024),
                in_=w_d[src][fb * 128:(fb + 1) * 128, :]),
                wait=[stg_free[par]], inc="LD", inc_n=16)
            last_tp = None
            for half in range(2):
                bi = 2 + half
                bn = "wbnc0" if half == 0 else "wbnc1"
                tk = None
                for gh in range(4):
                    g = half * 4 + gh
                    tk = P.emit("tensor", lambda t, g=g, gh=gh, bi=bi, par=par: t.transpose(
                        psums[bi][:, gh * 128:(gh + 1) * 128],
                        slc("wstage", par * 1024 + g * 128, par * 1024 + g * 128 + 128),
                        sbv("ident")),
                        wait=[bank_free.get(bi), tok_stg] if gh == 0 else (),
                        inc="MM" if gh == 3 else None)
                tok_ev = P.emit("vector", lambda v, bi=bi, bn=bn: v.tensor_copy(
                    sbv(bn), psums[bi][:]), wait=[tk, bank_free.get((bn,))], inc="EV")
                bank_free[bi] = tok_ev
                dmt = None
                for gh in range(4):
                    g = half * 4 + gh
                    dmt = P.emit("gpsimd", lambda gp, g=g, gh=gh, fb=fb, bn=bn, dstT=dstT:
                                 gp.dma_start(out=dstT[g, :, fb * 128:(fb + 1) * 128],
                                              in_=slc(bn, gh * 128, gh * 128 + 128)),
                                 wait=[tok_ev], inc="LG", inc_n=16)
                bank_free[(bn,)] = dmt
                last_tp = tk
            stg_free[par] = last_tp

    for db in range(D // 128):
        par = db % 2
        tok_stg = P.emit("sync", lambda s, db=db, par=par: s.dma_start(
            out=slc("wstage", par * 4096, par * 4096 + 4096),
            in_=w_d["wout"][db * 128:(db + 1) * 128, :]),
            wait=[stg_free.get(("wo", par))] if ("wo", par) in stg_free
            else [stg_free[0], stg_free[1]], inc="LD", inc_n=16)
        last_tp = None
        for ffc in range(FF // 128):
            half = ffc % 2
            bi = 4 + half
            bn = "wbnc0" if half == 0 else "wbnc1"
            tok_tp = P.emit("tensor", lambda t, db=db, ffc=ffc, bi=bi, par=par: t.transpose(
                psums[bi][:, 0:128],
                slc("wstage", par * 4096 + ffc * 128, par * 4096 + ffc * 128 + 128),
                sbv("ident")),
                wait=[bank_free.get(bi), tok_stg], inc="MM")
            tok_ev = P.emit("vector", lambda v, bi=bi, bn=bn: v.tensor_copy(
                slc(bn, 0, 128), psums[bi][:, 0:128]),
                wait=[tok_tp, bank_free.get((bn,))], inc="EV")
            bank_free[bi] = tok_ev
            dmt = P.emit("gpsimd", lambda gp, ffc=ffc, db=db, bn=bn: gp.dma_start(
                out=woutT_d[ffc, :, db * 128:(db + 1) * 128], in_=slc(bn, 0, 128)),
                wait=[tok_ev], inc="LG", inc_n=16)
            bank_free[(bn,)] = dmt
            last_tp = tok_tp
        stg_free[("wo", par)] = last_tp

    drop("wstage", "wbnc0", "wbnc1")

    P.start_phase()
    sb("rsioc", 32, F32R)
    sb("gatesT", T, rows=8)
    sb("ewT", T, rows=8)
    sb("rowA", T, rows=1)
    sb("rowB", T, rows=1)
    sb("xtokA", 1024)
    sb("xtokB", 1024)
    sb("bnscr", 16)
    sb("mv", 2)
    sb("lncol", 8)
    sb("xn_s", 1024)
    for g in range(G):
        sb(f"xnb{g}", 512, F32R)
    sb("expstgA", 1024, F32R)
    sb("expstgB", 1024, F32R)

    def gates_ap(r0=0, r1=8, c0=0, c1=T):
        return slc("gatesT", c0, c1, rows=r1 - r0, row0=r0)

    def ew_ap(r0=0, r1=8, c0=0, c1=T):
        return slc("ewT", c0, c1, rows=r1 - r0, row0=r0)

    exp_chunks = [(e, g) for e in range(E) for g in range(G)]
    exp_state = {"i": 0, "A": None, "B": None}

    def emit_expert_casts(n):
        for _ in range(n):
            i = exp_state["i"]
            if i >= len(exp_chunks):
                return
            e, g = exp_chunks[i]
            key = "A" if i % 2 == 0 else "B"
            stgn = "expstg" + key
            tok_l = P.emit("sync", lambda s, e=e, g=g, stgn=stgn: s.dma_start(
                out=sbv(stgn), in_=exps_d[e, g * 128:(g + 1) * 128, :]),
                wait=[exp_state[key]], inc="LD", inc_n=16)
            tok_c = P.emit("gpsimd", lambda gp, e=e, g=g, stgn=stgn: gp.dma_start(
                out=expB_d[e, g], in_=sbv(stgn)),
                wait=[tok_l], inc="LG", inc_n=16)
            exp_state[key] = tok_c
            exp_state["i"] = i + 1

    xuse = {"A": None, "B": None}
    bfree = {}
    tok_gates_fin = None
    tok_xnF = {}
    for ti in range(N0):
        emit_expert_casts(8)
        evtoks = [None] * G
        for s4 in range(4):
            sub = ti * 4 + s4
            key = "A" if s4 % 2 == 0 else "B"
            xt = "xtok" + key
            tok_x = P.emit("sync", lambda s, sub=sub, xt=xt: s.dma_start(
                out=sbv(xt), in_=x_d[sub * 128:(sub + 1) * 128, :]),
                wait=[xuse[key]], inc="LD", inc_n=16)
            tb1 = P.emit("vector", lambda v, xt=xt: v.bn_stats(
                out=slc("bnscr", 0, 6), in_=slc(xt, 0, 512)),
                wait=[tok_x], inc="EV", hazard=True)
            tb2 = P.emit("vector", lambda v, xt=xt: v.bn_stats(
                out=slc("bnscr", 6, 12), in_=slc(xt, 512, 1024)),
                wait=[tok_x], inc="EV", hazard=True)
            tb3 = P.emit("vector", lambda v: v.bn_aggr(
                out=sbv("mv"), in_=slc("bnscr", 0, 12)), wait=[tb1, tb2], inc="EV", hazard=True)
            t1 = P.emit("vector", lambda v: v.tensor_scalar(
                out=slc("lncol", 0, 1), in0=slc("mv", 1, 2), scalar1=1e-5, scalar2=None,
                op0=ALU.add), wait=[tb3], inc="EV", hazard=True)
            t2 = P.emit("scalar", lambda sc: sc.activation(
                out=slc("lncol", 1, 2), in_=slc("lncol", 0, 1), func=AF.Ln),
                wait=[t1], inc="AC", hazard=True)
            t3 = P.emit("scalar", lambda sc: sc.activation(
                out=slc("lncol", 2, 3), in_=slc("lncol", 1, 2), func=AF.Exp, scale=-0.5),
                wait=[t2], inc="AC", hazard=True)
            t_xn = P.emit("vector", lambda v, xt=xt: v.tensor_scalar(
                out=sbv("xn_s"), in0=sbv(xt), scalar1=slc("mv", 0, 1),
                scalar2=slc("lncol", 2, 3), op0=ALU.subtract, op1=ALU.mult),
                wait=[t3, bfree.get("xn_s")], inc="EV")
            xuse[key] = t_xn
            t4 = P.emit("vector", lambda v: v.tensor_tensor(
                out=slc("lncol", 3, 4), in0=slc("mv", 1, 2), in1=slc("lncol", 2, 3),
                op=ALU.mult), wait=[t3], inc="EV", hazard=True)
            t5 = P.emit("vector", lambda v: v.tensor_scalar(
                out=slc("lncol", 4, 5), in0=slc("lncol", 3, 4), scalar1=slc("lncol", 2, 3),
                scalar2=1e-6, op0=ALU.mult, op1=ALU.add), wait=[t4], inc="EV", hazard=True)
            t6 = P.emit("scalar", lambda sc: sc.activation(
                out=slc("lncol", 5, 6), in_=slc("lncol", 4, 5), func=AF.Ln),
                wait=[t5], inc="AC", hazard=True)
            P.emit("scalar", lambda sc, sub=sub: sc.activation(
                out=slc("rsioc", sub, sub + 1), in_=slc("lncol", 5, 6), func=AF.Exp,
                scale=-0.5), wait=[t6], inc="AC", hazard=True)
            tk_last = None
            for g in range(G):
                tk_last = P.emit("tensor", lambda t, g=g, s4=s4: t.transpose(
                    psums[g][:, s4 * 128:(s4 + 1) * 128],
                    slc("xn_s", g * 128, g * 128 + 128), sbv("ident")),
                    wait=[t_xn] + ([bfree.get(("p0", g))] if s4 == 0 else []),
                    inc="MM" if s4 == 3 else None)
                if s4 == 3:
                    tok_ev = P.emit("vector", lambda v, g=g: v.tensor_copy(
                        sbv(f"xnb{g}"), psums[g][:]),
                        wait=[tk_last, bfree.get(("xnb", g))], inc="EV")
                    evtoks[g] = tok_ev
                    d1 = P.emit("gpsimd", lambda gp, g=g, ti=ti: gp.dma_start(
                        out=xnF_d[g, :, ti * 512:(ti + 1) * 512], in_=sbv(f"xnb{g}")),
                        wait=[tok_ev], inc="LG", inc_n=16)
                    d2 = P.emit("gpsimd", lambda gp, g=g, ti=ti: gp.dma_start(
                        out=xnB_d[g, :, ti * 512:(ti + 1) * 512], in_=sbv(f"xnb{g}")),
                        wait=[tok_ev], inc="LG", inc_n=16)
                    tok_xnF[(ti, g)] = (d1, d2)
            bfree["xn_s"] = tk_last
        tok_mm = None
        for g in range(G):
            tok_mm = P.emit("tensor", lambda t, g=g: t.matmul(
                psums[0][0:8, :], slc("gwsb", g * 8, g * 8 + 8), sbv(f"xnb{g}"),
                start=(g == 0), stop=(g == G - 1)),
                wait=[evtoks[g], tok_consts], inc="MM" if g == G - 1 else None)
        tok_gev = P.emit("vector", lambda v, ti=ti: v.tensor_copy(
            gates_ap(0, 8, ti * 512, (ti + 1) * 512), psums[0][0:8, :]),
            wait=[tok_mm], inc="EV")
        bfree[("p0", 0)] = tok_gev
        for g in range(1, G):
            bfree[("p0", g)] = evtoks[g]
        for g in range(G):
            bfree[("xnb", g)] = tok_mm
        tok_gates_fin = tok_gev
    emit_expert_casts(len(exp_chunks))
    tok_expB = [exp_state["A"], exp_state["B"]]

    drop("xtokA", "xtokB", "bnscr", "mv", "lncol", "xn_s",
         "expstgA", "expstgB", *(f"xnb{g}" for g in range(G)))

    if upto == "ph0":
        return _close(nc, P, semh)

    P.start_phase()
    sb("g_tok", 256)
    sb("skt", 256)
    sb("sks", 128)
    sb("pm8", 128, rows=8)
    sb("mx8", 1, rows=8)
    sb("mxrowR", 8, F32R, rows=1)
    sb("lseR", 8, F32R, rows=1)
    sb("prowR", 8, F32R)
    sb("selm", 256)
    sb("hardm", 256)
    sb("bis", 48)
    sb("bmsk", 8, mybir.dt.int32)
    sb("cntR", 8, F32R)
    sb("crR", 8, F32R, rows=1)

    def gq(name):
        return sbv(name).rearrange("p (e q) -> p e q", e=8)

    def gqT(name):
        return sbv(name).rearrange("p (e q) -> p q e", e=8)

    t_cl = P.emit("vector", lambda v: v.tensor_scalar(
        out=gates_ap(), in0=gates_ap(), scalar1=1e-6, scalar2=None, op0=ALU.max),
        wait=[tok_gates_fin], inc="EV", hazard=True)
    tok_g0 = P.emit("scalar", lambda sc: sc.activation(
        out=gates_ap(), in_=gates_ap(), func=AF.Ln), wait=[t_cl], inc="AC", hazard=True)
    tts = None
    tok_gtok = {}
    for t8 in range(NTILE):
        bi = 6 + (t8 // 16) % 2
        tts = P.emit("tensor", lambda t, t8=t8, bi=bi: t.transpose(
            psums[bi][:, (t8 % 16) * 8:(t8 % 16) * 8 + 8],
            gates_ap(0, 8, t8 * 128, (t8 + 1) * 128), sbv("ident")[0:8, 0:8]),
            wait=[tok_g0] + ([bfree.get(("gt", bi))] if t8 % 16 == 0 else []),
            inc="MM" if t8 % 16 == 15 else None)
        if t8 % 16 == 15:
            first = t8 - 15
            tok_ge = P.emit("vector", lambda v, bi=bi, first=first: v.tensor_copy(
                sbv("g_tok").rearrange("p (e q) -> p q e", e=8)[:, first:first + 16, :],
                psums[bi][:, 0:128].rearrange("p (q e) -> p q e", e=8)),
                wait=[tts], inc="EV")
            bfree[("gt", bi)] = tok_ge
            tok_gtok[t8 // 16] = tok_ge

    def skslc(c0, c1, rows=None):
        return slc("sks", c0, c1, rows=rows)

    tok_g = None
    for it in range(8):
        w0 = [tok_g] if tok_g else [tok_gtok[0], tok_gtok[1]]
        t_pm = P.emit("vector", lambda v: v.tensor_reduce(
            out=skslc(112, 120), in_=gq("g_tok"), axis=AX.X, op=ALU.max),
            wait=w0, inc="EV", hazard=True)
        t_tp1 = P.emit("tensor", lambda t: t.transpose(
            psums[5][0:8, 0:128], skslc(112, 120), sbv("ident")),
            wait=[t_pm, bfree.get("snk5")], inc="MM")
        t_pm8 = P.emit("vector", lambda v: v.tensor_copy(sbv("pm8"), psums[5][0:8, 0:128]),
                       wait=[t_tp1], inc="EV", hazard=True)
        t_mx8 = P.emit("vector", lambda v: v.tensor_reduce(
            out=sbv("mx8"), in_=sbv("pm8"), axis=AX.X, op=ALU.max),
            wait=[t_pm8], inc="EV", hazard=True)
        t_tp2 = P.emit("tensor", lambda t: t.transpose(
            psums[5][0:1, 128:136], sbv("mx8"), sbv("ident")[0:8, 0:8]),
            wait=[t_mx8], inc="MM")
        t_mxr = P.emit("vector", lambda v: v.tensor_copy(
            sbv("mxrowR"), psums[5][0:1, 128:136]),
            wait=[t_tp2], inc="EV", hazard=True)
        t_bc1 = P.emit("tensor", lambda t: t.matmul(
            psums[5][:, 136:144], sbv("ones1"), sbv("mxrowR"), start=True, stop=True),
            wait=[t_mxr], inc="MM")
        t_mxbc = P.emit("vector", lambda v: v.tensor_copy(skslc(0, 8), psums[5][:, 136:144]),
                        wait=[t_bc1], inc="EV", hazard=True)
        t_sub = P.emit("vector", lambda v: v.tensor_tensor(
            out=gq("skt"), in0=gq("g_tok"),
            in1=skslc(0, 8).unsqueeze(2).broadcast_to((128, 8, 32)), op=ALU.subtract),
            wait=[t_mxbc], inc="EV", hazard=True)
        t_exp = P.emit("scalar", lambda sc: sc.activation(
            out=sbv("skt"), in_=sbv("skt"), func=AF.Exp), wait=[t_sub], inc="AC", hazard=True)

        def _prow_fn(v):
            with nc.allow_low_precision(reason="partial lse sums; f32r rounding ~1e-4"):
                return v.tensor_reduce(out=sbv("prowR"), in_=gq("skt"), axis=AX.X, op=ALU.add)
        t_pr = P.emit("vector", _prow_fn, wait=[t_exp], inc="EV", hazard=True)
        t_sm = P.emit("tensor", lambda t: t.matmul(
            psums[5][0:1, 144:152], sbv("onesc"), sbv("prowR"), start=True, stop=True),
            wait=[t_pr], inc="MM")
        t_ln = P.emit("scalar", lambda sc: sc.activation(
            out=skslc(120, 128, rows=1), in_=psums[5][0:1, 144:152], func=AF.Ln),
            wait=[t_sm], inc="AC", hazard=True)
        t_lse = P.emit("vector", lambda v: v.tensor_tensor(
            out=sbv("lseR"), in0=skslc(120, 128, rows=1), in1=sbv("mxrowR"), op=ALU.add),
            wait=[t_ln], inc="EV", hazard=True)
        t_bc2 = P.emit("tensor", lambda t: t.matmul(
            psums[5][:, 152:160], sbv("ones1"), sbv("lseR"), start=True, stop=True),
            wait=[t_lse], inc="MM")
        t_lsebc = P.emit("vector", lambda v: v.tensor_copy(skslc(8, 16), psums[5][:, 152:160]),
                         wait=[t_bc2], inc="EV", hazard=True)
        bfree["snk5"] = t_lsebc
        t_nn = P.emit("vector", lambda v: v.tensor_tensor(
            out=gq("g_tok"), in0=gq("g_tok"),
            in1=skslc(8, 16).unsqueeze(2).broadcast_to((128, 8, 32)), op=ALU.subtract),
            wait=[t_lsebc], inc="EV", hazard=True)
        t_em = P.emit("vector", lambda v: v.tensor_reduce(
            out=skslc(16, 48), in_=gqT("g_tok"), axis=AX.X, op=ALU.max),
            wait=[t_nn], inc="EV", hazard=True)
        t_es = P.emit("vector", lambda v: v.tensor_tensor(
            out=gqT("skt"), in0=gqT("g_tok"),
            in1=skslc(16, 48).unsqueeze(2).broadcast_to((128, 32, 8)), op=ALU.subtract),
            wait=[t_em], inc="EV", hazard=True)
        t_ee = P.emit("scalar", lambda sc: sc.activation(
            out=sbv("skt"), in_=sbv("skt"), func=AF.Exp), wait=[t_es], inc="AC", hazard=True)
        t_esum = P.emit("vector", lambda v: v.tensor_reduce(
            out=skslc(48, 80), in_=gqT("skt"), axis=AX.X, op=ALU.add),
            wait=[t_ee], inc="EV", hazard=True)
        t_eln = P.emit("scalar", lambda sc: sc.activation(
            out=skslc(80, 112), in_=skslc(48, 80), func=AF.Ln),
            wait=[t_esum], inc="AC", hazard=True)
        t_else = P.emit("vector", lambda v: v.tensor_tensor(
            out=skslc(80, 112), in0=skslc(80, 112), in1=skslc(16, 48), op=ALU.add),
            wait=[t_eln], inc="EV", hazard=True)
        tok_g = P.emit("vector", lambda v: v.tensor_tensor(
            out=gqT("g_tok"), in0=gqT("g_tok"),
            in1=skslc(80, 112).unsqueeze(2).broadcast_to((128, 32, 8)), op=ALU.subtract),
            wait=[t_else], inc="EV", hazard=True)
    tok_gfin = P.emit("scalar", lambda sc: sc.activation(
        out=sbv("g_tok"), in_=sbv("g_tok"), func=AF.Exp), wait=[tok_g], inc="AC", hazard=True)
    tok_gtok = {0: tok_gfin, 1: tok_gfin}

    if "gtokdbg" in dbg_d:
        P.emit("gpsimd", lambda gp: gp.dma_start(out=dbg_d["gtokdbg"], in_=sbv("g_tok")),
               wait=[tok_gtok[0], tok_gtok[1]], inc="LG", inc_n=16)
    t_lo = P.emit("vector", lambda v: v.tensor_scalar(
        out=slc("bis", 0, 8), in0=slc("g_tok", 0, 8), scalar1=0.0, scalar2=None,
        op0=ALU.mult), wait=[tok_gtok[0], tok_gtok[1]], inc="EV", hazard=True)
    t_hi = P.emit("vector", lambda v: v.tensor_scalar(
        out=slc("bis", 8, 16), in0=slc("g_tok", 0, 8), scalar1=0.0, scalar2=1.001,
        op0=ALU.mult, op1=ALU.add), inc="EV", hazard=True)
    prev = [t_lo, t_hi]
    for bit in range(35):
        t_mid = P.emit("vector", lambda v: v.tensor_tensor(
            out=slc("bis", 16, 24), in0=slc("bis", 0, 8), in1=slc("bis", 8, 16), op=ALU.add),
            wait=prev, inc="EV", hazard=True)
        t_mid = P.emit("vector", lambda v: v.tensor_scalar(
            out=slc("bis", 16, 24), in0=slc("bis", 16, 24), scalar1=0.5, scalar2=None,
            op0=ALU.mult), wait=[t_mid], inc="EV", hazard=True)
        if bit == 0 and "cntdbg" in dbg_d:
            t_snap = P.emit("vector", lambda v: v.tensor_copy(
                slc("bis", 40, 48), slc("bis", 16, 24)), wait=[t_mid], inc="EV", hazard=True)
        t_cmp = P.emit("vector", lambda v: v.tensor_tensor(
            out=gq("selm"), in0=gq("g_tok"),
            in1=slc("bis", 16, 24).unsqueeze(2).broadcast_to((128, 8, 32)),
            op=ALU.is_gt), wait=[t_mid], inc="EV", hazard=True)

        def _cnt_fn(v):
            with nc.allow_low_precision(reason="counts near 512 exact in f32r"):
                return v.tensor_reduce(out=sbv("cntR"), in_=gq("selm"),
                                       axis=AX.X, op=ALU.add)
        t_cnt = P.emit("vector", _cnt_fn, wait=[t_cmp], inc="EV", hazard=True)
        t_m1 = P.emit("tensor", lambda t: t.matmul(
            psums[5][0:1, 0:8], sbv("onesc"), sbv("cntR"),
            start=True, stop=True), wait=[t_cnt], inc="MM")
        t_cr = P.emit("vector", lambda v: v.tensor_copy(
            sbv("crR"), psums[5][0:1, 0:8]), wait=[t_m1], inc="EV", hazard=True)
        t_m2 = P.emit("tensor", lambda t: t.matmul(
            psums[5][:, 8:16], sbv("ones1"), sbv("crR"),
            start=True, stop=True), wait=[t_cr], inc="MM")
        if bit == 0 and "cntdbg" in dbg_d:
            tdb = P.emit("vector", lambda v: v.tensor_copy(
                slc("bis", 32, 40), psums[5][:, 8:16]), wait=[t_m2], inc="EV", hazard=True)
            tdb2 = P.emit("gpsimd", lambda gp: gp.dma_start(
                out=dbg_d["cntdbg"][:, 0:8], in_=slc("bis", 32, 40)),
                wait=[tdb], inc="LG", inc_n=16)
            P.emit("gpsimd", lambda gp: gp.dma_start(
                out=dbg_d["cntdbg"][:, 8:16], in_=slc("bis", 40, 48)),
                wait=[tdb2, ("EV", P.counts["EV"])], inc="LG", inc_n=16)
        t_msk = P.emit("vector", lambda v: v.tensor_scalar(
            out=sbv("bmsk"), in0=psums[5][:, 8:16], scalar1=511.5, scalar2=None,
            op0=ALU.is_ge), wait=[t_m2], inc="EV", hazard=True)
        t_lo = P.emit("vector", lambda v: v.select(
            out=slc("bis", 0, 8), mask=sbv("bmsk"), on_true=slc("bis", 16, 24),
            on_false=slc("bis", 0, 8)), wait=[t_msk], inc="EV", hazard=True)
        t_hi = P.emit("vector", lambda v: v.select(
            out=slc("bis", 8, 16), mask=sbv("bmsk"), on_true=slc("bis", 8, 16),
            on_false=slc("bis", 16, 24)), wait=[t_msk], inc="EV", hazard=True)
        prev = [t_lo, t_hi]
    if "taudbg" in dbg_d:
        P.emit("gpsimd", lambda gp: gp.dma_start(out=dbg_d["taudbg"], in_=slc("bis", 0, 8)),
               wait=[t_lo], inc="LG", inc_n=16)
    t_sel = P.emit("vector", lambda v: v.tensor_tensor(
        out=gq("selm"), in0=gq("g_tok"),
        in1=slc("bis", 0, 8).unsqueeze(2).broadcast_to((128, 8, 32)), op=ALU.is_gt),
        wait=[t_lo], inc="EV", hazard=True)
    t_hard = P.emit("vector", lambda v: v.tensor_scalar(
        out=sbv("hardm"), in0=sbv("g_tok"), scalar1=0.5, scalar2=None, op0=ALU.is_gt),
        wait=[tok_gtok[0], tok_gtok[1]], inc="EV", hazard=True)
    t_nsel = P.emit("vector", lambda v: v.tensor_reduce(
        out=slc("win", 0, 32), in_=gqT("selm"), axis=AX.X, op=ALU.add),
        wait=[t_sel], inc="EV", hazard=True)
    t_ind = P.emit("vector", lambda v: v.tensor_scalar(
        out=slc("win", 32, 64), in0=slc("win", 0, 32), scalar1=1.5, scalar2=None,
        op0=ALU.is_lt), wait=[t_nsel], inc="EV", hazard=True)
    t_w1 = P.emit("vector", lambda v: v.tensor_tensor(
        out=sbv("hardm"), in0=sbv("hardm"), in1=sbv("selm"), op=ALU.mult),
        wait=[t_hard, t_sel], inc="EV", hazard=True)
    t_win = P.emit("vector", lambda v: v.tensor_tensor(
        out=gqT("win"), in0=gqT("hardm"),
        in1=slc("win", 32, 64).unsqueeze(2).broadcast_to((128, 32, 8)), op=ALU.mult),
        wait=[t_w1, t_ind], inc="EV", hazard=True)
    if "windbg" in dbg_d:
        P.emit("gpsimd", lambda gp: gp.dma_start(out=dbg_d["windbg"], in_=gq("win")),
               wait=[t_win], inc="LG", inc_n=16)
    if "seldbg" in dbg_d:
        P.emit("gpsimd", lambda gp: gp.dma_start(out=dbg_d["seldbg"], in_=gq("selm")),
               wait=[t_sel, t_w1], inc="LG", inc_n=16)
    drop("g_tok", "skt", "sks", "pm8", "mx8", "mxrowR", "lseR", "prowR",
         "selm", "hardm", "bis", "bmsk", "cntR", "crR",
         "gatesT", "ewT", "rowA", "rowB")

    if upto == "sink":
        return _close(nc, P, semh)

    P.start_phase()
    for nm in ("wlnzT", "wdtT", "wyactT", "wyT"):
        sb(nm, G * 1024, F32R)
    sb("wstage2", 8192)
    small = [("wlnzT", "wlnz", "gsio_c"), ("wdtT", "wdt", "gsio_c"),
             ("wyactT", "wyact", "gsio_c"), ("wyT", "wy", None)]
    stage_free2 = None
    bank_free2 = {}
    for nm, src, scale in small:
        tok_stg = P.emit("sync", lambda s, src=src: s.dma_start(
            out=sbv("wstage2").rearrange("p (fb d) -> p fb d", fb=8),
            in_=w_d[src].rearrange("(fb p) d -> p fb d", p=128)),
            wait=[stage_free2, tok_consts], inc="LD", inc_n=16)
        last_tp = None
        for g in range(G):
            for half in range(2):
                bi = half
                tk = None
                for fbh in range(4):
                    fb = half * 4 + fbh
                    tk = P.emit("tensor", lambda t, g=g, fb=fb, fbh=fbh, bi=bi: t.transpose(
                        psums[bi][:, fbh * 128:(fbh + 1) * 128],
                        slc("wstage2", fb * 1024 + g * 128, fb * 1024 + g * 128 + 128),
                        sbv("ident")),
                        wait=[bank_free2.get(bi), tok_stg] if fbh == 0 else (),
                        inc="MM" if fbh == 3 else None)

                def mk_ev(nm=nm, g=g, half=half, bi=bi, scale=scale):
                    def fn(v):
                        dst = slc(nm, g * 1024 + half * 512, g * 1024 + half * 512 + 512)
                        if scale is None:
                            return v.tensor_copy(dst, psums[bi][:])
                        return v.tensor_scalar(out=dst, in0=psums[bi][:],
                                               scalar1=slc(scale, g, g + 1), scalar2=None,
                                               op0=ALU.mult)
                    return fn
                bank_free2[bi] = P.emit("vector", mk_ev(), wait=[tk], inc="EV")
                last_tp = tk
        stage_free2 = last_tp
    drop("wstage2")

    P.start_phase()
    sb("xt1A", 2048, F32R)
    sb("xt1B", 2048, F32R)
    sb("rsrowS", 256, F32R, rows=1)
    sb("rbc", 256)
    sb("zs", 256)
    sb("ds", 256)
    sb("dscr", 1024)
    sb("hcur", 2048)
    sb("hinit", 8)
    sb("hR", 2048, F32R)
    sb("sact", 256)
    sb("xa", 2048, F32R)
    sb("sq", 512, F32R)
    sb("x2s", 1024, F32R)
    sb("rfrF", 256, rows=1)
    sb("rfrR", 256, F32R, rows=1)
    sb("xab", 1024)

    def d0s(p):
        return slc("dscr", p * 256, p * 256 + 256)

    def d1s(p):
        return slc("dscr", 512 + p * 256, 512 + p * 256 + 256)

    pb = {}
    prev_xt = {}
    tok_x2 = {}
    tok_xaT = {}
    xa_guard = []
    for it in range(N1):
        key = "A" if it % 2 == 0 else "B"
        xt = "xt1" + key
        sl0 = it * TAU1
        ldt = []
        for g in range(G):
            ldt.append(P.emit("sync", lambda s, g=g, xt=xt, sl0=sl0: s.dma_start(
                out=slc(xt, g * 256, (g + 1) * 256), in_=xnF_d[g, :, sl0:sl0 + 256]),
                wait=[prev_xt.get(key), prev_xt.get(key + "v"), tok_xnF[(sl0 // 512, g)][0]],
                inc="LD", inc_n=16))
        rtp = None
        for s2 in range(2):
            sub = sl0 // 128 + s2
            rtp = P.emit("tensor", lambda t, sub=sub, s2=s2: t.transpose(
                psums[4][0:1, 256 + s2 * 128: 256 + s2 * 128 + 128].bitcast(F32R),
                slc("rsioc", sub, sub + 1), sbv("identr")),
                wait=[pb.get("4r")] if s2 == 0 else (), inc="MM" if s2 == 1 else None)
        t_rsr = P.emit("vector", lambda v: v.tensor_copy(
            sbv("rsrowS"), psums[4][0:1, 256:512].bitcast(F32R)),
            wait=[rtp, pb.get("rsrow")], inc="EV")
        t_rb = P.emit("tensor", lambda t: t.matmul(
            psums[4][:, 0:256], sbv("ones1"), sbv("rsrowS"), start=True, stop=True),
            wait=[t_rsr, pb.get(4)], inc="MM")
        pb["4r"] = t_rb
        t_rbc = P.emit("vector", lambda v: v.tensor_copy(sbv("rbc"), psums[4][:, 0:256]),
                       wait=[t_rb, pb.get("rbc")], inc="EV")
        pb[4] = t_rbc
        pb["rsrow"] = t_rb
        scan_toks = []
        for fc in range(G):
            dpar = fc % 2
            mmz = None
            for g in range(G):
                mmz = P.emit("tensor", lambda t, g=g, fc=fc, xt=xt: t.matmul(
                    psums[0][:, 0:256],
                    slc("wlnzT", g * 1024 + fc * 128, g * 1024 + fc * 128 + 128),
                    slc(xt, g * 256, (g + 1) * 256),
                    start=(g == 0), stop=(g == G - 1)),
                    wait=[ldt[g]] + ([pb.get(0)] if g == 0 else []),
                    inc="MM" if g == G - 1 else None)
            mmd = None
            for g in range(G):
                mmd = P.emit("tensor", lambda t, g=g, fc=fc, xt=xt: t.matmul(
                    psums[1][:, 0:256],
                    slc("wdtT", g * 1024 + fc * 128, g * 1024 + fc * 128 + 128),
                    slc(xt, g * 256, (g + 1) * 256),
                    start=(g == 0), stop=(g == G - 1)),
                    wait=[pb.get(1)] if g == 0 else (), inc="MM" if g == G - 1 else None)
            t_z = P.emit("vector", lambda v: v.tensor_tensor(
                out=sbv("zs"), in0=psums[0][:, 0:256], in1=sbv("rbc"), op=ALU.mult),
                wait=[mmz, t_rbc], inc="EV")
            pb[0] = t_z
            t_d = P.emit("vector", lambda v: v.tensor_tensor(
                out=sbv("ds"), in0=psums[1][:, 0:256], in1=sbv("rbc"), op=ALU.mult),
                wait=[mmd], inc="EV")
            t_sz = P.emit("scalar", lambda sc, fc=fc: sc.activation(
                out=sbv("zs"), in_=sbv("zs"), func=AF.Sigmoid, bias=slc("blnz_c", fc, fc + 1)),
                wait=[t_z], inc="AC")
            t_d0 = P.emit("scalar", lambda sc, fc=fc, dpar=dpar: sc.activation(
                out=d0s(dpar), in_=sbv("ds"), func=AF.Sigmoid,
                scale=-1.0, bias=slc("nbdt_c", fc, fc + 1)),
                wait=[t_d, scan_toks[-2] if len(scan_toks) >= 2 else pb.get(("dscr", dpar))],
                inc="AC")
            t_sd = P.emit("scalar", lambda sc, fc=fc: sc.activation(
                out=sbv("ds"), in_=sbv("ds"), func=AF.Sigmoid, bias=slc("bdt_c", fc, fc + 1)),
                wait=[t_d0], inc="AC", hazard=True)
            pb[1] = t_sd
            t_d1 = P.emit("vector", lambda v, dpar=dpar: v.tensor_tensor(
                out=d1s(dpar), in0=sbv("ds"), in1=sbv("zs"), op=ALU.mult),
                wait=[t_sd, t_sz,
                      scan_toks[-2] if len(scan_toks) >= 2 else pb.get(("dscr1", dpar))],
                inc="EV")

            def mk_scan(fc=fc, it=it, dpar=dpar):
                def fn(v):
                    init = slc("h0_c", fc, fc + 1) if it == 0 else slc("hinit", fc, fc + 1)
                    return v.tensor_tensor_scan(
                        out=slc("hcur", fc * 256, (fc + 1) * 256),
                        data0=d0s(dpar), data1=d1s(dpar),
                        initial=init, op0=ALU.mult, op1=ALU.add)
                return fn
            t_scan = P.emit("vector", mk_scan(), wait=[t_d0, t_d1], inc="EV")
            P.emit("vector", lambda v, fc=fc: v.tensor_copy(
                slc("hinit", fc, fc + 1), slc("hcur", fc * 256 + 255, fc * 256 + 256)),
                wait=[t_scan], inc="EV")
            scan_toks.append(t_scan)
        t_hR = P.emit("vector", lambda v: v.tensor_copy(sbv("hR"), sbv("hcur")),
                      wait=[scan_toks[-1], pb.get("hRr")], inc="EV")
        if "hdbg" in dbg_d:
            for g in range(G):
                P.emit("gpsimd", lambda gp, g=g, sl0=sl0: gp.dma_start(
                    out=dbg_d["hdbg"][g, :, sl0:sl0 + 256],
                    in_=slc("hcur", g * 256, (g + 1) * 256)),
                    wait=[t_hR], inc="LG", inc_n=16)
        sq_mm = None
        last_mmy = None
        last_mmw = None
        t_y_prev = None
        t_xa_prev = None
        t_sq = None
        for fc in range(G):
            mmy = None
            for g in range(G):
                mmy = P.emit("tensor", lambda t, g=g, fc=fc, xt=xt: t.matmul(
                    psums[2][:, 0:256],
                    slc("wyactT", g * 1024 + fc * 128, g * 1024 + fc * 128 + 128),
                    slc(xt, g * 256, (g + 1) * 256),
                    start=(g == 0), stop=(g == G - 1)),
                    wait=[pb.get(2)] if g == 0 else (), inc="MM" if g == G - 1 else None)
            mmw = None
            for g in range(G):
                mmw = P.emit("tensor", lambda t, g=g, fc=fc: t.matmul(
                    psums[3][:, 0:256],
                    slc("wyT", g * 1024 + fc * 128, g * 1024 + fc * 128 + 128),
                    slc("hR", g * 256, (g + 1) * 256),
                    start=(g == 0), stop=(g == G - 1)),
                    wait=[t_hR] + ([pb.get(3)] if g == 0 else []),
                    inc="MM" if g == G - 1 else None)
            t_ya = P.emit("vector", lambda v: v.tensor_tensor(
                out=sbv("sact"), in0=psums[2][:, 0:256], in1=sbv("rbc"), op=ALU.mult),
                wait=[mmy, t_y_prev], inc="EV")
            pb[2] = t_ya
            t_sl = P.emit("scalar", lambda sc, fc=fc: sc.activation(
                out=sbv("sact"), in_=sbv("sact"), func=AF.Silu, bias=slc("byact_c", fc, fc + 1)),
                wait=[t_ya], inc="AC", hazard=True)
            t_y = P.emit("vector", lambda v, fc=fc: v.scalar_tensor_tensor(
                out=sbv("zs"), in0=psums[3][:, 0:256], scalar=slc("by_c", fc, fc + 1),
                in1=sbv("sact"), op0=ALU.add, op1=ALU.mult),
                wait=[mmw, t_sl, t_xa_prev], inc="EV")
            pb[3] = t_y
            t_y_prev = t_y
            t_xa = P.emit("vector", lambda v, fc=fc, xt=xt: v.tensor_tensor(
                out=slc("xa", fc * 256, (fc + 1) * 256),
                in0=slc(xt, fc * 256, (fc + 1) * 256), in1=sbv("zs"), op=ALU.add),
                wait=[t_y] + (xa_guard if fc == 0 else []), inc="EV")
            t_xa_prev = t_xa
            spar = fc % 2
            t_sq = P.emit("scalar", lambda sc, fc=fc, spar=spar: sc.activation(
                out=slc("sq", spar * 256, spar * 256 + 256),
                in_=slc("xa", fc * 256, (fc + 1) * 256), func=AF.Square),
                wait=[t_xa, pb.get(("sq", spar))], inc="AC")
            sq_mm = P.emit("tensor", lambda t, fc=fc, spar=spar: t.matmul(
                psums[5][0:1, 0:256], sbv("onesc"),
                slc("sq", spar * 256, spar * 256 + 256),
                start=(fc == 0), stop=(fc == G - 1)),
                wait=[t_sq] + ([pb.get(5)] if fc == 0 else []),
                inc="MM" if fc == G - 1 else None)
            pb[("sq", spar)] = sq_mm
            last_mmy = mmy
            last_mmw = mmw
        pb["rbc"] = t_ya
        pb["hRr"] = last_mmw
        t_rf0 = P.emit("vector", lambda v: v.tensor_scalar(
            out=sbv("rfrF"), in0=psums[5][0:1, 0:256], scalar1=1.0 / D,
            scalar2=1e-6, op0=ALU.mult, op1=ALU.add), wait=[sq_mm, pb.get("rfF")],
            inc="EV", hazard=True)
        pb[5] = t_rf0
        t_rf1 = P.emit("scalar", lambda sc: sc.activation(
            out=sbv("rfrF"), in_=sbv("rfrF"), func=AF.Ln), wait=[t_rf0], inc="AC", hazard=True)
        pb["rfF"] = t_rf1
        t_rf = P.emit("scalar", lambda sc: sc.activation(
            out=sbv("rfrR"), in_=sbv("rfrF"), func=AF.Exp, scale=-0.5),
            wait=[t_rf1, pb.get("rfR")], inc="AC", hazard=True)
        t_rb2 = P.emit("tensor", lambda t: t.matmul(
            psums[4][:, 256:512], sbv("ones1"), sbv("rfrR"),
            start=True, stop=True), wait=[t_rf, pb.get("4b")], inc="MM")
        pb["rfR"] = t_rb2
        t_x2_last = None
        for fc in range(G):
            slot = fc % 4
            t_x2 = P.emit("vector", lambda v, fc=fc, slot=slot: v.scalar_tensor_tensor(
                out=slc("x2s", slot * 256, slot * 256 + 256),
                in0=slc("xa", fc * 256, (fc + 1) * 256),
                scalar=slc("gffn_c", fc, fc + 1), in1=psums[4][:, 256:512],
                op0=ALU.mult, op1=ALU.mult),
                wait=[t_rb2, tok_x2.get((it, fc - 4), tok_x2.get((it - 1, fc + 4)))],
                inc="EV")
            tok_x2[(it, fc)] = P.emit("gpsimd", lambda gp, fc=fc, sl0=sl0, slot=slot: gp.dma_start(
                out=x2F_d[fc, :, sl0:sl0 + 256],
                in_=slc("x2s", slot * 256, slot * 256 + 256)),
                wait=[t_x2], inc="LG", inc_n=16)
            t_x2_last = t_x2
        pb["4b"] = t_x2_last
        pb[("dscr", 0)] = scan_toks[-2]
        pb[("dscr", 1)] = scan_toks[-1]
        pb[("dscr1", 0)] = scan_toks[-2]
        pb[("dscr1", 1)] = scan_toks[-1]
        xa_guard = [t_x2_last, t_sq]
        for s2 in range(2):
            tks = {}
            for g in range(G):
                bi = 6 + (g // 4)
                tks[g] = P.emit("tensor", lambda t, g=g, s2=s2, bi=bi: t.transpose(
                    psums[bi][:, (g % 4) * 128:(g % 4) * 128 + 128].bitcast(F32R),
                    slc("xa", g * 256 + s2 * 128, g * 256 + s2 * 128 + 128),
                    sbv("identr")),
                    wait=[pb.get(("xat", bi))] if g % 4 == 0 else (),
                    inc="MM" if g % 4 == 3 else None)
            t_e1 = P.emit("vector", lambda v: v.tensor_copy(
                slc("xab", 0, 512), psums[6][:]), wait=[tks[3], pb.get("xabd")], inc="EV")
            t_e2 = P.emit("vector", lambda v: v.tensor_copy(
                slc("xab", 512, 1024), psums[7][:]), wait=[tks[7]], inc="EV")
            pb[("xat", 6)] = t_e1
            pb[("xat", 7)] = t_e2
            tl = it * 2 + s2
            tok_xaT[tl] = P.emit("gpsimd", lambda gp, tl=tl: gp.dma_start(
                out=xaT_d[tl], in_=sbv("xab")), wait=[t_e1, t_e2], inc="LG", inc_n=16)
            pb["xabd"] = tok_xaT[tl]
            xa_guard += [tks[3], tks[7]]
        prev_xt[key] = last_mmy
        prev_xt[key + "v"] = t_xa_prev
    drop("xt1A", "xt1B", "rsrowS", "rbc", "zs", "ds", "dscr", "hcur", "hinit", "hR",
         "sact", "xa", "sq", "x2s", "rfrF", "rfrR", "xab",
         "wlnzT", "wdtT", "wyactT", "wyT", "rsioc")

    if upto == "ph1a":
        return _close(nc, P, semh)

    P.start_phase()
    sb("x2tA", 2048, F32R)
    sb("x2tB", 2048, F32R)
    sb("xaTt", 2048)
    sb("wfcbA", 4096, F32R)
    sb("wfcbB", 4096, F32R)
    sb("wfabA", 4096, F32R)
    sb("wfabB", 4096, F32R)
    sb("woutbA", 4096, F32R)
    sb("woutbB", 4096, F32R)
    sb("hid", 1024, F32R)
    sb("sact1b", 512)
    sb("sbnc", 2048)
    sb("boutrS", D, F32R, rows=1)
    tok_bo = P.emit("sync", lambda s: s.dma_start(out=sbv("boutrS"), in_=boutr_d),
                    inc="LD", inc_n=16)

    prev_x2t = {}
    prev_w = {}
    prev_s = {}
    for it in range(N1):
        key = "A" if it % 2 == 0 else "B"
        x2t = "x2t" + key
        sl0 = it * TAU1
        ld_x2 = []
        for g in range(G):
            ld_x2.append(P.emit("sync", lambda s, g=g, x2t=x2t, sl0=sl0: s.dma_start(
                out=slc(x2t, g * 256, (g + 1) * 256), in_=x2F_d[g, :, sl0:sl0 + 256]),
                wait=[prev_x2t.get(key), tok_x2.get((it, g))], inc="LD", inc_n=16))
        ld_xa = []
        for tc in range(2):
            tl = it * 2 + tc
            ld_xa.append(P.emit("sync", lambda s, tc=tc, tl=tl: s.dma_start(
                out=slc("xaTt", tc * 1024, (tc + 1) * 1024), in_=xaT_d[tl]),
                wait=[prev_s.get(("sd", tc)), tok_xaT.get(tl)], inc="LD", inc_n=16))
        tok_offc = None
        hid_guard = {0: None, 1: None}
        for ffc in range(8):
            wk = "A" if ffc % 2 == 0 else "B"
            wfcb, wfab, woutb = "wfcb" + wk, "wfab" + wk, "woutb" + wk
            ld_w = {}
            for g in range(G):
                ld_w[("fc", g)] = P.emit("sync", lambda s, g=g, ffc=ffc, wfcb=wfcb: s.dma_start(
                    out=slc(wfcb, g * 512, (g + 1) * 512),
                    in_=wfcT_d[g, :, ffc * 512:(ffc + 1) * 512]),
                    wait=[prev_w.get(wfcb)] if g == 0 else (), inc="LD", inc_n=16)
                ld_w[("fa", g)] = P.emit("sync", lambda s, g=g, ffc=ffc, wfab=wfab: s.dma_start(
                    out=slc(wfab, g * 512, (g + 1) * 512),
                    in_=wfcaT_d[g, :, ffc * 512:(ffc + 1) * 512]),
                    wait=[prev_w.get(wfab)] if g == 0 else (), inc="LD", inc_n=16)
            ld_wo = {}
            for k4 in range(4):
                ld_wo[k4] = P.emit("sync", lambda s, k4=k4, ffc=ffc, woutb=woutb: s.dma_start(
                    out=slc(woutb, k4 * 1024, (k4 + 1) * 1024),
                    in_=woutT_d[ffc * 4 + k4]),
                    wait=[prev_w.get(woutb)] if k4 == 0 else (), inc="LD", inc_n=16)
            for fs in range(4):
                fpar = fs % 2
                bz = fpar
                ba = 2 + fpar
                mmf = None
                for g in range(G):
                    mmf = P.emit("tensor", lambda t, g=g, fs=fs, bz=bz, wfcb=wfcb, x2t=x2t: t.matmul(
                        psums[bz][:, 0:256],
                        slc(wfcb, g * 512 + fs * 128, g * 512 + fs * 128 + 128),
                        slc(x2t, g * 256, (g + 1) * 256),
                        start=(g == 0), stop=(g == G - 1)),
                        wait=[ld_w[("fc", g)], ld_x2[g]] + ([pb.get(("1b", bz))] if g == 0 else []),
                        inc="MM" if g == G - 1 else None)
                mma = None
                for g in range(G):
                    mma = P.emit("tensor", lambda t, g=g, fs=fs, ba=ba, wfab=wfab, x2t=x2t: t.matmul(
                        psums[ba][:, 0:256],
                        slc(wfab, g * 512 + fs * 128, g * 512 + fs * 128 + 128),
                        slc(x2t, g * 256, (g + 1) * 256),
                        start=(g == 0), stop=(g == G - 1)),
                        wait=[ld_w[("fa", g)]] + ([pb.get(("1b", ba))] if g == 0 else []),
                        inc="MM" if g == G - 1 else None)
                fcol = ffc * 4 + fs
                t_sl = P.emit("scalar", lambda sc, fcol=fcol, ba=ba, fpar=fpar: sc.activation(
                    out=slc("sact1b", fpar * 256, fpar * 256 + 256),
                    in_=psums[ba][:, 0:256], func=AF.Silu, bias=slc("bfca_c", fcol, fcol + 1)),
                    wait=[mma, pb.get(("sl", fpar))], inc="AC")
                pb[("1b", ba)] = t_sl
                t_hid = P.emit("vector", lambda v, fcol=fcol, bz=bz, fpar=fpar: v.scalar_tensor_tensor(
                    out=slc("hid", fpar * 256, fpar * 256 + 256),
                    in0=psums[bz][:, 0:256], scalar=slc("bfc_c", fcol, fcol + 1),
                    in1=slc("sact1b", fpar * 256, fpar * 256 + 256),
                    op0=ALU.add, op1=ALU.mult),
                    wait=[mmf, t_sl, hid_guard[fpar]], inc="EV")
                pb[("1b", bz)] = t_hid
                pb[("sl", fpar)] = t_hid
                last_out = None
                for tc in range(2):
                    for nh in range(2):
                        bo = 4 + tc * 2 + nh
                        is_first = (ffc == 0 and fs == 0)
                        is_tok = (tc == 1 and nh == 1)
                        last_out = P.emit(
                            "tensor",
                            lambda t, tc=tc, nh=nh, bo=bo, fpar=fpar, woutb=woutb, k4=fs,
                            is_first=is_first: t.matmul(
                                psums[bo][:, 0:512],
                                slc("hid", fpar * 256 + tc * 128, fpar * 256 + tc * 128 + 128),
                                slc(woutb, k4 * 1024 + nh * 512, k4 * 1024 + nh * 512 + 512),
                                start=is_first, stop=False),
                            wait=[t_hid, ld_wo[fs]]
                            + ([pb.get(("1b", bo))] if is_first else []),
                            inc="MM" if is_tok else None)
                hid_guard[fpar] = last_out
                tok_offc = last_out
            prev_w[wfcb] = tok_offc
            prev_w[wfab] = tok_offc
            prev_w[woutb] = tok_offc
        for tc in range(2):
            for nh in range(2):
                bo = 4 + tc * 2 + nh
                t_b = P.emit("tensor", lambda t, nh=nh, bo=bo: t.matmul(
                    psums[bo][:, 0:512], sbv("ones1"),
                    slc("boutrS", nh * 512, nh * 512 + 512),
                    start=False, stop=True), wait=[tok_bo], inc="MM")
                t_s = P.emit("vector", lambda v, tc=tc, nh=nh, bo=bo: v.tensor_tensor(
                    out=slc("sbnc", tc * 1024 + nh * 512, tc * 1024 + nh * 512 + 512),
                    in0=psums[bo][:, 0:512],
                    in1=slc("xaTt", tc * 1024 + nh * 512, tc * 1024 + nh * 512 + 512),
                    op=ALU.add), wait=[t_b, ld_xa[tc]], inc="EV")
                pb[("1b", bo)] = t_s
        for tc in range(2):
            tl = it * 2 + tc
            t_sd = P.emit("gpsimd", lambda gp, tc=tc, tl=tl: gp.dma_start(
                out=sT_d[tl], in_=slc("sbnc", tc * 1024, (tc + 1) * 1024)),
                wait=[pb[("1b", 4 + tc * 2)], pb[("1b", 4 + tc * 2 + 1)]], inc="LG", inc_n=16)
            prev_s[("sd", tc)] = t_sd
            prev_s[("sT", tl)] = t_sd
        prev_x2t[key] = tok_offc
    drop("x2tA", "x2tB", "xaTt", "wfcbA", "wfcbB", "wfabA", "wfabB",
         "woutbA", "woutbB", "hid", "sact1b", "sbnc", "boutrS")

    if upto == "ph1b":
        return _close(nc, P, semh)

    P.start_phase()
    sb("xnbtA", 4096, BF16)
    sb("xnbtB", 4096, BF16)
    sb("expbA", 8192, BF16)
    sb("expbB", 8192, BF16)
    sb("sE", 4096)
    sb("xE", 4096)
    sb("yacc", 4096)
    sb("outb", 4096)
    prev2 = {}
    for it in range(N2):
        key = "A" if it % 2 == 0 else "B"
        xnbt = "xnbt" + key
        sl0 = it * 512
        ldb = []
        for g in range(G):
            ldb.append(P.emit("sync", lambda s, g=g, xnbt=xnbt, sl0=sl0: s.dma_start(
                out=slc(xnbt, g * 512, (g + 1) * 512), in_=xnB_d[g, :, sl0:sl0 + 512]),
                wait=[prev2.get(xnbt), tok_xnF[(sl0 // 512, g)][1]], inc="LD", inc_n=16))
        ld_s = []
        ld_x = []
        for tc in range(4):
            tl = it * 4 + tc
            ld_s.append(P.emit("sync", lambda s, tc=tc, tl=tl: s.dma_start(
                out=slc("sE", tc * 1024, (tc + 1) * 1024), in_=sT_d[tl]),
                wait=[prev2.get(("sE", tc)), prev_s.get(("sT", tl))], inc="LD", inc_n=16))
            ld_x.append(P.emit("sync", lambda s, tc=tc, tl=tl: s.dma_start(
                out=slc("xE", tc * 1024, (tc + 1) * 1024), in_=x_d[tl * 128:(tl + 1) * 128, :]),
                wait=[prev2.get(("xE", tc))], inc="LD", inc_n=16))
        acc_last = {}
        last_mm_it = None
        for e in range(E):
            ek = "A" if e % 2 == 0 else "B"
            en = "expb" + ek
            ld_e = {}
            for g in range(G):
                w = []
                if g == 0:
                    w.append(prev2.get((en, "free")))
                    if it == 0 and e < 2:
                        w += tok_expB
                ld_e[g] = P.emit("sync", lambda s, e=e, g=g, en=en: s.dma_start(
                    out=slc(en, g * 1024, (g + 1) * 1024), in_=expB_d[e, g]),
                    wait=w, inc="LD", inc_n=16)
            if it == 0 and e == 0 and "expbdbg" in dbg_d:
                P.emit("gpsimd", lambda gp, en=en: gp.dma_start(
                    out=dbg_d["expbdbg"], in_=sbv(en)),
                    wait=[ld_e[g2] for g2 in range(G)], inc="LG", inc_n=16)
            for tc in range(4):
                for oh in range(2):
                    bo = (e % 2) * 2 + oh
                    mm = None
                    nrep = 2 if (it == 0 and e == 0) else 1
                    for rep in range(nrep):
                        for g in range(G):
                            mm = P.emit("tensor", lambda t, g=g, e=e, oh=oh, tc=tc, bo=bo,
                                        xnbt=xnbt, en=en: t.matmul(
                                psums[bo][:, 0:512],
                                slc(xnbt, g * 512 + tc * 128, g * 512 + tc * 128 + 128),
                                slc(en, g * 1024 + oh * 512, g * 1024 + oh * 512 + 512),
                                start=(g == 0), stop=(g == G - 1)),
                                wait=[ldb[g], ld_e[g]] + ([pb.get(("2", bo))] if g == 0 else []),
                                inc="MM" if g == G - 1 else None)
                    tl = it * 4 + tc

                    def mk_ev(e=e, oh=oh, tc=tc, bo=bo, tl=tl, mm=mm):
                        def fn(v):
                            wcol = sbv("win")[:, e * 32 + tl:e * 32 + tl + 1]
                            dst = slc("yacc", tc * 1024 + oh * 512, tc * 1024 + oh * 512 + 512)
                            if e == 0:
                                return v.tensor_scalar(out=dst, in0=psums[bo][:, 0:512],
                                                       scalar1=wcol, scalar2=None, op0=ALU.mult)
                            return v.scalar_tensor_tensor(out=dst, in0=psums[bo][:, 0:512],
                                                          scalar=wcol, in1=dst,
                                                          op0=ALU.mult, op1=ALU.add)
                        return fn
                    acc_last[(tc, oh)] = P.emit("vector", mk_ev(),
                                                wait=[mm, t_win, acc_last.get((tc, oh))],
                                                inc="EV")
                    pb[("2", bo)] = acc_last[(tc, oh)]
                    last_mm_it = mm
            prev2[(en, "free")] = last_mm_it
        if it == 0 and "yaccdbg" in dbg_d:
            P.emit("gpsimd", lambda gp: gp.dma_start(
                out=dbg_d["yaccdbg"], in_=slc("yacc", 0, 1024)),
                wait=[acc_last[(0, 0)], acc_last[(0, 1)]], inc="LG", inc_n=16)
        for tc in range(4):
            tl = it * 4 + tc
            t_g = P.emit("scalar", lambda sc, tc=tc: sc.activation(
                out=slc("yacc", tc * 1024, (tc + 1) * 1024),
                in_=slc("yacc", tc * 1024, (tc + 1) * 1024), func=AF.Silu),
                wait=[acc_last[(tc, 0)], acc_last[(tc, 1)]], inc="AC", hazard=True)
            t_sg = P.emit("vector", lambda v, tc=tc: v.tensor_tensor(
                out=slc("outb", tc * 1024, (tc + 1) * 1024),
                in0=slc("yacc", tc * 1024, (tc + 1) * 1024),
                in1=slc("sE", tc * 1024, (tc + 1) * 1024), op=ALU.mult),
                wait=[t_g, ld_s[tc], prev2.get(("outd", tc))], inc="EV")
            t_out = P.emit("vector", lambda v, tc=tc: v.tensor_tensor(
                out=slc("outb", tc * 1024, (tc + 1) * 1024),
                in0=slc("outb", tc * 1024, (tc + 1) * 1024),
                in1=slc("xE", tc * 1024, (tc + 1) * 1024), op=ALU.add),
                wait=[t_sg, ld_x[tc]], inc="EV")
            t_od = P.emit("gpsimd", lambda gp, tc=tc, tl=tl: gp.dma_start(
                out=out_d[tl * 128:(tl + 1) * 128, :],
                in_=slc("outb", tc * 1024, (tc + 1) * 1024)),
                wait=[t_out], inc="OD", inc_n=16)
            prev2[("sE", tc)] = t_sg
            prev2[("xE", tc)] = t_out
            prev2[("outd", tc)] = t_od
        prev2[xnbt] = last_mm_it

    return _close(nc, P, semh)


def _close(nc, P, semh):
    waits = tuple((s, v) for s, v in P.counts.items() if v > 0
                  and s in ("LD", "LG", "LS", "OD"))
    P.items.append(("gpsimd", lambda gp: gp.nop(), waits, None))
    with nc.Block() as block:
        P.replay(block, semh)
    return nc


def make_in_maps(inputs):
    x = np.ascontiguousarray(inputs["L0_attn_x"], dtype=np.float32)
    ins = {
        "wlnz": inputs["L0_attn_w_ln_z"], "wdt": inputs["L0_attn_w_dt"],
        "wy": inputs["L0_attn_w_y"], "wyact": inputs["L0_attn_w_yact"],
        "wfc": inputs["L0_attn_w_fc"], "wfca": inputs["L0_attn_w_fcact"],
        "wout": inputs["L0_attn_w_out"], "exps": inputs["L0_ffn_experts"],
        "gw": inputs["L0_ffn_gate_w"][0],
        "blnz": inputs["L0_attn_b_ln_z"], "bdt": inputs["L0_attn_b_dt"],
        "by": inputs["L0_attn_b_y"], "byact": inputs["L0_attn_b_yact"],
        "bfc": inputs["L0_attn_b_fc"], "bfca": inputs["L0_attn_b_fcact"],
        "bout": inputs["L0_attn_b_out"],
        "gsio": inputs["L0_attn_g_sio"], "gffn": inputs["L0_attn_g_ffn"],
        "h0": inputs["L0_attn_h0"],
    }
    ins = {k: np.ascontiguousarray(v, dtype=np.float32) for k, v in ins.items()}
    ins["boutr"] = ins["bout"].reshape(1, D).copy()
    ins["ident"] = np.eye(128, dtype=np.float32)
    ins["identr"] = np.eye(128, dtype=np.float32)
    ins["ones1"] = np.ones((1, 128), dtype=np.float32)
    ins["onesc"] = np.ones((128, 1), dtype=np.float32)
    return [dict(ins, x=x[i]) for i in range(NCORE)]


def kernel(**inputs):
    nc = build_program()
    in_maps = make_in_maps(inputs)
    res = run_bass_kernel_spmd(nc, in_maps, core_ids=list(range(NCORE)))
    out = np.stack([np.asarray(res.results[i]["out"]) for i in range(NCORE)], axis=0)
    return out.astype(np.float32)


# version 09
# speedup vs baseline: 1.0041x; 1.0041x over previous
import numpy as np
import concourse.bass as bass
import concourse.mybir as mybir
from concourse.bass_utils import run_bass_kernel_spmd

F32 = mybir.dt.float32
F32R = mybir.dt.float32r
BF16 = mybir.dt.bfloat16
ALU = mybir.AluOpType
AF = mybir.ActivationFunctionType
AX = mybir.AxisListType

T = 4096
D = 1024
FF = 4096
E = 8
G = 8
NCORE = 8
TAU1 = 256
N0 = T // 512
N1 = T // TAU1
N2 = T // 512
NTILE = T // 128

ENGINES = ("sync", "vector", "scalar", "tensor", "gpsimd")
SEMS = ["LD", "LG", "LS", "MM", "EV", "AC", "PL", "OD"]


class Prog:
    def __init__(self):
        self.counts = {s: 0 for s in SEMS}
        self.items = []
        self.tok_info = {}
        self.phase_bar = ()
        self.phase_pending = set()

    def start_phase(self):
        self.phase_bar = tuple((s, v) for s, v in self.counts.items() if v > 0)
        self.phase_pending = set(ENGINES)

    def emit(self, engine, fn, wait=(), inc=None, inc_n=1, hazard=False):
        wl = [w for w in wait if w is not None]
        if engine in self.phase_pending:
            self.phase_pending.discard(engine)
            wl = list(self.phase_bar) + wl
        tok = None
        if inc is not None:
            self.counts[inc] += inc_n
            tok = (inc, self.counts[inc])
            self.tok_info[tok] = (engine, hazard)
        self.items.append((engine, fn, tuple(wl), (inc, inc_n) if inc else None))
        return tok

    def replay(self, block, sems):
        for eng_name in ENGINES:
            items = [(fn, waits, inc) for (e, fn, waits, inc) in self.items if e == eng_name]
            if not items:
                continue

            def body(eng, _items=items, _eng=eng_name):
                hi = {}
                for fn, waits, inc in _items:
                    for (s, v) in waits:
                        if hi.get(s, -1) >= v:
                            continue
                        eng.wait_ge(sems[s], v)
                        hi[s] = v
                    instr = fn(eng)
                    if inc is not None:
                        instr.then_inc(sems[inc[0]], inc[1])
            getattr(block, eng_name)(body)


def build_program(taps=(), upto="all"):
    nc = bass.Bass("TRN2", target_bir_lowering=False)
    P = Prog()

    def kind(name):
        return "ExternalOutput" if name in taps else "Internal"

    x_d = nc.dram_tensor("x", [T, D], F32, kind="ExternalInput").ap()
    out_d = nc.dram_tensor("out", [T, D], F32, kind="ExternalOutput").ap()
    w_d = {}
    for nm, shape in [("wlnz", [D, D]), ("wdt", [D, D]), ("wy", [D, D]),
                      ("wyact", [D, D]), ("wfc", [FF, D]), ("wfca", [FF, D]),
                      ("wout", [D, FF])]:
        w_d[nm] = nc.dram_tensor(nm, shape, F32, kind="ExternalInput").ap()
    exps_d = nc.dram_tensor("exps", [E, D, D], F32R, kind="ExternalInput").ap()
    gw_d = nc.dram_tensor("gw", [D, E], F32R, kind="ExternalInput").ap()
    vec_d = {}
    for nm, n in [("blnz", D), ("bdt", D), ("by", D), ("byact", D), ("bfc", FF),
                  ("bfca", FF), ("bout", D), ("gsio", D), ("gffn", D), ("h0", D)]:
        vec_d[nm] = nc.dram_tensor(nm, [n], F32, kind="ExternalInput").ap()
    boutr_d = nc.dram_tensor("boutr", [1, D], F32R, kind="ExternalInput").ap()
    ident_d = nc.dram_tensor("ident", [128, 128], F32, kind="ExternalInput").ap()
    identr_d = nc.dram_tensor("identr", [128, 128], F32R, kind="ExternalInput").ap()
    ones1_d = nc.dram_tensor("ones1", [1, 128], F32R, kind="ExternalInput").ap()
    onesc_d = nc.dram_tensor("onesc", [128, 1], F32R, kind="ExternalInput").ap()

    xnF_d = nc.dram_tensor("xnF", [G, 128, T], F32R, kind=kind("xnF")).ap()
    xnB_d = nc.dram_tensor("xnB", [G, 128, T], BF16, kind=kind("xnB")).ap()
    x2F_d = nc.dram_tensor("x2F", [G, 128, T], F32R, kind=kind("x2F")).ap()
    xaT_d = nc.dram_tensor("xaT", [NTILE, 128, D], F32, kind=kind("xaT")).ap()
    sT_d = nc.dram_tensor("sT", [NTILE, 128, D], F32, kind=kind("sT")).ap()
    wfcT_d = nc.dram_tensor("wfcT", [G, 128, FF], F32R, kind=kind("wfcT")).ap()
    wfcaT_d = nc.dram_tensor("wfcaT", [G, 128, FF], F32R, kind=kind("wfcaT")).ap()
    woutT_d = nc.dram_tensor("woutT", [FF // 128, 128, D], F32R, kind=kind("woutT")).ap()
    expB_d = nc.dram_tensor("expB", [E, G, 128, D], BF16, kind=kind("expB")).ap()
    dbg_d = {}
    for nm, shape, dt_ in [("gdbg", [8, T], F32), ("windbg", [128, E, NTILE], F32),
                           ("hdbg", [G, 128, T], F32), ("seldbg", [128, E, NTILE], F32),
                           ("taudbg", [128, 8], F32), ("gtokdbg", [128, 256], F32),
                           ("cntdbg", [128, 16], F32), ("yaccdbg", [128, 1024], F32),
                           ("expbdbg", [128, 8192], BF16)]:
        if nm in taps:
            dbg_d[nm] = nc.dram_tensor(nm, shape, dt_, kind="ExternalOutput").ap()

    psums = [nc.psum_tensor(f"PS{i}", [128, 512], F32).__enter__() for i in range(8)]
    semh = {s: nc.semaphore(s).__enter__() for s in SEMS}

    tens = {}
    dead = {}
    seq = [0]

    def sb(name, cols, dtype=F32, rows=128):
        cm = nc.sbuf_tensor("sb_" + name, [rows, cols], dtype)
        h = cm.__enter__()
        seq[0] += 1
        tens[name] = (h, cm, dtype, rows, seq[0])

    def drop(*names):
        for name in sorted(names, key=lambda n: -tens[n][4]):
            rec = tens.pop(name)
            dead[name] = rec
            rec[1].__exit__(None, None, None)

    def sbv(name, rows=None, row0=0):
        h, _, dt_, r, _s = tens[name] if name in tens else dead[name]
        if rows is None and row0 == 0:
            return h[:]
        return h[row0:(row0 + rows) if rows is not None else r, :]

    def slc(name, c0, c1, rows=None, row0=0):
        h, _, dt_, r, _s = tens[name] if name in tens else dead[name]
        r1 = (row0 + rows) if rows is not None else r
        return h[row0:r1, c0:c1]

    sb("ident", 128)
    sb("identr", 128, F32R)
    sb("ones1", 128, F32R, rows=1)
    sb("onesc", 1, F32R)
    sb("ones1B", 128, BF16, rows=1)
    sb("gwsb", 64, F32R)
    colspec = {"blnz_c": ("blnz", 8), "bdt_c": ("bdt", 8), "nbdt_c": (None, 8),
               "by_c": ("by", 8), "byact_c": ("byact", 8), "bfc_c": ("bfc", 32),
               "bfca_c": ("bfca", 32), "gsio_c": ("gsio", 8), "gffn_c": ("gffn", 8),
               "h0_c": ("h0", 8)}
    for nm, (_, k) in colspec.items():
        sb(nm, k)
    sb("win", 256)

    P.emit("sync", lambda s: s.dma_start(out=sbv("ident"), in_=ident_d), inc="LD", inc_n=16)
    P.emit("sync", lambda s: s.dma_start(out=sbv("identr"), in_=identr_d), inc="LD", inc_n=16)
    P.emit("sync", lambda s: s.dma_start(out=sbv("ones1"), in_=ones1_d), inc="LD", inc_n=16)
    P.emit("gpsimd", lambda gp: gp.dma_start(out=sbv("ones1B"), in_=ones1_d),
           wait=[("LD", 48)], inc="LG", inc_n=16)
    P.emit("sync", lambda s: s.dma_start(out=sbv("onesc"), in_=onesc_d), inc="LD", inc_n=16)
    for g in range(G):
        P.emit("sync", lambda s, g=g: s.dma_start(
            out=slc("gwsb", g * 8, g * 8 + 8), in_=gw_d[g * 128:(g + 1) * 128, :]),
            inc="LD", inc_n=16)
    for nm, (src, k) in colspec.items():
        if src is None:
            continue
        for j in range(k):
            P.emit("sync", lambda s, nm=nm, src=src, j=j: s.dma_start(
                out=slc(nm, j, j + 1),
                in_=vec_d[src][j * 128:(j + 1) * 128].rearrange("(p o) -> p o", o=1)),
                inc="LD", inc_n=16)
    tok_consts = ("LD", P.counts["LD"])
    tok_nbdt = P.emit("vector", lambda v: v.tensor_scalar(
        out=sbv("nbdt_c"), in0=sbv("bdt_c"), scalar1=-1.0, scalar2=None, op0=ALU.mult),
        wait=[tok_consts], inc="EV")

    def phW_gen():
        bank_free = {}
        stg_free = {0: None, 1: None}
        for src, dstT in (("wfc", wfcT_d), ("wfca", wfcaT_d)):
            for fb in range(FF // 128):
                par = fb % 2
                tok_stg = P.emit("sync", lambda s, src=src, fb=fb, par=par: s.dma_start(
                    out=slc("wstage", par * 1024, par * 1024 + 1024),
                    in_=w_d[src][fb * 128:(fb + 1) * 128, :]),
                    wait=[stg_free[par], tok_consts], inc="LD", inc_n=16)
                last_tp = None
                for half in range(2):
                    bi = 2 + half
                    bn = "wbnc0" if half == 0 else "wbnc1"
                    tk = None
                    for gh in range(4):
                        g = half * 4 + gh
                        tk = P.emit("tensor", lambda t, g=g, gh=gh, bi=bi, par=par: t.transpose(
                            psums[bi][:, gh * 128:(gh + 1) * 128],
                            slc("wstage", par * 1024 + g * 128, par * 1024 + g * 128 + 128),
                            sbv("ident")),
                            wait=[bank_free.get(bi), tok_stg] if gh == 0 else (),
                            inc="MM" if gh == 3 else None)
                    tok_ev = P.emit("vector", lambda v, bi=bi, bn=bn: v.tensor_copy(
                        sbv(bn), psums[bi][:]), wait=[tk, bank_free.get((bn,))], inc="EV")
                    bank_free[bi] = tok_ev
                    dmt = None
                    for gh in range(4):
                        g = half * 4 + gh
                        dmt = P.emit("gpsimd", lambda gp, g=g, gh=gh, fb=fb, bn=bn, dstT=dstT:
                                     gp.dma_start(out=dstT[g, :, fb * 128:(fb + 1) * 128],
                                                  in_=slc(bn, gh * 128, gh * 128 + 128)),
                                     wait=[tok_ev], inc="LG", inc_n=16)
                    bank_free[(bn,)] = dmt
                    last_tp = tk
                stg_free[par] = last_tp
                yield
        for db in range(D // 128):
            par = db % 2
            tok_stg = P.emit("sync", lambda s, db=db, par=par: s.dma_start(
                out=slc("wstage", par * 4096, par * 4096 + 4096),
                in_=w_d["wout"][db * 128:(db + 1) * 128, :]),
                wait=[stg_free.get(("wo", par))] if ("wo", par) in stg_free
                else [stg_free[0], stg_free[1]], inc="LD", inc_n=16)
            last_tp = None
            for ffc in range(FF // 128):
                half = ffc % 2
                bi = 2 + half
                bn = "wbnc0" if half == 0 else "wbnc1"
                tok_tp = P.emit("tensor", lambda t, db=db, ffc=ffc, bi=bi, par=par: t.transpose(
                    psums[bi][:, 0:128],
                    slc("wstage", par * 4096 + ffc * 128, par * 4096 + ffc * 128 + 128),
                    sbv("ident")),
                    wait=[bank_free.get(bi), tok_stg], inc="MM")
                tok_ev = P.emit("vector", lambda v, bi=bi, bn=bn: v.tensor_copy(
                    slc(bn, 0, 128), psums[bi][:, 0:128]),
                    wait=[tok_tp, bank_free.get((bn,))], inc="EV")
                bank_free[bi] = tok_ev
                dmt = P.emit("gpsimd", lambda gp, ffc=ffc, db=db, bn=bn: gp.dma_start(
                    out=woutT_d[ffc, :, db * 128:(db + 1) * 128], in_=slc(bn, 0, 128)),
                    wait=[tok_ev], inc="LG", inc_n=16)
                bank_free[(bn,)] = dmt
                last_tp = tok_tp
            stg_free[("wo", par)] = last_tp
            yield
        while True:
            yield

    P.start_phase()
    sb("rsioc", 32, F32R)
    sb("gatesT", T, rows=8)
    sb("ewT", T, rows=8)
    sb("rowA", T, rows=1)
    sb("rowB", T, rows=1)
    sb("xtokA", 1024)
    sb("xtokB", 1024)
    sb("bnscrA", 16)
    sb("bnscrB", 16)
    sb("mvA", 2)
    sb("mvB", 2)
    sb("lncolA", 8)
    sb("lncolB", 8)
    sb("xn_sA", 1024)
    sb("xn_sB", 1024)
    sb("xnball", 4096, F32R)
    sb("expstgA", 1024, F32R)
    sb("expstgB", 1024, F32R)
    sb("g_tok", 256)
    sb("skt", 256)
    sb("sks", 128)
    sb("pm8", 128, rows=8)
    sb("mx8", 1, rows=8)
    sb("mxrowR", 8, F32R, rows=1)
    sb("lseR", 8, F32R, rows=1)
    sb("prowR", 8, F32R)
    sb("selm", 256)
    sb("hardm", 256)
    sb("bis", 48)
    sb("bmsk", 8, mybir.dt.int32)
    sb("cntR", 8, F32R)
    sb("crR", 8, F32R, rows=1)
    sb("wstage", 8192)
    sb("wbnc0", 512, F32R)
    sb("wbnc1", 512, F32R)

    def gates_ap(r0=0, r1=8, c0=0, c1=T):
        return slc("gatesT", c0, c1, rows=r1 - r0, row0=r0)

    def ew_ap(r0=0, r1=8, c0=0, c1=T):
        return slc("ewT", c0, c1, rows=r1 - r0, row0=r0)

    exp_chunks = [(e, g) for e in range(E) for g in range(G)]
    exp_state = {"i": 0, "A": None, "B": None}

    def emit_expert_casts(n):
        for _ in range(n):
            i = exp_state["i"]
            if i >= len(exp_chunks):
                return
            e, g = exp_chunks[i]
            key = "A" if i % 2 == 0 else "B"
            stgn = "expstg" + key
            tok_l = P.emit("sync", lambda s, e=e, g=g, stgn=stgn: s.dma_start(
                out=sbv(stgn), in_=exps_d[e, g * 128:(g + 1) * 128, :]),
                wait=[exp_state[key]], inc="LD", inc_n=16)
            tok_c = P.emit("gpsimd", lambda gp, e=e, g=g, stgn=stgn: gp.dma_start(
                out=expB_d[e, g], in_=sbv(stgn)),
                wait=[tok_l], inc="LG", inc_n=16)
            exp_state[key] = tok_c
            exp_state["i"] = i + 1

    xuse = {"A": None, "B": None}
    bfree = {}
    tok_gates_fin = None
    tok_xnF = {}
    wgen = phW_gen()
    for ti in range(N0):
        emit_expert_casts(8)
        evtoks = {}
        for s4 in range(4):
            sub = ti * 4 + s4
            key = "A" if s4 % 2 == 0 else "B"
            xt = "xtok" + key
            bn, mv, lc, xs = "bnscr" + key, "mv" + key, "lncol" + key, "xn_s" + key
            b0, b1 = (0, 1) if s4 % 2 == 0 else (6, 7)
            next(wgen)
            tok_x = P.emit("sync", lambda s, sub=sub, xt=xt: s.dma_start(
                out=sbv(xt), in_=x_d[sub * 128:(sub + 1) * 128, :]),
                wait=[xuse[key]], inc="LD", inc_n=16)
            tb1 = P.emit("vector", lambda v, xt=xt, bn=bn: v.bn_stats(
                out=slc(bn, 0, 6), in_=slc(xt, 0, 512)),
                wait=[tok_x, bfree.get("st" + key)], inc="EV", hazard=True)
            tb2 = P.emit("vector", lambda v, xt=xt, bn=bn: v.bn_stats(
                out=slc(bn, 6, 12), in_=slc(xt, 512, 1024)),
                wait=[tok_x], inc="EV", hazard=True)
            tb3 = P.emit("vector", lambda v, bn=bn, mv=mv: v.bn_aggr(
                out=sbv(mv), in_=slc(bn, 0, 12)), wait=[tb1, tb2], inc="EV", hazard=True)
            t1 = P.emit("vector", lambda v, mv=mv, lc=lc: v.tensor_scalar(
                out=slc(lc, 0, 1), in0=slc(mv, 1, 2), scalar1=1e-5, scalar2=None,
                op0=ALU.add), wait=[tb3], inc="EV", hazard=True)
            t2 = P.emit("scalar", lambda sc, lc=lc: sc.activation(
                out=slc(lc, 1, 2), in_=slc(lc, 0, 1), func=AF.Ln),
                wait=[t1], inc="AC", hazard=True)
            t3 = P.emit("scalar", lambda sc, lc=lc: sc.activation(
                out=slc(lc, 2, 3), in_=slc(lc, 1, 2), func=AF.Exp, scale=-0.5),
                wait=[t2], inc="AC", hazard=True)
            t_xn = P.emit("vector", lambda v, xt=xt, mv=mv, lc=lc, xs=xs: v.tensor_scalar(
                out=sbv(xs), in0=sbv(xt), scalar1=slc(mv, 0, 1),
                scalar2=slc(lc, 2, 3), op0=ALU.subtract, op1=ALU.mult),
                wait=[t3, bfree.get("xn_s" + key)], inc="EV")
            xuse[key] = t_xn
            t4 = P.emit("vector", lambda v, mv=mv, lc=lc: v.tensor_tensor(
                out=slc(lc, 3, 4), in0=slc(mv, 1, 2), in1=slc(lc, 2, 3),
                op=ALU.mult), wait=[t3], inc="EV", hazard=True)
            t5 = P.emit("vector", lambda v, lc=lc: v.tensor_scalar(
                out=slc(lc, 4, 5), in0=slc(lc, 3, 4), scalar1=slc(lc, 2, 3),
                scalar2=1e-6, op0=ALU.mult, op1=ALU.add), wait=[t4], inc="EV", hazard=True)
            t6 = P.emit("scalar", lambda sc, lc=lc: sc.activation(
                out=slc(lc, 5, 6), in_=slc(lc, 4, 5), func=AF.Ln),
                wait=[t5], inc="AC", hazard=True)
            P.emit("scalar", lambda sc, sub=sub, lc=lc: sc.activation(
                out=slc("rsioc", sub, sub + 1), in_=slc(lc, 5, 6), func=AF.Exp,
                scale=-0.5), wait=[t6], inc="AC", hazard=True)
            bfree["st" + key] = t6
            tkb = {}
            for g in range(G):
                bi = b0 if g < 4 else b1
                tkb[bi] = P.emit("tensor", lambda t, g=g, xs=xs, bi=bi: t.transpose(
                    psums[bi][:, (g % 4) * 128:(g % 4) * 128 + 128],
                    slc(xs, g * 128, g * 128 + 128), sbv("ident")),
                    wait=[t_xn] + ([bfree.get(("p0", bi))] if g % 4 == 0 else []),
                    inc="MM" if g % 4 == 3 else None)
            for half, bi in ((0, b0), (1, b1)):
                tok_ev = P.emit("vector", lambda v, half=half, bi=bi, s4=s4: v.tensor_copy(
                    sbv("xnball").rearrange("p (g q) -> p g q", g=8)[:, half * 4:half * 4 + 4,
                                                                    s4 * 128:(s4 + 1) * 128],
                    psums[bi][:].rearrange("p (g q) -> p g q", g=4)),
                    wait=[tkb[bi], bfree.get(("xne", s4, half)), bfree.get("xnd")], inc="EV")
                evtoks[(s4, half)] = tok_ev
                bfree[("p0", bi)] = tok_ev
            bfree["xn_s" + key] = tkb[b1]
        ev_all = [evtoks[(s4, h)] for s4 in range(4) for h in range(2)]
        for g in range(G):
            d1 = P.emit("gpsimd", lambda gp, g=g, ti=ti: gp.dma_start(
                out=xnF_d[g, :, ti * 512:(ti + 1) * 512],
                in_=slc("xnball", g * 512, (g + 1) * 512)),
                wait=ev_all if g == 0 else (), inc="LG", inc_n=16)
            d2 = P.emit("gpsimd", lambda gp, g=g, ti=ti: gp.dma_start(
                out=xnB_d[g, :, ti * 512:(ti + 1) * 512],
                in_=slc("xnball", g * 512, (g + 1) * 512)),
                inc="LG", inc_n=16)
            tok_xnF[(ti, g)] = (d1, d2)
            bfree["xnd"] = d2
        tok_mm = None
        for g in range(G):
            tok_mm = P.emit("tensor", lambda t, g=g: t.matmul(
                psums[0][0:8, :], slc("gwsb", g * 8, g * 8 + 8),
                slc("xnball", g * 512, (g + 1) * 512),
                start=(g == 0), stop=(g == G - 1)),
                wait=ev_all + [tok_consts] if g == 0 else (),
                inc="MM" if g == G - 1 else None)
        tok_gev = P.emit("vector", lambda v, ti=ti: v.tensor_copy(
            gates_ap(0, 8, ti * 512, (ti + 1) * 512), psums[0][0:8, :]),
            wait=[tok_mm], inc="EV")
        bfree[("p0", 0)] = tok_gev
        for s4 in range(4):
            for h in range(2):
                bfree[("xne", s4, h)] = tok_mm
        tok_gates_fin = tok_gev
    for _ in range(80):
        next(wgen)
    drop("wbnc1", "wbnc0", "wstage")
    emit_expert_casts(len(exp_chunks))
    tok_expB = [exp_state["A"], exp_state["B"]]

    if upto == "ph0":
        return _close(nc, P, semh)

    bfree[("gt", 6)] = bfree.get(("p0", 6))
    bfree[("gt", 7)] = bfree.get(("p0", 7))

    def gq(name):
        return sbv(name).rearrange("p (e q) -> p e q", e=8)

    def gqT(name):
        return sbv(name).rearrange("p (e q) -> p q e", e=8)

    t_cl = P.emit("vector", lambda v: v.tensor_scalar(
        out=gates_ap(), in0=gates_ap(), scalar1=1e-6, scalar2=None, op0=ALU.max),
        wait=[tok_gates_fin], inc="EV", hazard=True)
    tok_g0 = P.emit("scalar", lambda sc: sc.activation(
        out=gates_ap(), in_=gates_ap(), func=AF.Ln), wait=[t_cl], inc="AC", hazard=True)
    tts = None
    tok_gtok = {}
    for t8 in range(NTILE):
        bi = 6 + (t8 // 16) % 2
        tts = P.emit("tensor", lambda t, t8=t8, bi=bi: t.transpose(
            psums[bi][:, (t8 % 16) * 8:(t8 % 16) * 8 + 8],
            gates_ap(0, 8, t8 * 128, (t8 + 1) * 128), sbv("ident")[0:8, 0:8]),
            wait=[tok_g0] + ([bfree.get(("gt", bi))] if t8 % 16 == 0 else []),
            inc="MM" if t8 % 16 == 15 else None)
        if t8 % 16 == 15:
            first = t8 - 15
            tok_ge = P.emit("vector", lambda v, bi=bi, first=first: v.tensor_copy(
                sbv("g_tok").rearrange("p (e q) -> p q e", e=8)[:, first:first + 16, :],
                psums[bi][:, 0:128].rearrange("p (q e) -> p q e", e=8)),
                wait=[tts], inc="EV")
            bfree[("gt", bi)] = tok_ge
            tok_gtok[t8 // 16] = tok_ge

    def skslc(c0, c1, rows=None):
        return slc("sks", c0, c1, rows=rows)

    tok_g = None
    for it in range(8):
        w0 = [tok_g] if tok_g else [tok_gtok[0], tok_gtok[1]]
        t_pm = P.emit("vector", lambda v: v.tensor_reduce(
            out=skslc(112, 120), in_=gq("g_tok"), axis=AX.X, op=ALU.max),
            wait=w0, inc="EV", hazard=True)
        t_tp1 = P.emit("tensor", lambda t: t.transpose(
            psums[5][0:8, 0:128], skslc(112, 120), sbv("ident")),
            wait=[t_pm, bfree.get("snk5")], inc="MM")
        t_pm8 = P.emit("vector", lambda v: v.tensor_copy(sbv("pm8"), psums[5][0:8, 0:128]),
                       wait=[t_tp1], inc="EV", hazard=True)
        t_mx8 = P.emit("vector", lambda v: v.tensor_reduce(
            out=sbv("mx8"), in_=sbv("pm8"), axis=AX.X, op=ALU.max),
            wait=[t_pm8], inc="EV", hazard=True)
        t_tp2 = P.emit("tensor", lambda t: t.transpose(
            psums[5][0:1, 128:136], sbv("mx8"), sbv("ident")[0:8, 0:8]),
            wait=[t_mx8], inc="MM")
        t_mxr = P.emit("vector", lambda v: v.tensor_copy(
            sbv("mxrowR"), psums[5][0:1, 128:136]),
            wait=[t_tp2], inc="EV", hazard=True)
        t_bc1 = P.emit("tensor", lambda t: t.matmul(
            psums[5][:, 136:144], sbv("ones1"), sbv("mxrowR"), start=True, stop=True),
            wait=[t_mxr], inc="MM")
        t_mxbc = P.emit("vector", lambda v: v.tensor_copy(skslc(0, 8), psums[5][:, 136:144]),
                        wait=[t_bc1], inc="EV", hazard=True)
        t_sub = P.emit("vector", lambda v: v.tensor_tensor(
            out=gq("skt"), in0=gq("g_tok"),
            in1=skslc(0, 8).unsqueeze(2).broadcast_to((128, 8, 32)), op=ALU.subtract),
            wait=[t_mxbc], inc="EV", hazard=True)
        t_exp = P.emit("scalar", lambda sc: sc.activation(
            out=sbv("skt"), in_=sbv("skt"), func=AF.Exp), wait=[t_sub], inc="AC", hazard=True)

        def _prow_fn(v):
            with nc.allow_low_precision(reason="partial lse sums; f32r rounding ~1e-4"):
                return v.tensor_reduce(out=sbv("prowR"), in_=gq("skt"), axis=AX.X, op=ALU.add)
        t_pr = P.emit("vector", _prow_fn, wait=[t_exp], inc="EV", hazard=True)
        t_sm = P.emit("tensor", lambda t: t.matmul(
            psums[5][0:1, 144:152], sbv("onesc"), sbv("prowR"), start=True, stop=True),
            wait=[t_pr], inc="MM")
        t_ln = P.emit("scalar", lambda sc: sc.activation(
            out=skslc(120, 128, rows=1), in_=psums[5][0:1, 144:152], func=AF.Ln),
            wait=[t_sm], inc="AC", hazard=True)
        t_lse = P.emit("vector", lambda v: v.tensor_tensor(
            out=sbv("lseR"), in0=skslc(120, 128, rows=1), in1=sbv("mxrowR"), op=ALU.add),
            wait=[t_ln], inc="EV", hazard=True)
        t_bc2 = P.emit("tensor", lambda t: t.matmul(
            psums[5][:, 152:160], sbv("ones1"), sbv("lseR"), start=True, stop=True),
            wait=[t_lse], inc="MM")
        t_lsebc = P.emit("vector", lambda v: v.tensor_copy(skslc(8, 16), psums[5][:, 152:160]),
                         wait=[t_bc2], inc="EV", hazard=True)
        bfree["snk5"] = t_lsebc
        t_nn = P.emit("vector", lambda v: v.tensor_tensor(
            out=gq("g_tok"), in0=gq("g_tok"),
            in1=skslc(8, 16).unsqueeze(2).broadcast_to((128, 8, 32)), op=ALU.subtract),
            wait=[t_lsebc], inc="EV", hazard=True)
        t_em = P.emit("vector", lambda v: v.tensor_reduce(
            out=skslc(16, 48), in_=gqT("g_tok"), axis=AX.X, op=ALU.max),
            wait=[t_nn], inc="EV", hazard=True)
        t_es = P.emit("vector", lambda v: v.tensor_tensor(
            out=gqT("skt"), in0=gqT("g_tok"),
            in1=skslc(16, 48).unsqueeze(2).broadcast_to((128, 32, 8)), op=ALU.subtract),
            wait=[t_em], inc="EV", hazard=True)
        t_ee = P.emit("scalar", lambda sc: sc.activation(
            out=sbv("skt"), in_=sbv("skt"), func=AF.Exp), wait=[t_es], inc="AC", hazard=True)
        t_esum = P.emit("vector", lambda v: v.tensor_reduce(
            out=skslc(48, 80), in_=gqT("skt"), axis=AX.X, op=ALU.add),
            wait=[t_ee], inc="EV", hazard=True)
        t_eln = P.emit("scalar", lambda sc: sc.activation(
            out=skslc(80, 112), in_=skslc(48, 80), func=AF.Ln),
            wait=[t_esum], inc="AC", hazard=True)
        t_else = P.emit("vector", lambda v: v.tensor_tensor(
            out=skslc(80, 112), in0=skslc(80, 112), in1=skslc(16, 48), op=ALU.add),
            wait=[t_eln], inc="EV", hazard=True)
        tok_g = P.emit("vector", lambda v: v.tensor_tensor(
            out=gqT("g_tok"), in0=gqT("g_tok"),
            in1=skslc(80, 112).unsqueeze(2).broadcast_to((128, 32, 8)), op=ALU.subtract),
            wait=[t_else], inc="EV", hazard=True)
    tok_gfin = P.emit("scalar", lambda sc: sc.activation(
        out=sbv("g_tok"), in_=sbv("g_tok"), func=AF.Exp), wait=[tok_g], inc="AC", hazard=True)
    tok_gtok = {0: tok_gfin, 1: tok_gfin}

    if "gtokdbg" in dbg_d:
        P.emit("gpsimd", lambda gp: gp.dma_start(out=dbg_d["gtokdbg"], in_=sbv("g_tok")),
               wait=[tok_gtok[0], tok_gtok[1]], inc="LG", inc_n=16)
    t_lo = P.emit("vector", lambda v: v.tensor_scalar(
        out=slc("bis", 0, 8), in0=slc("g_tok", 0, 8), scalar1=0.0, scalar2=None,
        op0=ALU.mult), wait=[tok_gtok[0], tok_gtok[1]], inc="EV", hazard=True)
    t_hi = P.emit("vector", lambda v: v.tensor_scalar(
        out=slc("bis", 8, 16), in0=slc("g_tok", 0, 8), scalar1=0.0, scalar2=1.001,
        op0=ALU.mult, op1=ALU.add), inc="EV", hazard=True)
    prev = [t_lo, t_hi]
    for bit in range(35):
        t_mid = P.emit("vector", lambda v: v.tensor_tensor(
            out=slc("bis", 16, 24), in0=slc("bis", 0, 8), in1=slc("bis", 8, 16), op=ALU.add),
            wait=prev, inc="EV", hazard=True)
        t_mid = P.emit("vector", lambda v: v.tensor_scalar(
            out=slc("bis", 16, 24), in0=slc("bis", 16, 24), scalar1=0.5, scalar2=None,
            op0=ALU.mult), wait=[t_mid], inc="EV", hazard=True)
        if bit == 0 and "cntdbg" in dbg_d:
            t_snap = P.emit("vector", lambda v: v.tensor_copy(
                slc("bis", 40, 48), slc("bis", 16, 24)), wait=[t_mid], inc="EV", hazard=True)
        t_cmp = P.emit("vector", lambda v: v.tensor_tensor(
            out=gq("selm"), in0=gq("g_tok"),
            in1=slc("bis", 16, 24).unsqueeze(2).broadcast_to((128, 8, 32)),
            op=ALU.is_gt), wait=[t_mid], inc="EV", hazard=True)

        def _cnt_fn(v):
            with nc.allow_low_precision(reason="counts near 512 exact in f32r"):
                return v.tensor_reduce(out=sbv("cntR"), in_=gq("selm"),
                                       axis=AX.X, op=ALU.add)
        t_cnt = P.emit("vector", _cnt_fn, wait=[t_cmp], inc="EV", hazard=True)
        t_m1 = P.emit("tensor", lambda t: t.matmul(
            psums[5][0:1, 0:8], sbv("onesc"), sbv("cntR"),
            start=True, stop=True), wait=[t_cnt], inc="MM")
        t_cr = P.emit("vector", lambda v: v.tensor_copy(
            sbv("crR"), psums[5][0:1, 0:8]), wait=[t_m1], inc="EV", hazard=True)
        t_m2 = P.emit("tensor", lambda t: t.matmul(
            psums[5][:, 8:16], sbv("ones1"), sbv("crR"),
            start=True, stop=True), wait=[t_cr], inc="MM")
        if bit == 0 and "cntdbg" in dbg_d:
            tdb = P.emit("vector", lambda v: v.tensor_copy(
                slc("bis", 32, 40), psums[5][:, 8:16]), wait=[t_m2], inc="EV", hazard=True)
            tdb2 = P.emit("gpsimd", lambda gp: gp.dma_start(
                out=dbg_d["cntdbg"][:, 0:8], in_=slc("bis", 32, 40)),
                wait=[tdb], inc="LG", inc_n=16)
            P.emit("gpsimd", lambda gp: gp.dma_start(
                out=dbg_d["cntdbg"][:, 8:16], in_=slc("bis", 40, 48)),
                wait=[tdb2, ("EV", P.counts["EV"])], inc="LG", inc_n=16)
        t_msk = P.emit("vector", lambda v: v.tensor_scalar(
            out=sbv("bmsk"), in0=psums[5][:, 8:16], scalar1=511.5, scalar2=None,
            op0=ALU.is_ge), wait=[t_m2], inc="EV", hazard=True)
        t_lo = P.emit("vector", lambda v: v.select(
            out=slc("bis", 0, 8), mask=sbv("bmsk"), on_true=slc("bis", 16, 24),
            on_false=slc("bis", 0, 8)), wait=[t_msk], inc="EV", hazard=True)
        t_hi = P.emit("vector", lambda v: v.select(
            out=slc("bis", 8, 16), mask=sbv("bmsk"), on_true=slc("bis", 8, 16),
            on_false=slc("bis", 16, 24)), wait=[t_msk], inc="EV", hazard=True)
        prev = [t_lo, t_hi]
    if "taudbg" in dbg_d:
        P.emit("gpsimd", lambda gp: gp.dma_start(out=dbg_d["taudbg"], in_=slc("bis", 0, 8)),
               wait=[t_lo], inc="LG", inc_n=16)
    t_sel = P.emit("vector", lambda v: v.tensor_tensor(
        out=gq("selm"), in0=gq("g_tok"),
        in1=slc("bis", 0, 8).unsqueeze(2).broadcast_to((128, 8, 32)), op=ALU.is_gt),
        wait=[t_lo], inc="EV", hazard=True)
    t_hard = P.emit("vector", lambda v: v.tensor_scalar(
        out=sbv("hardm"), in0=sbv("g_tok"), scalar1=0.5, scalar2=None, op0=ALU.is_gt),
        wait=[tok_gtok[0], tok_gtok[1]], inc="EV", hazard=True)
    t_nsel = P.emit("vector", lambda v: v.tensor_reduce(
        out=slc("win", 0, 32), in_=gqT("selm"), axis=AX.X, op=ALU.add),
        wait=[t_sel], inc="EV", hazard=True)
    t_ind = P.emit("vector", lambda v: v.tensor_scalar(
        out=slc("win", 32, 64), in0=slc("win", 0, 32), scalar1=1.5, scalar2=None,
        op0=ALU.is_lt), wait=[t_nsel], inc="EV", hazard=True)
    t_w1 = P.emit("vector", lambda v: v.tensor_tensor(
        out=sbv("hardm"), in0=sbv("hardm"), in1=sbv("selm"), op=ALU.mult),
        wait=[t_hard, t_sel], inc="EV", hazard=True)
    t_win = P.emit("vector", lambda v: v.tensor_tensor(
        out=gqT("win"), in0=gqT("hardm"),
        in1=slc("win", 32, 64).unsqueeze(2).broadcast_to((128, 32, 8)), op=ALU.mult),
        wait=[t_w1, t_ind], inc="EV", hazard=True)
    if "windbg" in dbg_d:
        P.emit("gpsimd", lambda gp: gp.dma_start(out=dbg_d["windbg"], in_=gq("win")),
               wait=[t_win], inc="LG", inc_n=16)
    if "seldbg" in dbg_d:
        P.emit("gpsimd", lambda gp: gp.dma_start(out=dbg_d["seldbg"], in_=gq("selm")),
               wait=[t_sel, t_w1], inc="LG", inc_n=16)
    drop("g_tok", "skt", "sks", "pm8", "mx8", "mxrowR", "lseR", "prowR",
         "selm", "hardm", "bis", "bmsk", "cntR", "crR",
         "gatesT", "ewT", "rowA", "rowB",
         "xtokA", "xtokB", "bnscrA", "bnscrB", "mvA", "mvB", "lncolA", "lncolB",
         "xn_sA", "xn_sB", "xnball", "expstgA", "expstgB")

    if upto == "sink":
        return _close(nc, P, semh)

    P.start_phase()
    for nm in ("wlnzT", "wdtT", "wyactT", "wyT"):
        sb(nm, G * 1024, F32R)
    sb("wstage2", 8192)
    small = [("wlnzT", "wlnz", "gsio_c"), ("wdtT", "wdt", "gsio_c"),
             ("wyactT", "wyact", "gsio_c"), ("wyT", "wy", None)]
    stage_free2 = None
    bank_free2 = {}
    for nm, src, scale in small:
        tok_stg = P.emit("sync", lambda s, src=src: s.dma_start(
            out=sbv("wstage2").rearrange("p (fb d) -> p fb d", fb=8),
            in_=w_d[src].rearrange("(fb p) d -> p fb d", p=128)),
            wait=[stage_free2, tok_consts], inc="LD", inc_n=16)
        last_tp = None
        for g in range(G):
            for half in range(2):
                bi = half
                tk = None
                for fbh in range(4):
                    fb = half * 4 + fbh
                    tk = P.emit("tensor", lambda t, g=g, fb=fb, fbh=fbh, bi=bi: t.transpose(
                        psums[bi][:, fbh * 128:(fbh + 1) * 128],
                        slc("wstage2", fb * 1024 + g * 128, fb * 1024 + g * 128 + 128),
                        sbv("ident")),
                        wait=[bank_free2.get(bi), tok_stg] if fbh == 0 else (),
                        inc="MM" if fbh == 3 else None)

                def mk_ev(nm=nm, g=g, half=half, bi=bi, scale=scale):
                    def fn(v):
                        dst = slc(nm, g * 1024 + half * 512, g * 1024 + half * 512 + 512)
                        if scale is None:
                            return v.tensor_copy(dst, psums[bi][:])
                        return v.tensor_scalar(out=dst, in0=psums[bi][:],
                                               scalar1=slc(scale, g, g + 1), scalar2=None,
                                               op0=ALU.mult)
                    return fn
                bank_free2[bi] = P.emit("vector", mk_ev(), wait=[tk], inc="EV")
                last_tp = tk
        stage_free2 = last_tp
    drop("wstage2")

    P.start_phase()
    sb("xt1A", 2048, F32R)
    sb("xt1B", 2048, F32R)
    sb("rsrowS", 256, F32R, rows=1)
    sb("rbc", 256)
    sb("zs", 256)
    sb("ds", 256)
    sb("dscr", 1024)
    sb("hcur", 2048)
    sb("hinit", 8)
    sb("hR", 2048, F32R)
    sb("sacts", 2048)
    sb("xa", 2048, F32R)
    sb("sq", 512, F32R)
    sb("x2s", 1024, F32R)
    sb("rfrF", 256, rows=1)
    sb("rfrR", 256, F32R, rows=1)
    sb("xab", 1024)

    def d0s(p):
        return slc("dscr", p * 256, p * 256 + 256)

    def d1s(p):
        return slc("dscr", 512 + p * 256, 512 + p * 256 + 256)

    pb = {}
    prev_xt = {}
    tok_x2 = {}
    tok_xaT = {}
    xa_guard = []
    for it in range(N1):
        key = "A" if it % 2 == 0 else "B"
        xt = "xt1" + key
        sl0 = it * TAU1
        ldt = []
        for g in range(G):
            ldt.append(P.emit("sync", lambda s, g=g, xt=xt, sl0=sl0: s.dma_start(
                out=slc(xt, g * 256, (g + 1) * 256), in_=xnF_d[g, :, sl0:sl0 + 256]),
                wait=[prev_xt.get(key), prev_xt.get(key + "v"), tok_xnF[(sl0 // 512, g)][0]],
                inc="LD", inc_n=16))
        rtp = None
        for s2 in range(2):
            sub = sl0 // 128 + s2
            rtp = P.emit("tensor", lambda t, sub=sub, s2=s2: t.transpose(
                psums[4][0:1, 256 + s2 * 128: 256 + s2 * 128 + 128].bitcast(F32R),
                slc("rsioc", sub, sub + 1), sbv("identr")),
                wait=[pb.get("4r")] if s2 == 0 else (), inc="MM" if s2 == 1 else None)
        t_rsr = P.emit("vector", lambda v: v.tensor_copy(
            sbv("rsrowS"), psums[4][0:1, 256:512].bitcast(F32R)),
            wait=[rtp, pb.get("rsrow")], inc="EV")
        t_rb = P.emit("tensor", lambda t: t.matmul(
            psums[4][:, 0:256], sbv("ones1"), sbv("rsrowS"), start=True, stop=True),
            wait=[t_rsr, pb.get(4)], inc="MM")
        pb["4r"] = t_rb
        t_rbc = P.emit("vector", lambda v: v.tensor_copy(sbv("rbc"), psums[4][:, 0:256]),
                       wait=[t_rb, pb.get("rbc")], inc="EV")
        pb[4] = t_rbc
        pb["rsrow"] = t_rb
        scan_toks = []
        slt = {}
        for fc in range(G):
            dpar = fc % 2
            bz1 = 0 if fc % 2 == 0 else 2
            bd1 = 1 if fc % 2 == 0 else 3
            mmz = None
            for g in range(G):
                mmz = P.emit("tensor", lambda t, g=g, fc=fc, xt=xt, bz1=bz1: t.matmul(
                    psums[bz1][:, 0:256],
                    slc("wlnzT", g * 1024 + fc * 128, g * 1024 + fc * 128 + 128),
                    slc(xt, g * 256, (g + 1) * 256),
                    start=(g == 0), stop=(g == G - 1)),
                    wait=[ldt[g]] + ([pb.get(("a", bz1))] if g == 0 else []),
                    inc="MM" if g == G - 1 else None)
            mmd = None
            for g in range(G):
                mmd = P.emit("tensor", lambda t, g=g, fc=fc, xt=xt, bd1=bd1: t.matmul(
                    psums[bd1][:, 0:256],
                    slc("wdtT", g * 1024 + fc * 128, g * 1024 + fc * 128 + 128),
                    slc(xt, g * 256, (g + 1) * 256),
                    start=(g == 0), stop=(g == G - 1)),
                    wait=[pb.get(("a", bd1))] if g == 0 else (), inc="MM" if g == G - 1 else None)
            t_z = P.emit("vector", lambda v, bz1=bz1: v.tensor_tensor(
                out=sbv("zs"), in0=psums[bz1][:, 0:256], in1=sbv("rbc"), op=ALU.mult),
                wait=[mmz, t_rbc], inc="EV")
            pb[("a", bz1)] = t_z
            t_d = P.emit("vector", lambda v, bd1=bd1: v.tensor_tensor(
                out=sbv("ds"), in0=psums[bd1][:, 0:256], in1=sbv("rbc"), op=ALU.mult),
                wait=[mmd], inc="EV")
            t_sz = P.emit("scalar", lambda sc, fc=fc: sc.activation(
                out=sbv("zs"), in_=sbv("zs"), func=AF.Sigmoid, bias=slc("blnz_c", fc, fc + 1)),
                wait=[t_z], inc="AC")
            t_d0 = P.emit("scalar", lambda sc, fc=fc, dpar=dpar: sc.activation(
                out=d0s(dpar), in_=sbv("ds"), func=AF.Sigmoid,
                scale=-1.0, bias=slc("nbdt_c", fc, fc + 1)),
                wait=[t_d, scan_toks[-2] if len(scan_toks) >= 2 else pb.get(("dscr", dpar))],
                inc="AC")
            t_sd = P.emit("scalar", lambda sc, fc=fc: sc.activation(
                out=sbv("ds"), in_=sbv("ds"), func=AF.Sigmoid, bias=slc("bdt_c", fc, fc + 1)),
                wait=[t_d0], inc="AC", hazard=True)
            pb[("a", bd1)] = t_sd
            t_d1 = P.emit("vector", lambda v, dpar=dpar: v.tensor_tensor(
                out=d1s(dpar), in0=sbv("ds"), in1=sbv("zs"), op=ALU.mult),
                wait=[t_sd, t_sz,
                      scan_toks[-2] if len(scan_toks) >= 2 else pb.get(("dscr1", dpar))],
                inc="EV")

            def mk_scan(fc=fc, it=it, dpar=dpar):
                def fn(v):
                    init = slc("h0_c", fc, fc + 1) if it == 0 else slc("hinit", fc, fc + 1)
                    return v.tensor_tensor_scan(
                        out=slc("hcur", fc * 256, (fc + 1) * 256),
                        data0=d0s(dpar), data1=d1s(dpar),
                        initial=init, op0=ALU.mult, op1=ALU.add)
                return fn
            t_scan = P.emit("vector", mk_scan(), wait=[t_d0, t_d1], inc="EV")
            P.emit("vector", lambda v, fc=fc: v.tensor_copy(
                slc("hinit", fc, fc + 1), slc("hcur", fc * 256 + 255, fc * 256 + 256)),
                wait=[t_scan], inc="EV")
            scan_toks.append(t_scan)
            by1 = 6 if fc % 2 == 0 else 7
            mmy = None
            for g in range(G):
                mmy = P.emit("tensor", lambda t, g=g, fc=fc, xt=xt, by1=by1: t.matmul(
                    psums[by1][:, 0:256],
                    slc("wyactT", g * 1024 + fc * 128, g * 1024 + fc * 128 + 128),
                    slc(xt, g * 256, (g + 1) * 256),
                    start=(g == 0), stop=(g == G - 1)),
                    wait=[pb.get(("ya", by1))] if g == 0 else (),
                    inc="MM" if g == G - 1 else None)
            t_yev = P.emit("vector", lambda v, fc=fc, by1=by1: v.tensor_tensor(
                out=slc("sacts", fc * 256, (fc + 1) * 256), in0=psums[by1][:, 0:256],
                in1=sbv("rbc"), op=ALU.mult),
                wait=[mmy, t_rbc, pb.get(("sacts", fc))], inc="EV")
            pb[("ya", by1)] = t_yev
            t_sl = P.emit("scalar", lambda sc, fc=fc: sc.activation(
                out=slc("sacts", fc * 256, (fc + 1) * 256),
                in_=slc("sacts", fc * 256, (fc + 1) * 256), func=AF.Silu,
                bias=slc("byact_c", fc, fc + 1)),
                wait=[t_yev], inc="AC", hazard=True)
            slt[fc] = t_sl
        t_hR = P.emit("vector", lambda v: v.tensor_copy(sbv("hR"), sbv("hcur")),
                      wait=[scan_toks[-1], pb.get("hRr")], inc="EV")
        if "hdbg" in dbg_d:
            for g in range(G):
                P.emit("gpsimd", lambda gp, g=g, sl0=sl0: gp.dma_start(
                    out=dbg_d["hdbg"][g, :, sl0:sl0 + 256],
                    in_=slc("hcur", g * 256, (g + 1) * 256)),
                    wait=[t_hR], inc="LG", inc_n=16)
        sq_mm = None
        last_mmy = None
        last_mmw = None
        t_y_prev = None
        t_xa_prev = None
        t_sq = None
        for fc in range(G):
            bw1 = 1 if fc % 2 == 0 else 3
            mmw = None
            for g in range(G):
                mmw = P.emit("tensor", lambda t, g=g, fc=fc, bw1=bw1: t.matmul(
                    psums[bw1][:, 0:256],
                    slc("wyT", g * 1024 + fc * 128, g * 1024 + fc * 128 + 128),
                    slc("hR", g * 256, (g + 1) * 256),
                    start=(g == 0), stop=(g == G - 1)),
                    wait=[t_hR] + ([pb.get(("a", bw1))] if g == 0 else []),
                    inc="MM" if g == G - 1 else None)
            t_y = P.emit("vector", lambda v, fc=fc, bw1=bw1: v.scalar_tensor_tensor(
                out=sbv("zs"), in0=psums[bw1][:, 0:256], scalar=slc("by_c", fc, fc + 1),
                in1=slc("sacts", fc * 256, (fc + 1) * 256), op0=ALU.add, op1=ALU.mult),
                wait=[mmw, slt[fc], t_xa_prev], inc="EV")
            pb[("a", bw1)] = t_y
            pb[("sacts", fc)] = t_y
            t_y_prev = t_y
            t_xa = P.emit("vector", lambda v, fc=fc, xt=xt: v.tensor_tensor(
                out=slc("xa", fc * 256, (fc + 1) * 256),
                in0=slc(xt, fc * 256, (fc + 1) * 256), in1=sbv("zs"), op=ALU.add),
                wait=[t_y] + (xa_guard if fc == 0 else []), inc="EV")
            t_xa_prev = t_xa
            spar = fc % 2
            t_sq = P.emit("scalar", lambda sc, fc=fc, spar=spar: sc.activation(
                out=slc("sq", spar * 256, spar * 256 + 256),
                in_=slc("xa", fc * 256, (fc + 1) * 256), func=AF.Square),
                wait=[t_xa, pb.get(("sq", spar))], inc="AC")
            sq_mm = P.emit("tensor", lambda t, fc=fc, spar=spar: t.matmul(
                psums[5][0:1, 0:256], sbv("onesc"),
                slc("sq", spar * 256, spar * 256 + 256),
                start=(fc == 0), stop=(fc == G - 1)),
                wait=[t_sq] + ([pb.get(5)] if fc == 0 else []),
                inc="MM" if fc == G - 1 else None)
            pb[("sq", spar)] = sq_mm
            last_mmy = mmy
            last_mmw = mmw
        pb["rbc"] = t_yev
        pb["hRr"] = last_mmw
        t_rf0 = P.emit("vector", lambda v: v.tensor_scalar(
            out=sbv("rfrF"), in0=psums[5][0:1, 0:256], scalar1=1.0 / D,
            scalar2=1e-6, op0=ALU.mult, op1=ALU.add), wait=[sq_mm, pb.get("rfF")],
            inc="EV", hazard=True)
        pb[5] = t_rf0
        t_rf1 = P.emit("scalar", lambda sc: sc.activation(
            out=sbv("rfrF"), in_=sbv("rfrF"), func=AF.Ln), wait=[t_rf0], inc="AC", hazard=True)
        pb["rfF"] = t_rf1
        t_rf = P.emit("scalar", lambda sc: sc.activation(
            out=sbv("rfrR"), in_=sbv("rfrF"), func=AF.Exp, scale=-0.5),
            wait=[t_rf1, pb.get("rfR")], inc="AC", hazard=True)
        t_rb2 = P.emit("tensor", lambda t: t.matmul(
            psums[4][:, 256:512], sbv("ones1"), sbv("rfrR"),
            start=True, stop=True), wait=[t_rf, pb.get("4b")], inc="MM")
        pb["rfR"] = t_rb2
        t_x2_last = None
        for fc in range(G):
            slot = fc % 4
            t_x2 = P.emit("vector", lambda v, fc=fc, slot=slot: v.scalar_tensor_tensor(
                out=slc("x2s", slot * 256, slot * 256 + 256),
                in0=slc("xa", fc * 256, (fc + 1) * 256),
                scalar=slc("gffn_c", fc, fc + 1), in1=psums[4][:, 256:512],
                op0=ALU.mult, op1=ALU.mult),
                wait=[t_rb2, tok_x2.get((it, fc - 4), tok_x2.get((it - 1, fc + 4)))],
                inc="EV")
            tok_x2[(it, fc)] = P.emit("gpsimd", lambda gp, fc=fc, sl0=sl0, slot=slot: gp.dma_start(
                out=x2F_d[fc, :, sl0:sl0 + 256],
                in_=slc("x2s", slot * 256, slot * 256 + 256)),
                wait=[t_x2], inc="LG", inc_n=16)
            t_x2_last = t_x2
        pb["4b"] = t_x2_last
        pb[("dscr", 0)] = scan_toks[-2]
        pb[("dscr", 1)] = scan_toks[-1]
        pb[("dscr1", 0)] = scan_toks[-2]
        pb[("dscr1", 1)] = scan_toks[-1]
        xa_guard = [t_x2_last, t_sq]
        for s2 in range(2):
            tks = {}
            for g in range(G):
                bi = 6 + (g // 4)
                tks[g] = P.emit("tensor", lambda t, g=g, s2=s2, bi=bi: t.transpose(
                    psums[bi][:, (g % 4) * 128:(g % 4) * 128 + 128].bitcast(F32R),
                    slc("xa", g * 256 + s2 * 128, g * 256 + s2 * 128 + 128),
                    sbv("identr")),
                    wait=[pb.get(("xat", bi)), pb.get(("ya", bi))] if g % 4 == 0 else (),
                    inc="MM" if g % 4 == 3 else None)
            t_e1 = P.emit("vector", lambda v: v.tensor_copy(
                slc("xab", 0, 512), psums[6][:]), wait=[tks[3], pb.get("xabd")], inc="EV")
            t_e2 = P.emit("vector", lambda v: v.tensor_copy(
                slc("xab", 512, 1024), psums[7][:]), wait=[tks[7]], inc="EV")
            pb[("xat", 6)] = t_e1
            pb[("xat", 7)] = t_e2
            pb[("ya", 6)] = t_e1
            pb[("ya", 7)] = t_e2
            tl = it * 2 + s2
            tok_xaT[tl] = P.emit("gpsimd", lambda gp, tl=tl: gp.dma_start(
                out=xaT_d[tl], in_=sbv("xab")), wait=[t_e1, t_e2], inc="LG", inc_n=16)
            pb["xabd"] = tok_xaT[tl]
            xa_guard += [tks[3], tks[7]]
        prev_xt[key] = last_mmy
        prev_xt[key + "v"] = t_xa_prev
    drop("xt1A", "xt1B", "rsrowS", "rbc", "zs", "ds", "dscr", "hcur", "hinit", "hR",
         "sacts", "xa", "sq", "x2s", "rfrF", "rfrR", "xab",
         "wlnzT", "wdtT", "wyactT", "wyT", "rsioc")

    if upto == "ph1a":
        return _close(nc, P, semh)

    P.start_phase()
    sb("x2tA", 2048, F32R)
    sb("x2tB", 2048, F32R)
    sb("xaTtA", 2048)
    sb("xaTtB", 2048)
    sb("wfcbA", 4096, F32R)
    sb("wfcbB", 4096, F32R)
    sb("wfabA", 4096, F32R)
    sb("wfabB", 4096, F32R)
    sb("woutbA", 4096, F32R)
    sb("woutbB", 4096, F32R)
    sb("hid", 1024, F32R)
    sb("sact1b", 512)
    sb("sbnc", 2048)
    sb("boutrS", D, F32R, rows=1)
    tok_bo = P.emit("sync", lambda s: s.dma_start(out=sbv("boutrS"), in_=boutr_d),
                    inc="LD", inc_n=16)

    prev_x2t = {}
    prev_w = {}
    prev_s = {}

    def emit_1b_loads(it):
        key = "A" if it % 2 == 0 else "B"
        x2t = "x2t" + key
        sl0 = it * TAU1
        lx2 = []
        for g in range(G):
            lx2.append(P.emit("sync", lambda s, g=g, x2t=x2t, sl0=sl0: s.dma_start(
                out=slc(x2t, g * 256, (g + 1) * 256), in_=x2F_d[g, :, sl0:sl0 + 256]),
                wait=[prev_x2t.get(key), tok_x2.get((it, g))], inc="LD", inc_n=16))
        xak = "xaTt" + key
        lxa = []
        for tc in range(2):
            tl = it * 2 + tc
            lxa.append(P.emit("sync", lambda s, tc=tc, tl=tl, xak=xak: s.dma_start(
                out=slc(xak, tc * 1024, (tc + 1) * 1024), in_=xaT_d[tl]),
                wait=[prev_s.get(("sd", key, tc)), tok_xaT.get(tl)], inc="LD", inc_n=16))
        return lx2, lxa

    loads = {0: emit_1b_loads(0)}
    for it in range(N1):
        key = "A" if it % 2 == 0 else "B"
        x2t = "x2t" + key
        sl0 = it * TAU1
        ld_x2, ld_xa = loads.pop(it)
        if it + 1 < N1:
            loads[it + 1] = emit_1b_loads(it + 1)
        tok_offc = None
        hid_guard = {0: None, 1: None}
        for ffc in range(8):
            wk = "A" if ffc % 2 == 0 else "B"
            wfcb, wfab, woutb = "wfcb" + wk, "wfab" + wk, "woutb" + wk
            ld_w = {}
            for g in range(G):
                ld_w[("fc", g)] = P.emit("sync", lambda s, g=g, ffc=ffc, wfcb=wfcb: s.dma_start(
                    out=slc(wfcb, g * 512, (g + 1) * 512),
                    in_=wfcT_d[g, :, ffc * 512:(ffc + 1) * 512]),
                    wait=[prev_w.get(wfcb)] if g == 0 else (), inc="LD", inc_n=16)
                ld_w[("fa", g)] = P.emit("sync", lambda s, g=g, ffc=ffc, wfab=wfab: s.dma_start(
                    out=slc(wfab, g * 512, (g + 1) * 512),
                    in_=wfcaT_d[g, :, ffc * 512:(ffc + 1) * 512]),
                    wait=[prev_w.get(wfab)] if g == 0 else (), inc="LD", inc_n=16)
            ld_wo = {}
            for k4 in range(4):
                ld_wo[k4] = P.emit("sync", lambda s, k4=k4, ffc=ffc, woutb=woutb: s.dma_start(
                    out=slc(woutb, k4 * 1024, (k4 + 1) * 1024),
                    in_=woutT_d[ffc * 4 + k4]),
                    wait=[prev_w.get(woutb)] if k4 == 0 else (), inc="LD", inc_n=16)
            def emit_outs(fs, fpar, t_hid_tok):
                nonlocal tok_offc
                lo = None
                for tc in range(2):
                    for nh in range(2):
                        bo = 4 + tc * 2 + nh
                        is_first = (ffc == 0 and fs == 0)
                        is_tok = (tc == 1 and nh == 1)
                        lo = P.emit(
                            "tensor",
                            lambda t, tc=tc, nh=nh, bo=bo, fpar=fpar, woutb=woutb, k4=fs,
                            is_first=is_first: t.matmul(
                                psums[bo][:, 0:512],
                                slc("hid", fpar * 256 + tc * 128, fpar * 256 + tc * 128 + 128),
                                slc(woutb, k4 * 1024 + nh * 512, k4 * 1024 + nh * 512 + 512),
                                start=is_first, stop=False),
                            wait=[t_hid_tok, ld_wo[fs]]
                            + ([pb.get(("1b", bo))] if is_first else []),
                            inc="MM" if is_tok else None)
                hid_guard[fpar] = lo
                tok_offc = lo

            pend = None
            for fs in range(4):
                fpar = fs % 2
                bz = fpar
                ba = 2 + fpar
                mmf = None
                mma = None
                nrep1b = 2 if (it == 0 and ffc == 0 and fs < 2) else 1
                for _rep in range(nrep1b):
                    for g in range(G):
                        mmf = P.emit("tensor", lambda t, g=g, fs=fs, bz=bz, wfcb=wfcb, x2t=x2t: t.matmul(
                            psums[bz][:, 0:256],
                            slc(wfcb, g * 512 + fs * 128, g * 512 + fs * 128 + 128),
                            slc(x2t, g * 256, (g + 1) * 256),
                            start=(g == 0), stop=(g == G - 1)),
                            wait=[ld_w[("fc", g)], ld_x2[g]] + ([pb.get(("1b", bz))] if g == 0 else []),
                            inc="MM" if g == G - 1 else None)
                    for g in range(G):
                        mma = P.emit("tensor", lambda t, g=g, fs=fs, ba=ba, wfab=wfab, x2t=x2t: t.matmul(
                            psums[ba][:, 0:256],
                            slc(wfab, g * 512 + fs * 128, g * 512 + fs * 128 + 128),
                            slc(x2t, g * 256, (g + 1) * 256),
                            start=(g == 0), stop=(g == G - 1)),
                            wait=[ld_w[("fa", g)]] + ([pb.get(("1b", ba))] if g == 0 else []),
                            inc="MM" if g == G - 1 else None)
                if pend is not None:
                    emit_outs(*pend)
                fcol = ffc * 4 + fs
                t_sl = P.emit("scalar", lambda sc, fcol=fcol, ba=ba, fpar=fpar: sc.activation(
                    out=slc("sact1b", fpar * 256, fpar * 256 + 256),
                    in_=psums[ba][:, 0:256], func=AF.Silu, bias=slc("bfca_c", fcol, fcol + 1)),
                    wait=[mma, pb.get(("sl", fpar))], inc="AC")
                pb[("1b", ba)] = t_sl
                t_hid = P.emit("vector", lambda v, fcol=fcol, bz=bz, fpar=fpar: v.scalar_tensor_tensor(
                    out=slc("hid", fpar * 256, fpar * 256 + 256),
                    in0=psums[bz][:, 0:256], scalar=slc("bfc_c", fcol, fcol + 1),
                    in1=slc("sact1b", fpar * 256, fpar * 256 + 256),
                    op0=ALU.add, op1=ALU.mult),
                    wait=[mmf, t_sl, hid_guard[fpar]], inc="EV")
                pb[("1b", bz)] = t_hid
                pb[("sl", fpar)] = t_hid
                pend = (fs, fpar, t_hid)
            emit_outs(*pend)
            prev_w[wfcb] = tok_offc
            prev_w[wfab] = tok_offc
            prev_w[woutb] = tok_offc
        for tc in range(2):
            for nh in range(2):
                bo = 4 + tc * 2 + nh
                t_b = P.emit("tensor", lambda t, nh=nh, bo=bo: t.matmul(
                    psums[bo][:, 0:512], sbv("ones1"),
                    slc("boutrS", nh * 512, nh * 512 + 512),
                    start=False, stop=True), wait=[tok_bo], inc="MM")
                t_s = P.emit("vector", lambda v, tc=tc, nh=nh, bo=bo, key=key: v.tensor_tensor(
                    out=slc("sbnc", tc * 1024 + nh * 512, tc * 1024 + nh * 512 + 512),
                    in0=psums[bo][:, 0:512],
                    in1=slc("xaTt" + key, tc * 1024 + nh * 512, tc * 1024 + nh * 512 + 512),
                    op=ALU.add), wait=[t_b, ld_xa[tc]], inc="EV")
                pb[("1b", bo)] = t_s
        for tc in range(2):
            tl = it * 2 + tc
            t_sd = P.emit("gpsimd", lambda gp, tc=tc, tl=tl: gp.dma_start(
                out=sT_d[tl], in_=slc("sbnc", tc * 1024, (tc + 1) * 1024)),
                wait=[pb[("1b", 4 + tc * 2)], pb[("1b", 4 + tc * 2 + 1)]], inc="LG", inc_n=16)
            prev_s[("sd", key, tc)] = t_sd
            prev_s[("sT", tl)] = t_sd
        prev_x2t[key] = tok_offc
    drop("x2tA", "x2tB", "xaTtA", "xaTtB", "wfcbA", "wfcbB", "wfabA", "wfabB",
         "woutbA", "woutbB", "hid", "sact1b", "sbnc", "boutrS")

    if upto == "ph1b":
        return _close(nc, P, semh)

    P.start_phase()
    sb("xnbtA", 4096, BF16)
    sb("xnbtB", 4096, BF16)
    sb("expbA", 8192, BF16)
    sb("expbB", 8192, BF16)
    sb("sEA", 4096)
    sb("sEB", 4096)
    sb("xEA", 4096)
    sb("xEB", 4096)
    sb("yacc", 4096)
    sb("outb", 4096)
    prev2 = {}

    def emit_2_loads(it):
        key = "A" if it % 2 == 0 else "B"
        xnbt = "xnbt" + key
        sl0 = it * 512
        lb = []
        for g in range(G):
            lb.append(P.emit("sync", lambda s, g=g, xnbt=xnbt, sl0=sl0: s.dma_start(
                out=slc(xnbt, g * 512, (g + 1) * 512), in_=xnB_d[g, :, sl0:sl0 + 512]),
                wait=[prev2.get(xnbt), tok_xnF[(sl0 // 512, g)][1]], inc="LD", inc_n=16))
        ls_ = []
        lx_ = []
        for tc in range(4):
            tl = it * 4 + tc
            ls_.append(P.emit("sync", lambda s, tc=tc, tl=tl, key=key: s.dma_start(
                out=slc("sE" + key, tc * 1024, (tc + 1) * 1024), in_=sT_d[tl]),
                wait=[prev2.get(("sE", key, tc)), prev_s.get(("sT", tl))], inc="LD", inc_n=16))
            lx_.append(P.emit("sync", lambda s, tc=tc, tl=tl, key=key: s.dma_start(
                out=slc("xE" + key, tc * 1024, (tc + 1) * 1024),
                in_=x_d[tl * 128:(tl + 1) * 128, :]),
                wait=[prev2.get(("xE", key, tc))], inc="LD", inc_n=16))
        return lb, ls_, lx_

    loads2 = {0: emit_2_loads(0)}
    for it in range(N2):
        key = "A" if it % 2 == 0 else "B"
        xnbt = "xnbt" + key
        sl0 = it * 512
        ldb, ld_s, ld_x = loads2.pop(it)
        if it + 1 < N2:
            loads2[it + 1] = emit_2_loads(it + 1)
        acc_last = {}
        last_mm_it = None
        for e in range(E):
            ek = "A" if e % 2 == 0 else "B"
            en = "expb" + ek
            ld_e = {}
            for g in range(G):
                w = []
                if g == 0:
                    w.append(prev2.get((en, "free")))
                    if it == 0 and e < 2:
                        w += tok_expB
                ld_e[g] = P.emit("sync", lambda s, e=e, g=g, en=en: s.dma_start(
                    out=slc(en, g * 1024, (g + 1) * 1024), in_=expB_d[e, g]),
                    wait=w, inc="LD", inc_n=16)
            if it == 0 and e == 0 and "expbdbg" in dbg_d:
                P.emit("gpsimd", lambda gp, en=en: gp.dma_start(
                    out=dbg_d["expbdbg"], in_=sbv(en)),
                    wait=[ld_e[g2] for g2 in range(G)], inc="LG", inc_n=16)
            for tc in range(4):
                for oh in range(2):
                    bo = (e % 2) * 2 + oh
                    mm = None
                    nrep = 2 if (it == 0 and e == 0) else 1
                    for rep in range(nrep):
                        for g in range(G):
                            mm = P.emit("tensor", lambda t, g=g, e=e, oh=oh, tc=tc, bo=bo,
                                        xnbt=xnbt, en=en: t.matmul(
                                psums[bo][:, 0:512],
                                slc(xnbt, g * 512 + tc * 128, g * 512 + tc * 128 + 128),
                                slc(en, g * 1024 + oh * 512, g * 1024 + oh * 512 + 512),
                                start=(g == 0), stop=(g == G - 1)),
                                wait=[ldb[g], ld_e[g]] + ([pb.get(("2", bo))] if g == 0 else []),
                                inc="MM" if g == G - 1 else None)
                    tl = it * 4 + tc

                    def mk_ev(e=e, oh=oh, tc=tc, bo=bo, tl=tl, mm=mm):
                        def fn(v):
                            wcol = sbv("win")[:, e * 32 + tl:e * 32 + tl + 1]
                            dst = slc("yacc", tc * 1024 + oh * 512, tc * 1024 + oh * 512 + 512)
                            if e == 0:
                                return v.tensor_scalar(out=dst, in0=psums[bo][:, 0:512],
                                                       scalar1=wcol, scalar2=None, op0=ALU.mult)
                            return v.scalar_tensor_tensor(out=dst, in0=psums[bo][:, 0:512],
                                                          scalar=wcol, in1=dst,
                                                          op0=ALU.mult, op1=ALU.add)
                        return fn
                    acc_last[(tc, oh)] = P.emit("vector", mk_ev(),
                                                wait=[mm, t_win, acc_last.get((tc, oh))],
                                                inc="EV")
                    pb[("2", bo)] = acc_last[(tc, oh)]
                    last_mm_it = mm
            prev2[(en, "free")] = last_mm_it
        if it == 0 and "yaccdbg" in dbg_d:
            P.emit("gpsimd", lambda gp: gp.dma_start(
                out=dbg_d["yaccdbg"], in_=slc("yacc", 0, 1024)),
                wait=[acc_last[(0, 0)], acc_last[(0, 1)]], inc="LG", inc_n=16)
        for tc in range(4):
            tl = it * 4 + tc
            t_g = P.emit("scalar", lambda sc, tc=tc: sc.activation(
                out=slc("yacc", tc * 1024, (tc + 1) * 1024),
                in_=slc("yacc", tc * 1024, (tc + 1) * 1024), func=AF.Silu),
                wait=[acc_last[(tc, 0)], acc_last[(tc, 1)]], inc="AC", hazard=True)
            t_sg = P.emit("vector", lambda v, tc=tc, key=key: v.tensor_tensor(
                out=slc("outb", tc * 1024, (tc + 1) * 1024),
                in0=slc("yacc", tc * 1024, (tc + 1) * 1024),
                in1=slc("sE" + key, tc * 1024, (tc + 1) * 1024), op=ALU.mult),
                wait=[t_g, ld_s[tc], prev2.get(("outd", tc))], inc="EV")
            t_out = P.emit("vector", lambda v, tc=tc, key=key: v.tensor_tensor(
                out=slc("outb", tc * 1024, (tc + 1) * 1024),
                in0=slc("outb", tc * 1024, (tc + 1) * 1024),
                in1=slc("xE" + key, tc * 1024, (tc + 1) * 1024), op=ALU.add),
                wait=[t_sg, ld_x[tc]], inc="EV")
            t_od = P.emit("gpsimd", lambda gp, tc=tc, tl=tl: gp.dma_start(
                out=out_d[tl * 128:(tl + 1) * 128, :],
                in_=slc("outb", tc * 1024, (tc + 1) * 1024)),
                wait=[t_out], inc="OD", inc_n=16)
            prev2[("sE", key, tc)] = t_sg
            prev2[("xE", key, tc)] = t_out
            prev2[("outd", tc)] = t_od
        prev2[xnbt] = last_mm_it

    return _close(nc, P, semh)


def _close(nc, P, semh):
    waits = tuple((s, v) for s, v in P.counts.items() if v > 0
                  and s in ("LD", "LG", "LS", "OD"))
    P.items.append(("gpsimd", lambda gp: gp.nop(), waits, None))
    with nc.Block() as block:
        P.replay(block, semh)
    return nc


def make_in_maps(inputs):
    x = np.ascontiguousarray(inputs["L0_attn_x"], dtype=np.float32)
    ins = {
        "wlnz": inputs["L0_attn_w_ln_z"], "wdt": inputs["L0_attn_w_dt"],
        "wy": inputs["L0_attn_w_y"], "wyact": inputs["L0_attn_w_yact"],
        "wfc": inputs["L0_attn_w_fc"], "wfca": inputs["L0_attn_w_fcact"],
        "wout": inputs["L0_attn_w_out"], "exps": inputs["L0_ffn_experts"],
        "gw": inputs["L0_ffn_gate_w"][0],
        "blnz": inputs["L0_attn_b_ln_z"], "bdt": inputs["L0_attn_b_dt"],
        "by": inputs["L0_attn_b_y"], "byact": inputs["L0_attn_b_yact"],
        "bfc": inputs["L0_attn_b_fc"], "bfca": inputs["L0_attn_b_fcact"],
        "bout": inputs["L0_attn_b_out"],
        "gsio": inputs["L0_attn_g_sio"], "gffn": inputs["L0_attn_g_ffn"],
        "h0": inputs["L0_attn_h0"],
    }
    ins = {k: np.ascontiguousarray(v, dtype=np.float32) for k, v in ins.items()}
    ins["boutr"] = ins["bout"].reshape(1, D).copy()
    ins["ident"] = np.eye(128, dtype=np.float32)
    ins["identr"] = np.eye(128, dtype=np.float32)
    ins["ones1"] = np.ones((1, 128), dtype=np.float32)
    ins["onesc"] = np.ones((128, 1), dtype=np.float32)
    return [dict(ins, x=x[i]) for i in range(NCORE)]


def kernel(**inputs):
    nc = build_program()
    in_maps = make_in_maps(inputs)
    res = run_bass_kernel_spmd(nc, in_maps, core_ids=list(range(NCORE)))
    out = np.stack([np.asarray(res.results[i]["out"]) for i in range(NCORE)], axis=0)
    return out.astype(np.float32)


# version 12
# speedup vs baseline: 1.1364x; 1.1318x over previous
import numpy as np
import concourse.bass as bass
import concourse.mybir as mybir
from concourse.bass_utils import run_bass_kernel_spmd

F32 = mybir.dt.float32
F32R = mybir.dt.float32r
BF16 = mybir.dt.bfloat16
ALU = mybir.AluOpType
AF = mybir.ActivationFunctionType
AX = mybir.AxisListType

T = 4096
D = 1024
FF = 4096
E = 8
G = 8
NCORE = 8
TAU1 = 256
N0 = T // 512
N1 = T // TAU1
N2 = T // 512
NTILE = T // 128

ENGINES = ("sync", "vector", "scalar", "tensor", "gpsimd")
SEMS = ["LD", "LG", "LS", "MM", "EV", "AC", "PL", "OD"]


class Prog:
    def __init__(self):
        self.counts = {s: 0 for s in SEMS}
        self.items = []
        self.tok_info = {}
        self.phase_bar = ()
        self.phase_pending = set()

    def start_phase(self):
        self.phase_bar = tuple((s, v) for s, v in self.counts.items() if v > 0)
        self.phase_pending = set(ENGINES)

    def emit(self, engine, fn, wait=(), inc=None, inc_n=1, hazard=False):
        wl = [w for w in wait if w is not None]
        if engine in self.phase_pending:
            self.phase_pending.discard(engine)
            wl = list(self.phase_bar) + wl
        tok = None
        if inc is not None:
            self.counts[inc] += inc_n
            tok = (inc, self.counts[inc])
            self.tok_info[tok] = (engine, hazard)
        self.items.append((engine, fn, tuple(wl), (inc, inc_n) if inc else None))
        return tok

    def replay(self, block, sems):
        for eng_name in ENGINES:
            items = [(fn, waits, inc) for (e, fn, waits, inc) in self.items if e == eng_name]
            if not items:
                continue

            def body(eng, _items=items, _eng=eng_name):
                hi = {}
                for fn, waits, inc in _items:
                    for (s, v) in waits:
                        if hi.get(s, -1) >= v:
                            continue
                        eng.wait_ge(sems[s], v)
                        hi[s] = v
                    instr = fn(eng)
                    if inc is not None:
                        instr.then_inc(sems[inc[0]], inc[1])
            getattr(block, eng_name)(body)


def build_program(taps=(), upto="all"):
    nc = bass.Bass("TRN2", target_bir_lowering=False)
    P = Prog()

    def kind(name):
        return "ExternalOutput" if name in taps else "Internal"

    x_d = nc.dram_tensor("x", [T, D], F32, kind="ExternalInput").ap()
    out_d = nc.dram_tensor("out", [T, D], F32, kind="ExternalOutput").ap()
    w_d = {}
    for nm, shape in [("wlnz", [D, D]), ("wdt", [D, D]), ("wy", [D, D]),
                      ("wyact", [D, D]), ("wfc", [FF, D]), ("wfca", [FF, D]),
                      ("wout", [D, FF])]:
        w_d[nm] = nc.dram_tensor(nm, shape, F32, kind="ExternalInput").ap()
    exps_d = nc.dram_tensor("exps", [E, D, D], F32R, kind="ExternalInput").ap()
    gw_d = nc.dram_tensor("gw", [D, E], F32R, kind="ExternalInput").ap()
    vec_d = {}
    for nm, n in [("blnz", D), ("bdt", D), ("by", D), ("byact", D), ("bfc", FF),
                  ("bfca", FF), ("bout", D), ("gsio", D), ("gffn", D), ("h0", D)]:
        vec_d[nm] = nc.dram_tensor(nm, [n], F32, kind="ExternalInput").ap()
    boutr_d = nc.dram_tensor("boutr", [1, D], F32R, kind="ExternalInput").ap()
    ident_d = nc.dram_tensor("ident", [128, 128], F32, kind="ExternalInput").ap()
    identr_d = nc.dram_tensor("identr", [128, 128], F32R, kind="ExternalInput").ap()
    ones1_d = nc.dram_tensor("ones1", [1, 128], F32R, kind="ExternalInput").ap()
    onesc_d = nc.dram_tensor("onesc", [128, 1], F32R, kind="ExternalInput").ap()

    xnF_d = nc.dram_tensor("xnF", [G, 128, T], F32R, kind=kind("xnF")).ap()
    xnB_d = nc.dram_tensor("xnB", [G, 128, T], BF16, kind=kind("xnB")).ap()
    x2F_d = nc.dram_tensor("x2F", [G, 128, T], BF16, kind=kind("x2F")).ap()
    xaT_d = nc.dram_tensor("xaT", [NTILE, 128, D], F32, kind=kind("xaT")).ap()
    sT_d = nc.dram_tensor("sT", [NTILE, 128, D], F32, kind=kind("sT")).ap()
    wfcT_d = nc.dram_tensor("wfcT", [G, 128, FF], BF16, kind=kind("wfcT")).ap()
    wfcaT_d = nc.dram_tensor("wfcaT", [G, 128, FF], BF16, kind=kind("wfcaT")).ap()
    woutT_d = nc.dram_tensor("woutT", [FF // 128, 128, D], BF16, kind=kind("woutT")).ap()
    expB_d = nc.dram_tensor("expB", [E, G, 128, D], BF16, kind=kind("expB")).ap()
    dbg_d = {}
    for nm, shape, dt_ in [("gdbg", [8, T], F32), ("windbg", [128, E, NTILE], F32),
                           ("hdbg", [G, 128, T], F32), ("seldbg", [128, E, NTILE], F32),
                           ("taudbg", [128, 8], F32), ("gtokdbg", [128, 256], F32),
                           ("cntdbg", [128, 16], F32), ("yaccdbg", [128, 1024], F32),
                           ("expbdbg", [128, 8192], BF16)]:
        if nm in taps:
            dbg_d[nm] = nc.dram_tensor(nm, shape, dt_, kind="ExternalOutput").ap()

    psums = [nc.psum_tensor(f"PS{i}", [128, 512], F32).__enter__() for i in range(8)]
    semh = {s: nc.semaphore(s).__enter__() for s in SEMS}

    tens = {}
    dead = {}
    seq = [0]

    def sb(name, cols, dtype=F32, rows=128):
        cm = nc.sbuf_tensor("sb_" + name, [rows, cols], dtype)
        h = cm.__enter__()
        seq[0] += 1
        tens[name] = (h, cm, dtype, rows, seq[0])

    def drop(*names):
        for name in sorted(names, key=lambda n: -tens[n][4]):
            rec = tens.pop(name)
            dead[name] = rec
            rec[1].__exit__(None, None, None)

    def sbv(name, rows=None, row0=0):
        h, _, dt_, r, _s = tens[name] if name in tens else dead[name]
        if rows is None and row0 == 0:
            return h[:]
        return h[row0:(row0 + rows) if rows is not None else r, :]

    def slc(name, c0, c1, rows=None, row0=0):
        h, _, dt_, r, _s = tens[name] if name in tens else dead[name]
        r1 = (row0 + rows) if rows is not None else r
        return h[row0:r1, c0:c1]

    sb("ident", 128)
    sb("identr", 128, F32R)
    sb("ones1", 128, F32R, rows=1)
    sb("onesc", 1, F32R)
    sb("ones1B", 128, BF16, rows=1)
    sb("gwsb", 64, F32R)
    colspec = {"blnz_c": ("blnz", 8), "bdt_c": ("bdt", 8), "nbdt_c": (None, 8),
               "by_c": ("by", 8), "byact_c": ("byact", 8), "bfc_c": ("bfc", 32),
               "bfca_c": ("bfca", 32), "gsio_c": ("gsio", 8), "gffn_c": ("gffn", 8),
               "h0_c": ("h0", 8)}
    for nm, (_, k) in colspec.items():
        sb(nm, k)
    sb("win", 256)

    P.emit("sync", lambda s: s.dma_start(out=sbv("ident"), in_=ident_d), inc="LD", inc_n=16)
    P.emit("sync", lambda s: s.dma_start(out=sbv("identr"), in_=identr_d), inc="LD", inc_n=16)
    P.emit("sync", lambda s: s.dma_start(out=sbv("ones1"), in_=ones1_d), inc="LD", inc_n=16)
    P.emit("gpsimd", lambda gp: gp.dma_start(out=sbv("ones1B"), in_=ones1_d),
           wait=[("LD", 48)], inc="LG", inc_n=16)
    P.emit("sync", lambda s: s.dma_start(out=sbv("onesc"), in_=onesc_d), inc="LD", inc_n=16)
    for g in range(G):
        P.emit("sync", lambda s, g=g: s.dma_start(
            out=slc("gwsb", g * 8, g * 8 + 8), in_=gw_d[g * 128:(g + 1) * 128, :]),
            inc="LD", inc_n=16)
    for nm, (src, k) in colspec.items():
        if src is None:
            continue
        for j in range(k):
            P.emit("sync", lambda s, nm=nm, src=src, j=j: s.dma_start(
                out=slc(nm, j, j + 1),
                in_=vec_d[src][j * 128:(j + 1) * 128].rearrange("(p o) -> p o", o=1)),
                inc="LD", inc_n=16)
    tok_consts = ("LD", P.counts["LD"])
    tok_nbdt = P.emit("vector", lambda v: v.tensor_scalar(
        out=sbv("nbdt_c"), in0=sbv("bdt_c"), scalar1=-1.0, scalar2=None, op0=ALU.mult),
        wait=[tok_consts], inc="EV")

    def phW_gen():
        bank_free = {}
        stg_free = {0: None, 1: None}
        for src, dstT in (("wfc", wfcT_d), ("wfca", wfcaT_d)):
            for fb in range(FF // 128):
                par = fb % 2
                tok_stg = P.emit("sync", lambda s, src=src, fb=fb, par=par: s.dma_start(
                    out=slc("wstage", par * 1024, par * 1024 + 1024),
                    in_=w_d[src][fb * 128:(fb + 1) * 128, :]),
                    wait=[stg_free[par], tok_consts], inc="LD", inc_n=16)
                last_tp = None
                for half in range(2):
                    bi = 2 + half
                    bn = "wbnc0" if half == 0 else "wbnc1"
                    tk = None
                    for gh in range(4):
                        g = half * 4 + gh
                        tk = P.emit("tensor", lambda t, g=g, gh=gh, bi=bi, par=par: t.transpose(
                            psums[bi][:, gh * 128:(gh + 1) * 128],
                            slc("wstage", par * 1024 + g * 128, par * 1024 + g * 128 + 128),
                            sbv("ident")),
                            wait=[bank_free.get(bi), tok_stg] if gh == 0 else (),
                            inc="MM" if gh == 3 else None)
                    tok_ev = P.emit("vector", lambda v, bi=bi, bn=bn: v.tensor_copy(
                        sbv(bn), psums[bi][:]), wait=[tk, bank_free.get((bn,))], inc="EV")
                    bank_free[bi] = tok_ev
                    dmt = None
                    for gh in range(4):
                        g = half * 4 + gh
                        dmt = P.emit("gpsimd", lambda gp, g=g, gh=gh, fb=fb, bn=bn, dstT=dstT:
                                     gp.dma_start(out=dstT[g, :, fb * 128:(fb + 1) * 128],
                                                  in_=slc(bn, gh * 128, gh * 128 + 128)),
                                     wait=[tok_ev], inc="LG", inc_n=16)
                    bank_free[(bn,)] = dmt
                    last_tp = tk
                stg_free[par] = last_tp
                yield
        for db in range(D // 128):
            par = db % 2
            tok_stg = P.emit("sync", lambda s, db=db, par=par: s.dma_start(
                out=slc("wstage", par * 4096, par * 4096 + 4096),
                in_=w_d["wout"][db * 128:(db + 1) * 128, :]),
                wait=[stg_free.get(("wo", par))] if ("wo", par) in stg_free
                else [stg_free[0], stg_free[1]], inc="LD", inc_n=16)
            last_tp = None
            for ffc in range(FF // 128):
                half = ffc % 2
                bi = 2 + half
                bn = "wbnc0" if half == 0 else "wbnc1"
                tok_tp = P.emit("tensor", lambda t, db=db, ffc=ffc, bi=bi, par=par: t.transpose(
                    psums[bi][:, 0:128],
                    slc("wstage", par * 4096 + ffc * 128, par * 4096 + ffc * 128 + 128),
                    sbv("ident")),
                    wait=[bank_free.get(bi), tok_stg], inc="MM")
                tok_ev = P.emit("vector", lambda v, bi=bi, bn=bn: v.tensor_copy(
                    slc(bn, 0, 128), psums[bi][:, 0:128]),
                    wait=[tok_tp, bank_free.get((bn,))], inc="EV")
                bank_free[bi] = tok_ev
                dmt = P.emit("gpsimd", lambda gp, ffc=ffc, db=db, bn=bn: gp.dma_start(
                    out=woutT_d[ffc, :, db * 128:(db + 1) * 128], in_=slc(bn, 0, 128)),
                    wait=[tok_ev], inc="LG", inc_n=16)
                bank_free[(bn,)] = dmt
                last_tp = tok_tp
            stg_free[("wo", par)] = last_tp
            yield
        while True:
            yield

    P.start_phase()
    sb("rsioc", 32, F32R)
    sb("gatesT", T, rows=8)
    sb("ewT", T, rows=8)
    sb("rowA", T, rows=1)
    sb("rowB", T, rows=1)
    sb("xtokA", 1024)
    sb("xtokB", 1024)
    sb("bnscrA", 16)
    sb("bnscrB", 16)
    sb("mvA", 2)
    sb("mvB", 2)
    sb("lncolA", 8)
    sb("lncolB", 8)
    sb("xn_sA", 1024)
    sb("xn_sB", 1024)
    sb("xnball", 4096, F32R)
    sb("expstgA", 1024, F32R)
    sb("expstgB", 1024, F32R)
    sb("g_tok", 256)
    sb("skt", 256)
    sb("sks", 128)
    sb("pm8", 128, rows=8)
    sb("mx8", 1, rows=8)
    sb("mxrowR", 8, F32R, rows=1)
    sb("lseR", 8, F32R, rows=1)
    sb("prowR", 8, F32R)
    sb("selm", 256)
    sb("hardm", 256)
    sb("bis", 48)
    sb("bmsk", 8, mybir.dt.int32)
    sb("cntR", 8, F32R)
    sb("crR", 8, F32R, rows=1)
    sb("wstage", 8192)
    sb("wbnc0", 512, F32R)
    sb("wbnc1", 512, F32R)

    def gates_ap(r0=0, r1=8, c0=0, c1=T):
        return slc("gatesT", c0, c1, rows=r1 - r0, row0=r0)

    def ew_ap(r0=0, r1=8, c0=0, c1=T):
        return slc("ewT", c0, c1, rows=r1 - r0, row0=r0)

    exp_chunks = [(e, g) for e in range(E) for g in range(G)]
    exp_state = {"i": 0, "A": None, "B": None}

    def emit_expert_casts(n):
        for _ in range(n):
            i = exp_state["i"]
            if i >= len(exp_chunks):
                return
            e, g = exp_chunks[i]
            key = "A" if i % 2 == 0 else "B"
            stgn = "expstg" + key
            tok_l = P.emit("sync", lambda s, e=e, g=g, stgn=stgn: s.dma_start(
                out=sbv(stgn), in_=exps_d[e, g * 128:(g + 1) * 128, :]),
                wait=[exp_state[key]], inc="LD", inc_n=16)
            tok_c = P.emit("gpsimd", lambda gp, e=e, g=g, stgn=stgn: gp.dma_start(
                out=expB_d[e, g], in_=sbv(stgn)),
                wait=[tok_l], inc="LG", inc_n=16)
            exp_state[key] = tok_c
            exp_state["i"] = i + 1

    xuse = {"A": None, "B": None}
    bfree = {}
    tok_gates_fin = None
    tok_xnF = {}
    wgen = phW_gen()
    for ti in range(N0):
        emit_expert_casts(8)
        evtoks = {}
        for s4 in range(4):
            sub = ti * 4 + s4
            key = "A" if s4 % 2 == 0 else "B"
            xt = "xtok" + key
            bn, mv, lc, xs = "bnscr" + key, "mv" + key, "lncol" + key, "xn_s" + key
            b0, b1 = (0, 1) if s4 % 2 == 0 else (6, 7)
            next(wgen)
            tok_x = P.emit("sync", lambda s, sub=sub, xt=xt: s.dma_start(
                out=sbv(xt), in_=x_d[sub * 128:(sub + 1) * 128, :]),
                wait=[xuse[key]], inc="LD", inc_n=16)
            tb1 = P.emit("vector", lambda v, xt=xt, bn=bn: v.bn_stats(
                out=slc(bn, 0, 6), in_=slc(xt, 0, 512)),
                wait=[tok_x, bfree.get("st" + key)], inc="EV", hazard=True)
            tb2 = P.emit("vector", lambda v, xt=xt, bn=bn: v.bn_stats(
                out=slc(bn, 6, 12), in_=slc(xt, 512, 1024)),
                wait=[tok_x], inc="EV", hazard=True)
            tb3 = P.emit("vector", lambda v, bn=bn, mv=mv: v.bn_aggr(
                out=sbv(mv), in_=slc(bn, 0, 12)), wait=[tb1, tb2], inc="EV", hazard=True)
            t1 = P.emit("vector", lambda v, mv=mv, lc=lc: v.tensor_scalar(
                out=slc(lc, 0, 1), in0=slc(mv, 1, 2), scalar1=1e-5, scalar2=None,
                op0=ALU.add), wait=[tb3], inc="EV", hazard=True)
            t2 = P.emit("scalar", lambda sc, lc=lc: sc.activation(
                out=slc(lc, 1, 2), in_=slc(lc, 0, 1), func=AF.Ln),
                wait=[t1], inc="AC", hazard=True)
            t3 = P.emit("scalar", lambda sc, lc=lc: sc.activation(
                out=slc(lc, 2, 3), in_=slc(lc, 1, 2), func=AF.Exp, scale=-0.5),
                wait=[t2], inc="AC", hazard=True)
            t_xn = P.emit("vector", lambda v, xt=xt, mv=mv, lc=lc, xs=xs: v.tensor_scalar(
                out=sbv(xs), in0=sbv(xt), scalar1=slc(mv, 0, 1),
                scalar2=slc(lc, 2, 3), op0=ALU.subtract, op1=ALU.mult),
                wait=[t3, bfree.get("xn_s" + key)], inc="EV")
            xuse[key] = t_xn
            t4 = P.emit("vector", lambda v, mv=mv, lc=lc: v.tensor_tensor(
                out=slc(lc, 3, 4), in0=slc(mv, 1, 2), in1=slc(lc, 2, 3),
                op=ALU.mult), wait=[t3], inc="EV", hazard=True)
            t5 = P.emit("vector", lambda v, lc=lc: v.tensor_scalar(
                out=slc(lc, 4, 5), in0=slc(lc, 3, 4), scalar1=slc(lc, 2, 3),
                scalar2=1e-6, op0=ALU.mult, op1=ALU.add), wait=[t4], inc="EV", hazard=True)
            t6 = P.emit("scalar", lambda sc, lc=lc: sc.activation(
                out=slc(lc, 5, 6), in_=slc(lc, 4, 5), func=AF.Ln),
                wait=[t5], inc="AC", hazard=True)
            P.emit("scalar", lambda sc, sub=sub, lc=lc: sc.activation(
                out=slc("rsioc", sub, sub + 1), in_=slc(lc, 5, 6), func=AF.Exp,
                scale=-0.5), wait=[t6], inc="AC", hazard=True)
            bfree["st" + key] = t6
            tkb = {}
            for g in range(G):
                bi = b0 if g < 4 else b1
                tkb[bi] = P.emit("tensor", lambda t, g=g, xs=xs, bi=bi: t.transpose(
                    psums[bi][:, (g % 4) * 128:(g % 4) * 128 + 128],
                    slc(xs, g * 128, g * 128 + 128), sbv("ident")),
                    wait=[t_xn] + ([bfree.get(("p0", bi))] if g % 4 == 0 else []),
                    inc="MM" if g % 4 == 3 else None)
            for half, bi in ((0, b0), (1, b1)):
                tok_ev = P.emit("vector", lambda v, half=half, bi=bi, s4=s4: v.tensor_copy(
                    sbv("xnball").rearrange("p (g q) -> p g q", g=8)[:, half * 4:half * 4 + 4,
                                                                    s4 * 128:(s4 + 1) * 128],
                    psums[bi][:].rearrange("p (g q) -> p g q", g=4)),
                    wait=[tkb[bi], bfree.get(("xne", s4, half)), bfree.get("xnd")], inc="EV")
                evtoks[(s4, half)] = tok_ev
                bfree[("p0", bi)] = tok_ev
            bfree["xn_s" + key] = tkb[b1]
        ev_all = [evtoks[(s4, h)] for s4 in range(4) for h in range(2)]
        for g in range(G):
            d1 = P.emit("gpsimd", lambda gp, g=g, ti=ti: gp.dma_start(
                out=xnF_d[g, :, ti * 512:(ti + 1) * 512],
                in_=slc("xnball", g * 512, (g + 1) * 512)),
                wait=ev_all if g == 0 else (), inc="LG", inc_n=16)
            d2 = P.emit("gpsimd", lambda gp, g=g, ti=ti: gp.dma_start(
                out=xnB_d[g, :, ti * 512:(ti + 1) * 512],
                in_=slc("xnball", g * 512, (g + 1) * 512)),
                inc="LG", inc_n=16)
            tok_xnF[(ti, g)] = (d1, d2)
            bfree["xnd"] = d2
        tok_mm = None
        for g in range(G):
            tok_mm = P.emit("tensor", lambda t, g=g: t.matmul(
                psums[0][0:8, :], slc("gwsb", g * 8, g * 8 + 8),
                slc("xnball", g * 512, (g + 1) * 512),
                start=(g == 0), stop=(g == G - 1)),
                wait=ev_all + [tok_consts] if g == 0 else (),
                inc="MM" if g == G - 1 else None)
        tok_gev = P.emit("vector", lambda v, ti=ti: v.tensor_copy(
            gates_ap(0, 8, ti * 512, (ti + 1) * 512), psums[0][0:8, :]),
            wait=[tok_mm], inc="EV")
        bfree[("p0", 0)] = tok_gev
        for s4 in range(4):
            for h in range(2):
                bfree[("xne", s4, h)] = tok_mm
        tok_gates_fin = tok_gev
    for _ in range(80):
        next(wgen)
    drop("wbnc1", "wbnc0", "wstage")
    emit_expert_casts(len(exp_chunks))
    tok_expB = [exp_state["A"], exp_state["B"]]

    if upto == "ph0":
        return _close(nc, P, semh)

    bfree[("gt", 6)] = bfree.get(("p0", 6))
    bfree[("gt", 7)] = bfree.get(("p0", 7))

    def gq(name):
        return sbv(name).rearrange("p (e q) -> p e q", e=8)

    def gqT(name):
        return sbv(name).rearrange("p (e q) -> p q e", e=8)

    t_cl = P.emit("vector", lambda v: v.tensor_scalar(
        out=gates_ap(), in0=gates_ap(), scalar1=1e-6, scalar2=None, op0=ALU.max),
        wait=[tok_gates_fin], inc="EV", hazard=True)
    tok_g0 = P.emit("scalar", lambda sc: sc.activation(
        out=gates_ap(), in_=gates_ap(), func=AF.Ln), wait=[t_cl], inc="AC", hazard=True)
    tts = None
    tok_gtok = {}
    for t8 in range(NTILE):
        bi = 6 + (t8 // 16) % 2
        tts = P.emit("tensor", lambda t, t8=t8, bi=bi: t.transpose(
            psums[bi][:, (t8 % 16) * 8:(t8 % 16) * 8 + 8],
            gates_ap(0, 8, t8 * 128, (t8 + 1) * 128), sbv("ident")[0:8, 0:8]),
            wait=[tok_g0] + ([bfree.get(("gt", bi))] if t8 % 16 == 0 else []),
            inc="MM" if t8 % 16 == 15 else None)
        if t8 % 16 == 15:
            first = t8 - 15
            tok_ge = P.emit("vector", lambda v, bi=bi, first=first: v.tensor_copy(
                sbv("g_tok").rearrange("p (e q) -> p q e", e=8)[:, first:first + 16, :],
                psums[bi][:, 0:128].rearrange("p (q e) -> p q e", e=8)),
                wait=[tts], inc="EV")
            bfree[("gt", bi)] = tok_ge
            tok_gtok[t8 // 16] = tok_ge

    def skslc(c0, c1, rows=None):
        return slc("sks", c0, c1, rows=rows)

    tok_g = None
    for it in range(8):
        w0 = [tok_g] if tok_g else [tok_gtok[0], tok_gtok[1]]
        t_pm = P.emit("vector", lambda v: v.tensor_reduce(
            out=skslc(112, 120), in_=gq("g_tok"), axis=AX.X, op=ALU.max),
            wait=w0, inc="EV", hazard=True)
        t_tp1 = P.emit("tensor", lambda t: t.transpose(
            psums[5][0:8, 0:128], skslc(112, 120), sbv("ident")),
            wait=[t_pm, bfree.get("snk5")], inc="MM")
        t_pm8 = P.emit("vector", lambda v: v.tensor_copy(sbv("pm8"), psums[5][0:8, 0:128]),
                       wait=[t_tp1], inc="EV", hazard=True)
        t_mx8 = P.emit("vector", lambda v: v.tensor_reduce(
            out=sbv("mx8"), in_=sbv("pm8"), axis=AX.X, op=ALU.max),
            wait=[t_pm8], inc="EV", hazard=True)
        t_tp2 = P.emit("tensor", lambda t: t.transpose(
            psums[5][0:1, 128:136], sbv("mx8"), sbv("ident")[0:8, 0:8]),
            wait=[t_mx8], inc="MM")
        t_mxr = P.emit("vector", lambda v: v.tensor_copy(
            sbv("mxrowR"), psums[5][0:1, 128:136]),
            wait=[t_tp2], inc="EV", hazard=True)
        t_bc1 = P.emit("tensor", lambda t: t.matmul(
            psums[5][:, 136:144], sbv("ones1"), sbv("mxrowR"), start=True, stop=True),
            wait=[t_mxr], inc="MM")
        t_mxbc = P.emit("vector", lambda v: v.tensor_copy(skslc(0, 8), psums[5][:, 136:144]),
                        wait=[t_bc1], inc="EV", hazard=True)
        t_sub = P.emit("vector", lambda v: v.tensor_tensor(
            out=gq("skt"), in0=gq("g_tok"),
            in1=skslc(0, 8).unsqueeze(2).broadcast_to((128, 8, 32)), op=ALU.subtract),
            wait=[t_mxbc], inc="EV", hazard=True)
        t_exp = P.emit("scalar", lambda sc: sc.activation(
            out=sbv("skt"), in_=sbv("skt"), func=AF.Exp), wait=[t_sub], inc="AC", hazard=True)

        def _prow_fn(v):
            with nc.allow_low_precision(reason="partial lse sums; f32r rounding ~1e-4"):
                return v.tensor_reduce(out=sbv("prowR"), in_=gq("skt"), axis=AX.X, op=ALU.add)
        t_pr = P.emit("vector", _prow_fn, wait=[t_exp], inc="EV", hazard=True)
        t_sm = P.emit("tensor", lambda t: t.matmul(
            psums[5][0:1, 144:152], sbv("onesc"), sbv("prowR"), start=True, stop=True),
            wait=[t_pr], inc="MM")
        t_ln = P.emit("scalar", lambda sc: sc.activation(
            out=skslc(120, 128, rows=1), in_=psums[5][0:1, 144:152], func=AF.Ln),
            wait=[t_sm], inc="AC", hazard=True)
        t_lse = P.emit("vector", lambda v: v.tensor_tensor(
            out=sbv("lseR"), in0=skslc(120, 128, rows=1), in1=sbv("mxrowR"), op=ALU.add),
            wait=[t_ln], inc="EV", hazard=True)
        t_bc2 = P.emit("tensor", lambda t: t.matmul(
            psums[5][:, 152:160], sbv("ones1"), sbv("lseR"), start=True, stop=True),
            wait=[t_lse], inc="MM")
        t_lsebc = P.emit("vector", lambda v: v.tensor_copy(skslc(8, 16), psums[5][:, 152:160]),
                         wait=[t_bc2], inc="EV", hazard=True)
        bfree["snk5"] = t_lsebc
        t_nn = P.emit("vector", lambda v: v.tensor_tensor(
            out=gq("g_tok"), in0=gq("g_tok"),
            in1=skslc(8, 16).unsqueeze(2).broadcast_to((128, 8, 32)), op=ALU.subtract),
            wait=[t_lsebc], inc="EV", hazard=True)
        t_em = P.emit("vector", lambda v: v.tensor_reduce(
            out=skslc(16, 48), in_=gqT("g_tok"), axis=AX.X, op=ALU.max),
            wait=[t_nn], inc="EV", hazard=True)
        t_es = P.emit("vector", lambda v: v.tensor_tensor(
            out=gqT("skt"), in0=gqT("g_tok"),
            in1=skslc(16, 48).unsqueeze(2).broadcast_to((128, 32, 8)), op=ALU.subtract),
            wait=[t_em], inc="EV", hazard=True)
        t_ee = P.emit("scalar", lambda sc: sc.activation(
            out=sbv("skt"), in_=sbv("skt"), func=AF.Exp), wait=[t_es], inc="AC", hazard=True)
        t_esum = P.emit("vector", lambda v: v.tensor_reduce(
            out=skslc(48, 80), in_=gqT("skt"), axis=AX.X, op=ALU.add),
            wait=[t_ee], inc="EV", hazard=True)
        t_eln = P.emit("scalar", lambda sc: sc.activation(
            out=skslc(80, 112), in_=skslc(48, 80), func=AF.Ln),
            wait=[t_esum], inc="AC", hazard=True)
        t_else = P.emit("vector", lambda v: v.tensor_tensor(
            out=skslc(80, 112), in0=skslc(80, 112), in1=skslc(16, 48), op=ALU.add),
            wait=[t_eln], inc="EV", hazard=True)
        tok_g = P.emit("vector", lambda v: v.tensor_tensor(
            out=gqT("g_tok"), in0=gqT("g_tok"),
            in1=skslc(80, 112).unsqueeze(2).broadcast_to((128, 32, 8)), op=ALU.subtract),
            wait=[t_else], inc="EV", hazard=True)
    tok_gfin = P.emit("scalar", lambda sc: sc.activation(
        out=sbv("g_tok"), in_=sbv("g_tok"), func=AF.Exp), wait=[tok_g], inc="AC", hazard=True)
    tok_gtok = {0: tok_gfin, 1: tok_gfin}

    if "gtokdbg" in dbg_d:
        P.emit("gpsimd", lambda gp: gp.dma_start(out=dbg_d["gtokdbg"], in_=sbv("g_tok")),
               wait=[tok_gtok[0], tok_gtok[1]], inc="LG", inc_n=16)
    t_lo = P.emit("vector", lambda v: v.tensor_scalar(
        out=slc("bis", 0, 8), in0=slc("g_tok", 0, 8), scalar1=0.0, scalar2=None,
        op0=ALU.mult), wait=[tok_gtok[0], tok_gtok[1]], inc="EV", hazard=True)
    t_hi = P.emit("vector", lambda v: v.tensor_scalar(
        out=slc("bis", 8, 16), in0=slc("g_tok", 0, 8), scalar1=0.0, scalar2=1.001,
        op0=ALU.mult, op1=ALU.add), inc="EV", hazard=True)
    prev = [t_lo, t_hi]
    for bit in range(35):
        t_mid = P.emit("vector", lambda v: v.tensor_tensor(
            out=slc("bis", 16, 24), in0=slc("bis", 0, 8), in1=slc("bis", 8, 16), op=ALU.add),
            wait=prev, inc="EV", hazard=True)
        t_mid = P.emit("vector", lambda v: v.tensor_scalar(
            out=slc("bis", 16, 24), in0=slc("bis", 16, 24), scalar1=0.5, scalar2=None,
            op0=ALU.mult), wait=[t_mid], inc="EV", hazard=True)
        if bit == 0 and "cntdbg" in dbg_d:
            t_snap = P.emit("vector", lambda v: v.tensor_copy(
                slc("bis", 40, 48), slc("bis", 16, 24)), wait=[t_mid], inc="EV", hazard=True)
        t_cmp = P.emit("vector", lambda v: v.tensor_tensor(
            out=gq("selm"), in0=gq("g_tok"),
            in1=slc("bis", 16, 24).unsqueeze(2).broadcast_to((128, 8, 32)),
            op=ALU.is_gt), wait=[t_mid], inc="EV", hazard=True)

        def _cnt_fn(v):
            with nc.allow_low_precision(reason="counts near 512 exact in f32r"):
                return v.tensor_reduce(out=sbv("cntR"), in_=gq("selm"),
                                       axis=AX.X, op=ALU.add)
        t_cnt = P.emit("vector", _cnt_fn, wait=[t_cmp], inc="EV", hazard=True)
        t_m1 = P.emit("tensor", lambda t: t.matmul(
            psums[5][0:1, 0:8], sbv("onesc"), sbv("cntR"),
            start=True, stop=True), wait=[t_cnt], inc="MM")
        t_cr = P.emit("vector", lambda v: v.tensor_copy(
            sbv("crR"), psums[5][0:1, 0:8]), wait=[t_m1], inc="EV", hazard=True)
        t_m2 = P.emit("tensor", lambda t: t.matmul(
            psums[5][:, 8:16], sbv("ones1"), sbv("crR"),
            start=True, stop=True), wait=[t_cr], inc="MM")
        if bit == 0 and "cntdbg" in dbg_d:
            tdb = P.emit("vector", lambda v: v.tensor_copy(
                slc("bis", 32, 40), psums[5][:, 8:16]), wait=[t_m2], inc="EV", hazard=True)
            tdb2 = P.emit("gpsimd", lambda gp: gp.dma_start(
                out=dbg_d["cntdbg"][:, 0:8], in_=slc("bis", 32, 40)),
                wait=[tdb], inc="LG", inc_n=16)
            P.emit("gpsimd", lambda gp: gp.dma_start(
                out=dbg_d["cntdbg"][:, 8:16], in_=slc("bis", 40, 48)),
                wait=[tdb2, ("EV", P.counts["EV"])], inc="LG", inc_n=16)
        t_msk = P.emit("vector", lambda v: v.tensor_scalar(
            out=sbv("bmsk"), in0=psums[5][:, 8:16], scalar1=511.5, scalar2=None,
            op0=ALU.is_ge), wait=[t_m2], inc="EV", hazard=True)
        t_lo = P.emit("vector", lambda v: v.select(
            out=slc("bis", 0, 8), mask=sbv("bmsk"), on_true=slc("bis", 16, 24),
            on_false=slc("bis", 0, 8)), wait=[t_msk], inc="EV", hazard=True)
        t_hi = P.emit("vector", lambda v: v.select(
            out=slc("bis", 8, 16), mask=sbv("bmsk"), on_true=slc("bis", 8, 16),
            on_false=slc("bis", 16, 24)), wait=[t_msk], inc="EV", hazard=True)
        prev = [t_lo, t_hi]
    if "taudbg" in dbg_d:
        P.emit("gpsimd", lambda gp: gp.dma_start(out=dbg_d["taudbg"], in_=slc("bis", 0, 8)),
               wait=[t_lo], inc="LG", inc_n=16)
    t_sel = P.emit("vector", lambda v: v.tensor_tensor(
        out=gq("selm"), in0=gq("g_tok"),
        in1=slc("bis", 0, 8).unsqueeze(2).broadcast_to((128, 8, 32)), op=ALU.is_gt),
        wait=[t_lo], inc="EV", hazard=True)
    t_hard = P.emit("vector", lambda v: v.tensor_scalar(
        out=sbv("hardm"), in0=sbv("g_tok"), scalar1=0.5, scalar2=None, op0=ALU.is_gt),
        wait=[tok_gtok[0], tok_gtok[1]], inc="EV", hazard=True)
    t_nsel = P.emit("vector", lambda v: v.tensor_reduce(
        out=slc("win", 0, 32), in_=gqT("selm"), axis=AX.X, op=ALU.add),
        wait=[t_sel], inc="EV", hazard=True)
    t_ind = P.emit("vector", lambda v: v.tensor_scalar(
        out=slc("win", 32, 64), in0=slc("win", 0, 32), scalar1=1.5, scalar2=None,
        op0=ALU.is_lt), wait=[t_nsel], inc="EV", hazard=True)
    t_w1 = P.emit("vector", lambda v: v.tensor_tensor(
        out=sbv("hardm"), in0=sbv("hardm"), in1=sbv("selm"), op=ALU.mult),
        wait=[t_hard, t_sel], inc="EV", hazard=True)
    t_win = P.emit("vector", lambda v: v.tensor_tensor(
        out=gqT("win"), in0=gqT("hardm"),
        in1=slc("win", 32, 64).unsqueeze(2).broadcast_to((128, 32, 8)), op=ALU.mult),
        wait=[t_w1, t_ind], inc="EV", hazard=True)
    if "windbg" in dbg_d:
        P.emit("gpsimd", lambda gp: gp.dma_start(out=dbg_d["windbg"], in_=gq("win")),
               wait=[t_win], inc="LG", inc_n=16)
    if "seldbg" in dbg_d:
        P.emit("gpsimd", lambda gp: gp.dma_start(out=dbg_d["seldbg"], in_=gq("selm")),
               wait=[t_sel, t_w1], inc="LG", inc_n=16)
    drop("g_tok", "skt", "sks", "pm8", "mx8", "mxrowR", "lseR", "prowR",
         "selm", "hardm", "bis", "bmsk", "cntR", "crR",
         "gatesT", "ewT", "rowA", "rowB",
         "xtokA", "xtokB", "bnscrA", "bnscrB", "mvA", "mvB", "lncolA", "lncolB",
         "xn_sA", "xn_sB", "xnball", "expstgA", "expstgB")

    if upto == "sink":
        return _close(nc, P, semh)

    P.start_phase()
    for nm in ("wlnzT", "wdtT", "wyactT", "wyT"):
        sb(nm, G * 1024, F32R)
    sb("wstage2", 8192)
    small = [("wlnzT", "wlnz", "gsio_c"), ("wdtT", "wdt", "gsio_c"),
             ("wyactT", "wyact", "gsio_c"), ("wyT", "wy", None)]
    stage_free2 = None
    bank_free2 = {}
    for nm, src, scale in small:
        tok_stg = P.emit("sync", lambda s, src=src: s.dma_start(
            out=sbv("wstage2").rearrange("p (fb d) -> p fb d", fb=8),
            in_=w_d[src].rearrange("(fb p) d -> p fb d", p=128)),
            wait=[stage_free2, tok_consts], inc="LD", inc_n=16)
        last_tp = None
        for g in range(G):
            for half in range(2):
                bi = half
                tk = None
                for fbh in range(4):
                    fb = half * 4 + fbh
                    tk = P.emit("tensor", lambda t, g=g, fb=fb, fbh=fbh, bi=bi: t.transpose(
                        psums[bi][:, fbh * 128:(fbh + 1) * 128],
                        slc("wstage2", fb * 1024 + g * 128, fb * 1024 + g * 128 + 128),
                        sbv("ident")),
                        wait=[bank_free2.get(bi), tok_stg] if fbh == 0 else (),
                        inc="MM" if fbh == 3 else None)

                def mk_ev(nm=nm, g=g, half=half, bi=bi, scale=scale):
                    def fn(v):
                        dst = slc(nm, g * 1024 + half * 512, g * 1024 + half * 512 + 512)
                        if scale is None:
                            return v.tensor_copy(dst, psums[bi][:])
                        return v.tensor_scalar(out=dst, in0=psums[bi][:],
                                               scalar1=slc(scale, g, g + 1), scalar2=None,
                                               op0=ALU.mult)
                    return fn
                bank_free2[bi] = P.emit("vector", mk_ev(), wait=[tk], inc="EV")
                last_tp = tk
        stage_free2 = last_tp
    drop("wstage2")

    P.start_phase()
    sb("xt1A", 2048, F32R)
    sb("xt1B", 2048, F32R)
    sb("rsrowS", 256, F32R, rows=1)
    sb("rbc", 256)
    sb("zs", 256)
    sb("ds", 256)
    sb("dscr", 1024)
    sb("hcur", 2048)
    sb("hinit", 8)
    sb("hR", 2048, F32R)
    sb("sacts", 2048)
    sb("xa", 2048, F32R)
    sb("sq", 512, F32R)
    sb("x2s", 1024, F32R)
    sb("rfrF", 256, rows=1)
    sb("rfrR", 256, F32R, rows=1)
    sb("xab", 1024)

    def d0s(p):
        return slc("dscr", p * 256, p * 256 + 256)

    def d1s(p):
        return slc("dscr", 512 + p * 256, 512 + p * 256 + 256)

    pb = {}
    prev_xt = {}
    tok_x2 = {}
    tok_xaT = {}
    xa_guard = []
    for it in range(N1):
        key = "A" if it % 2 == 0 else "B"
        xt = "xt1" + key
        sl0 = it * TAU1
        tok_ldt = P.emit("sync", lambda s, xt=xt, sl0=sl0: s.dma_start(
            out=sbv(xt).rearrange("p (g c) -> p g c", g=G),
            in_=xnF_d[:, :, sl0:sl0 + 256].rearrange("g p c -> p g c")),
            wait=[prev_xt.get(key), prev_xt.get(key + "v")]
            + [tok_xnF[(sl0 // 512, g)][0] for g in range(G)],
            inc="LD", inc_n=16)
        ldt = [tok_ldt] * G
        rtp = None
        for s2 in range(2):
            sub = sl0 // 128 + s2
            rtp = P.emit("tensor", lambda t, sub=sub, s2=s2: t.transpose(
                psums[4][0:1, 256 + s2 * 128: 256 + s2 * 128 + 128].bitcast(F32R),
                slc("rsioc", sub, sub + 1), sbv("identr")),
                wait=[pb.get("4r")] if s2 == 0 else (), inc="MM" if s2 == 1 else None)
        t_rsr = P.emit("vector", lambda v: v.tensor_copy(
            sbv("rsrowS"), psums[4][0:1, 256:512].bitcast(F32R)),
            wait=[rtp, pb.get("rsrow")], inc="EV")
        t_rb = P.emit("tensor", lambda t: t.matmul(
            psums[4][:, 0:256], sbv("ones1"), sbv("rsrowS"), start=True, stop=True),
            wait=[t_rsr, pb.get(4)], inc="MM")
        pb["4r"] = t_rb
        t_rbc = P.emit("vector", lambda v: v.tensor_copy(sbv("rbc"), psums[4][:, 0:256]),
                       wait=[t_rb, pb.get("rbc")], inc="EV")
        pb[4] = t_rbc
        pb["rsrow"] = t_rb
        scan_toks = []
        slt = {}
        for fc in range(G):
            dpar = fc % 2
            bz1 = 0 if fc % 2 == 0 else 2
            bd1 = 1 if fc % 2 == 0 else 3
            mmz = None
            for g in range(G):
                mmz = P.emit("tensor", lambda t, g=g, fc=fc, xt=xt, bz1=bz1: t.matmul(
                    psums[bz1][:, 0:256],
                    slc("wlnzT", g * 1024 + fc * 128, g * 1024 + fc * 128 + 128),
                    slc(xt, g * 256, (g + 1) * 256),
                    start=(g == 0), stop=(g == G - 1)),
                    wait=[ldt[g]] + ([pb.get(("a", bz1))] if g == 0 else []),
                    inc="MM" if g == G - 1 else None)
            mmd = None
            for g in range(G):
                mmd = P.emit("tensor", lambda t, g=g, fc=fc, xt=xt, bd1=bd1: t.matmul(
                    psums[bd1][:, 0:256],
                    slc("wdtT", g * 1024 + fc * 128, g * 1024 + fc * 128 + 128),
                    slc(xt, g * 256, (g + 1) * 256),
                    start=(g == 0), stop=(g == G - 1)),
                    wait=[pb.get(("a", bd1))] if g == 0 else (), inc="MM" if g == G - 1 else None)
            t_z = P.emit("vector", lambda v, bz1=bz1: v.tensor_tensor(
                out=sbv("zs"), in0=psums[bz1][:, 0:256], in1=sbv("rbc"), op=ALU.mult),
                wait=[mmz, t_rbc], inc="EV")
            pb[("a", bz1)] = t_z
            t_d = P.emit("vector", lambda v, bd1=bd1: v.tensor_tensor(
                out=sbv("ds"), in0=psums[bd1][:, 0:256], in1=sbv("rbc"), op=ALU.mult),
                wait=[mmd], inc="EV")
            t_sz = P.emit("scalar", lambda sc, fc=fc: sc.activation(
                out=sbv("zs"), in_=sbv("zs"), func=AF.Sigmoid, bias=slc("blnz_c", fc, fc + 1)),
                wait=[t_z], inc="AC")
            t_d0 = P.emit("scalar", lambda sc, fc=fc, dpar=dpar: sc.activation(
                out=d0s(dpar), in_=sbv("ds"), func=AF.Sigmoid,
                scale=-1.0, bias=slc("nbdt_c", fc, fc + 1)),
                wait=[t_d, scan_toks[-2] if len(scan_toks) >= 2 else pb.get(("dscr", dpar))],
                inc="AC")
            t_sd = P.emit("scalar", lambda sc, fc=fc: sc.activation(
                out=sbv("ds"), in_=sbv("ds"), func=AF.Sigmoid, bias=slc("bdt_c", fc, fc + 1)),
                wait=[t_d0], inc="AC", hazard=True)
            pb[("a", bd1)] = t_sd
            t_d1 = P.emit("vector", lambda v, dpar=dpar: v.tensor_tensor(
                out=d1s(dpar), in0=sbv("ds"), in1=sbv("zs"), op=ALU.mult),
                wait=[t_sd, t_sz,
                      scan_toks[-2] if len(scan_toks) >= 2 else pb.get(("dscr1", dpar))],
                inc="EV")

            def mk_scan(fc=fc, it=it, dpar=dpar):
                def fn(v):
                    init = slc("h0_c", fc, fc + 1) if it == 0 else slc("hinit", fc, fc + 1)
                    return v.tensor_tensor_scan(
                        out=slc("hcur", fc * 256, (fc + 1) * 256),
                        data0=d0s(dpar), data1=d1s(dpar),
                        initial=init, op0=ALU.mult, op1=ALU.add)
                return fn
            t_scan = P.emit("vector", mk_scan(), wait=[t_d0, t_d1], inc="EV")
            P.emit("vector", lambda v, fc=fc: v.tensor_copy(
                slc("hinit", fc, fc + 1), slc("hcur", fc * 256 + 255, fc * 256 + 256)),
                wait=[t_scan], inc="EV")
            scan_toks.append(t_scan)
            by1 = 6 if fc % 2 == 0 else 7
            mmy = None
            for g in range(G):
                mmy = P.emit("tensor", lambda t, g=g, fc=fc, xt=xt, by1=by1: t.matmul(
                    psums[by1][:, 0:256],
                    slc("wyactT", g * 1024 + fc * 128, g * 1024 + fc * 128 + 128),
                    slc(xt, g * 256, (g + 1) * 256),
                    start=(g == 0), stop=(g == G - 1)),
                    wait=[pb.get(("ya", by1))] if g == 0 else (),
                    inc="MM" if g == G - 1 else None)
            t_yev = P.emit("vector", lambda v, fc=fc, by1=by1: v.tensor_tensor(
                out=slc("sacts", fc * 256, (fc + 1) * 256), in0=psums[by1][:, 0:256],
                in1=sbv("rbc"), op=ALU.mult),
                wait=[mmy, t_rbc, pb.get(("sacts", fc))], inc="EV")
            pb[("ya", by1)] = t_yev
            t_sl = P.emit("scalar", lambda sc, fc=fc: sc.activation(
                out=slc("sacts", fc * 256, (fc + 1) * 256),
                in_=slc("sacts", fc * 256, (fc + 1) * 256), func=AF.Silu,
                bias=slc("byact_c", fc, fc + 1)),
                wait=[t_yev], inc="AC", hazard=True)
            slt[fc] = t_sl
        t_hR = P.emit("vector", lambda v: v.tensor_copy(sbv("hR"), sbv("hcur")),
                      wait=[scan_toks[-1], pb.get("hRr")], inc="EV")
        if "hdbg" in dbg_d:
            for g in range(G):
                P.emit("gpsimd", lambda gp, g=g, sl0=sl0: gp.dma_start(
                    out=dbg_d["hdbg"][g, :, sl0:sl0 + 256],
                    in_=slc("hcur", g * 256, (g + 1) * 256)),
                    wait=[t_hR], inc="LG", inc_n=16)
        sq_mm = None
        last_mmy = None
        last_mmw = None
        t_y_prev = None
        t_xa_prev = None
        t_sq = None
        for fc in range(G):
            bw1 = 1 if fc % 2 == 0 else 3
            mmw = None
            for g in range(G):
                mmw = P.emit("tensor", lambda t, g=g, fc=fc, bw1=bw1: t.matmul(
                    psums[bw1][:, 0:256],
                    slc("wyT", g * 1024 + fc * 128, g * 1024 + fc * 128 + 128),
                    slc("hR", g * 256, (g + 1) * 256),
                    start=(g == 0), stop=(g == G - 1)),
                    wait=[t_hR] + ([pb.get(("a", bw1))] if g == 0 else []),
                    inc="MM" if g == G - 1 else None)
            t_y = P.emit("vector", lambda v, fc=fc, bw1=bw1: v.scalar_tensor_tensor(
                out=sbv("zs"), in0=psums[bw1][:, 0:256], scalar=slc("by_c", fc, fc + 1),
                in1=slc("sacts", fc * 256, (fc + 1) * 256), op0=ALU.add, op1=ALU.mult),
                wait=[mmw, slt[fc], t_xa_prev], inc="EV")
            pb[("a", bw1)] = t_y
            pb[("sacts", fc)] = t_y
            t_y_prev = t_y
            t_xa = P.emit("vector", lambda v, fc=fc, xt=xt: v.tensor_tensor(
                out=slc("xa", fc * 256, (fc + 1) * 256),
                in0=slc(xt, fc * 256, (fc + 1) * 256), in1=sbv("zs"), op=ALU.add),
                wait=[t_y] + (xa_guard if fc == 0 else []), inc="EV")
            t_xa_prev = t_xa
            spar = fc % 2
            t_sq = P.emit("scalar", lambda sc, fc=fc, spar=spar: sc.activation(
                out=slc("sq", spar * 256, spar * 256 + 256),
                in_=slc("xa", fc * 256, (fc + 1) * 256), func=AF.Square),
                wait=[t_xa, pb.get(("sq", spar))], inc="AC")
            sq_mm = P.emit("tensor", lambda t, fc=fc, spar=spar: t.matmul(
                psums[5][0:1, 0:256], sbv("onesc"),
                slc("sq", spar * 256, spar * 256 + 256),
                start=(fc == 0), stop=(fc == G - 1)),
                wait=[t_sq] + ([pb.get(5)] if fc == 0 else []),
                inc="MM" if fc == G - 1 else None)
            pb[("sq", spar)] = sq_mm
            last_mmy = mmy
            last_mmw = mmw
        pb["rbc"] = t_yev
        pb["hRr"] = last_mmw
        t_rf0 = P.emit("vector", lambda v: v.tensor_scalar(
            out=sbv("rfrF"), in0=psums[5][0:1, 0:256], scalar1=1.0 / D,
            scalar2=1e-6, op0=ALU.mult, op1=ALU.add), wait=[sq_mm, pb.get("rfF")],
            inc="EV", hazard=True)
        pb[5] = t_rf0
        t_rf1 = P.emit("scalar", lambda sc: sc.activation(
            out=sbv("rfrF"), in_=sbv("rfrF"), func=AF.Ln), wait=[t_rf0], inc="AC", hazard=True)
        pb["rfF"] = t_rf1
        t_rf = P.emit("scalar", lambda sc: sc.activation(
            out=sbv("rfrR"), in_=sbv("rfrF"), func=AF.Exp, scale=-0.5),
            wait=[t_rf1, pb.get("rfR")], inc="AC", hazard=True)
        t_rb2 = P.emit("tensor", lambda t: t.matmul(
            psums[4][:, 256:512], sbv("ones1"), sbv("rfrR"),
            start=True, stop=True), wait=[t_rf, pb.get("4b")], inc="MM")
        pb["rfR"] = t_rb2
        t_x2_last = None
        for fc in range(G):
            slot = fc % 4
            t_x2 = P.emit("vector", lambda v, fc=fc, slot=slot: v.scalar_tensor_tensor(
                out=slc("x2s", slot * 256, slot * 256 + 256),
                in0=slc("xa", fc * 256, (fc + 1) * 256),
                scalar=slc("gffn_c", fc, fc + 1), in1=psums[4][:, 256:512],
                op0=ALU.mult, op1=ALU.mult),
                wait=[t_rb2, tok_x2.get((it, fc - 4), tok_x2.get((it - 1, fc + 4)))],
                inc="EV")
            tok_x2[(it, fc)] = P.emit("gpsimd", lambda gp, fc=fc, sl0=sl0, slot=slot: gp.dma_start(
                out=x2F_d[fc, :, sl0:sl0 + 256],
                in_=slc("x2s", slot * 256, slot * 256 + 256)),
                wait=[t_x2], inc="LG", inc_n=16)
            t_x2_last = t_x2
        pb["4b"] = t_x2_last
        pb[("dscr", 0)] = scan_toks[-2]
        pb[("dscr", 1)] = scan_toks[-1]
        pb[("dscr1", 0)] = scan_toks[-2]
        pb[("dscr1", 1)] = scan_toks[-1]
        xa_guard = [t_x2_last, t_sq]
        for s2 in range(2):
            tks = {}
            for g in range(G):
                bi = 6 + (g // 4)
                tks[g] = P.emit("tensor", lambda t, g=g, s2=s2, bi=bi: t.transpose(
                    psums[bi][:, (g % 4) * 128:(g % 4) * 128 + 128].bitcast(F32R),
                    slc("xa", g * 256 + s2 * 128, g * 256 + s2 * 128 + 128),
                    sbv("identr")),
                    wait=[pb.get(("xat", bi)), pb.get(("ya", bi))] if g % 4 == 0 else (),
                    inc="MM" if g % 4 == 3 else None)
            t_e1 = P.emit("vector", lambda v: v.tensor_copy(
                slc("xab", 0, 512), psums[6][:]), wait=[tks[3], pb.get("xabd")], inc="EV")
            t_e2 = P.emit("vector", lambda v: v.tensor_copy(
                slc("xab", 512, 1024), psums[7][:]), wait=[tks[7]], inc="EV")
            pb[("xat", 6)] = t_e1
            pb[("xat", 7)] = t_e2
            pb[("ya", 6)] = t_e1
            pb[("ya", 7)] = t_e2
            tl = it * 2 + s2
            tok_xaT[tl] = P.emit("gpsimd", lambda gp, tl=tl: gp.dma_start(
                out=xaT_d[tl], in_=sbv("xab")), wait=[t_e1, t_e2], inc="LG", inc_n=16)
            pb["xabd"] = tok_xaT[tl]
            xa_guard += [tks[3], tks[7]]
        prev_xt[key] = last_mmy
        prev_xt[key + "v"] = t_xa_prev
    drop("xt1A", "xt1B", "rsrowS", "rbc", "zs", "ds", "dscr", "hcur", "hinit", "hR",
         "sacts", "xa", "sq", "x2s", "rfrF", "rfrR", "xab",
         "wlnzT", "wdtT", "wyactT", "wyT", "rsioc")

    if upto == "ph1a":
        return _close(nc, P, semh)

    P.start_phase()
    sb("x2tA", 2048, BF16)
    sb("x2tB", 2048, BF16)
    sb("xaTtA", 2048)
    sb("xaTtB", 2048)
    sb("wfcbA", 4096, BF16)
    sb("wfcbB", 4096, BF16)
    sb("wfabA", 4096, BF16)
    sb("wfabB", 4096, BF16)
    sb("woutbA", 4096, BF16)
    sb("woutbB", 4096, BF16)
    sb("hid", 1024, BF16)
    sb("sact1b", 512)
    sb("sbnc", 2048)
    sb("boutrS", D, BF16, rows=1)
    tok_bo = P.emit("gpsimd", lambda gp: gp.dma_start(out=sbv("boutrS"), in_=boutr_d),
                    inc="LG", inc_n=16)

    prev_x2t = {}
    prev_w = {}
    prev_s = {}

    def emit_1b_loads(it):
        key = "A" if it % 2 == 0 else "B"
        x2t = "x2t" + key
        sl0 = it * TAU1
        tok_lx2 = P.emit("sync", lambda s, x2t=x2t, sl0=sl0: s.dma_start(
            out=sbv(x2t).rearrange("p (g c) -> p g c", g=G),
            in_=x2F_d[:, :, sl0:sl0 + 256].rearrange("g p c -> p g c")),
            wait=[prev_x2t.get(key)] + [tok_x2.get((it, g)) for g in range(G)],
            inc="LD", inc_n=16)
        lx2 = [tok_lx2] * G
        xak = "xaTt" + key
        tok_lxa = P.emit("sync", lambda s, it=it, xak=xak: s.dma_start(
            out=sbv(xak).rearrange("p (t c) -> p t c", t=2),
            in_=xaT_d[it * 2:(it + 1) * 2].rearrange("t p c -> p t c")),
            wait=[prev_s.get(("sd", key, 0)), prev_s.get(("sd", key, 1)),
                  tok_xaT.get(it * 2), tok_xaT.get(it * 2 + 1)], inc="LD", inc_n=16)
        lxa = [tok_lxa] * 2
        return lx2, lxa

    loads = {0: emit_1b_loads(0)}
    for it in range(N1):
        key = "A" if it % 2 == 0 else "B"
        x2t = "x2t" + key
        sl0 = it * TAU1
        ld_x2, ld_xa = loads.pop(it)
        if it + 1 < N1:
            loads[it + 1] = emit_1b_loads(it + 1)
        tok_offc = None
        hid_guard = {0: None, 1: None}
        for ffc in range(8):
            wk = "A" if ffc % 2 == 0 else "B"
            wfcb, wfab, woutb = "wfcb" + wk, "wfab" + wk, "woutb" + wk
            tok_wfc = P.emit("sync", lambda s, ffc=ffc, wfcb=wfcb: s.dma_start(
                out=sbv(wfcb).rearrange("p (g c) -> p g c", g=G),
                in_=wfcT_d[:, :, ffc * 512:(ffc + 1) * 512].rearrange("g p c -> p g c")),
                wait=[prev_w.get(wfcb)], inc="LD", inc_n=16)
            tok_wfa = P.emit("sync", lambda s, ffc=ffc, wfab=wfab: s.dma_start(
                out=sbv(wfab).rearrange("p (g c) -> p g c", g=G),
                in_=wfcaT_d[:, :, ffc * 512:(ffc + 1) * 512].rearrange("g p c -> p g c")),
                wait=[prev_w.get(wfab)], inc="LD", inc_n=16)
            ld_w = {}
            for g in range(G):
                ld_w[("fc", g)] = tok_wfc
                ld_w[("fa", g)] = tok_wfa
            tok_wo = P.emit("sync", lambda s, ffc=ffc, woutb=woutb: s.dma_start(
                out=sbv(woutb).rearrange("p (k c) -> p k c", k=4),
                in_=woutT_d[ffc * 4:(ffc + 1) * 4].rearrange("k p c -> p k c")),
                wait=[prev_w.get(woutb)], inc="LD", inc_n=16)
            ld_wo = {k4: tok_wo for k4 in range(4)}
            def emit_outs(fs, fpar, t_hid_tok):
                nonlocal tok_offc
                lo = None
                for tc in range(2):
                    for nh in range(2):
                        bo = 4 + tc * 2 + nh
                        is_first = (ffc == 0 and fs == 0)
                        is_tok = (tc == 1 and nh == 1)
                        lo = P.emit(
                            "tensor",
                            lambda t, tc=tc, nh=nh, bo=bo, fpar=fpar, woutb=woutb, k4=fs,
                            is_first=is_first: t.matmul(
                                psums[bo][:, 0:512],
                                slc("hid", fpar * 256 + tc * 128, fpar * 256 + tc * 128 + 128),
                                slc(woutb, k4 * 1024 + nh * 512, k4 * 1024 + nh * 512 + 512),
                                start=is_first, stop=False),
                            wait=[t_hid_tok, ld_wo[fs]]
                            + ([pb.get(("1b", bo))] if is_first else []),
                            inc="MM" if is_tok else None)
                hid_guard[fpar] = lo
                tok_offc = lo

            pend = None
            for fs in range(4):
                fpar = fs % 2
                bz = fpar
                ba = 2 + fpar
                mmf = None
                mma = None
                nrep1b = 2 if (it == 0 and ffc == 0 and fs < 2) else 1
                for _rep in range(nrep1b):
                    for g in range(G):
                        mmf = P.emit("tensor", lambda t, g=g, fs=fs, bz=bz, wfcb=wfcb, x2t=x2t: t.matmul(
                            psums[bz][:, 0:256],
                            slc(wfcb, g * 512 + fs * 128, g * 512 + fs * 128 + 128),
                            slc(x2t, g * 256, (g + 1) * 256),
                            start=(g == 0), stop=(g == G - 1)),
                            wait=[ld_w[("fc", g)], ld_x2[g]] + ([pb.get(("1b", bz))] if g == 0 else []),
                            inc="MM" if g == G - 1 else None)
                    for g in range(G):
                        mma = P.emit("tensor", lambda t, g=g, fs=fs, ba=ba, wfab=wfab, x2t=x2t: t.matmul(
                            psums[ba][:, 0:256],
                            slc(wfab, g * 512 + fs * 128, g * 512 + fs * 128 + 128),
                            slc(x2t, g * 256, (g + 1) * 256),
                            start=(g == 0), stop=(g == G - 1)),
                            wait=[ld_w[("fa", g)]] + ([pb.get(("1b", ba))] if g == 0 else []),
                            inc="MM" if g == G - 1 else None)
                if pend is not None:
                    emit_outs(*pend)
                fcol = ffc * 4 + fs
                t_sl = P.emit("scalar", lambda sc, fcol=fcol, ba=ba, fpar=fpar: sc.activation(
                    out=slc("sact1b", fpar * 256, fpar * 256 + 256),
                    in_=psums[ba][:, 0:256], func=AF.Silu, bias=slc("bfca_c", fcol, fcol + 1)),
                    wait=[mma, pb.get(("sl", fpar))], inc="AC")
                pb[("1b", ba)] = t_sl
                t_hid = P.emit("vector", lambda v, fcol=fcol, bz=bz, fpar=fpar: v.scalar_tensor_tensor(
                    out=slc("hid", fpar * 256, fpar * 256 + 256),
                    in0=psums[bz][:, 0:256], scalar=slc("bfc_c", fcol, fcol + 1),
                    in1=slc("sact1b", fpar * 256, fpar * 256 + 256),
                    op0=ALU.add, op1=ALU.mult),
                    wait=[mmf, t_sl, hid_guard[fpar]], inc="EV")
                pb[("1b", bz)] = t_hid
                pb[("sl", fpar)] = t_hid
                pend = (fs, fpar, t_hid)
            emit_outs(*pend)
            prev_w[wfcb] = tok_offc
            prev_w[wfab] = tok_offc
            prev_w[woutb] = tok_offc
        for tc in range(2):
            for nh in range(2):
                bo = 4 + tc * 2 + nh
                t_b = P.emit("tensor", lambda t, nh=nh, bo=bo: t.matmul(
                    psums[bo][:, 0:512], sbv("ones1B"),
                    slc("boutrS", nh * 512, nh * 512 + 512),
                    start=False, stop=True), wait=[tok_bo], inc="MM")
                t_s = P.emit("vector", lambda v, tc=tc, nh=nh, bo=bo, key=key: v.tensor_tensor(
                    out=slc("sbnc", tc * 1024 + nh * 512, tc * 1024 + nh * 512 + 512),
                    in0=psums[bo][:, 0:512],
                    in1=slc("xaTt" + key, tc * 1024 + nh * 512, tc * 1024 + nh * 512 + 512),
                    op=ALU.add), wait=[t_b, ld_xa[tc]], inc="EV")
                pb[("1b", bo)] = t_s
        for tc in range(2):
            tl = it * 2 + tc
            t_sd = P.emit("gpsimd", lambda gp, tc=tc, tl=tl: gp.dma_start(
                out=sT_d[tl], in_=slc("sbnc", tc * 1024, (tc + 1) * 1024)),
                wait=[pb[("1b", 4 + tc * 2)], pb[("1b", 4 + tc * 2 + 1)]], inc="LG", inc_n=16)
            prev_s[("sd", key, tc)] = t_sd
            prev_s[("sT", tl)] = t_sd
        prev_x2t[key] = tok_offc
    drop("x2tA", "x2tB", "xaTtA", "xaTtB", "wfcbA", "wfcbB", "wfabA", "wfabB",
         "woutbA", "woutbB", "hid", "sact1b", "sbnc", "boutrS")

    if upto == "ph1b":
        return _close(nc, P, semh)

    P.start_phase()
    sb("xnbtA", 4096, BF16)
    sb("xnbtB", 4096, BF16)
    sb("expbA", 8192, BF16)
    sb("expbB", 8192, BF16)
    sb("sEA", 4096)
    sb("sEB", 4096)
    sb("xEA", 4096)
    sb("xEB", 4096)
    sb("yacc", 4096)
    sb("outb", 4096)
    prev2 = {}

    def emit_2_loads(it):
        key = "A" if it % 2 == 0 else "B"
        xnbt = "xnbt" + key
        sl0 = it * 512
        tok_lb = P.emit("sync", lambda s, xnbt=xnbt, sl0=sl0: s.dma_start(
            out=sbv(xnbt).rearrange("p (g c) -> p g c", g=G),
            in_=xnB_d[:, :, sl0:sl0 + 512].rearrange("g p c -> p g c")),
            wait=[prev2.get(xnbt)] + [tok_xnF[(sl0 // 512, g)][1] for g in range(G)],
            inc="LD", inc_n=16)
        lb = [tok_lb] * G
        tok_ls = P.emit("sync", lambda s, it=it, key=key: s.dma_start(
            out=sbv("sE" + key).rearrange("p (t c) -> p t c", t=4),
            in_=sT_d[it * 4:(it + 1) * 4].rearrange("t p c -> p t c")),
            wait=[prev2.get(("sE", key, tc)) for tc in range(4)]
            + [prev_s.get(("sT", it * 4 + tc)) for tc in range(4)], inc="LD", inc_n=16)
        ls_ = [tok_ls] * 4
        tok_lx = P.emit("sync", lambda s, it=it, key=key: s.dma_start(
            out=sbv("xE" + key).rearrange("p (t c) -> p t c", t=4),
            in_=x_d[it * 512:(it + 1) * 512, :].rearrange("(t p) c -> p t c", p=128)),
            wait=[prev2.get(("xE", key, tc)) for tc in range(4)], inc="LD", inc_n=16)
        lx_ = [tok_lx] * 4
        return lb, ls_, lx_

    loads2 = {0: emit_2_loads(0)}
    for it in range(N2):
        key = "A" if it % 2 == 0 else "B"
        xnbt = "xnbt" + key
        sl0 = it * 512
        ldb, ld_s, ld_x = loads2.pop(it)
        if it + 1 < N2:
            loads2[it + 1] = emit_2_loads(it + 1)
        acc_last = {}
        last_mm_it = None
        for e in range(E):
            ek = "A" if e % 2 == 0 else "B"
            en = "expb" + ek
            w = [prev2.get((en, "free"))]
            if it == 0 and e < 2:
                w += tok_expB
            tok_le = P.emit("sync", lambda s, e=e, en=en: s.dma_start(
                out=sbv(en).rearrange("p (g c) -> p g c", g=G),
                in_=expB_d[e].rearrange("g p c -> p g c")),
                wait=w, inc="LD", inc_n=16)
            ld_e = {g: tok_le for g in range(G)}
            if it == 0 and e == 0 and "expbdbg" in dbg_d:
                P.emit("gpsimd", lambda gp, en=en: gp.dma_start(
                    out=dbg_d["expbdbg"], in_=sbv(en)),
                    wait=[ld_e[g2] for g2 in range(G)], inc="LG", inc_n=16)
            for tc in range(4):
                for oh in range(2):
                    bo = (e % 2) * 2 + oh
                    mm = None
                    nrep = 2 if (it == 0 and e == 0) else 1
                    for rep in range(nrep):
                        for g in range(G):
                            mm = P.emit("tensor", lambda t, g=g, e=e, oh=oh, tc=tc, bo=bo,
                                        xnbt=xnbt, en=en: t.matmul(
                                psums[bo][:, 0:512],
                                slc(xnbt, g * 512 + tc * 128, g * 512 + tc * 128 + 128),
                                slc(en, g * 1024 + oh * 512, g * 1024 + oh * 512 + 512),
                                start=(g == 0), stop=(g == G - 1)),
                                wait=[ldb[g], ld_e[g]] + ([pb.get(("2", bo))] if g == 0 else []),
                                inc="MM" if g == G - 1 else None)
                    tl = it * 4 + tc

                    def mk_ev(e=e, oh=oh, tc=tc, bo=bo, tl=tl, mm=mm):
                        def fn(v):
                            wcol = sbv("win")[:, e * 32 + tl:e * 32 + tl + 1]
                            dst = slc("yacc", tc * 1024 + oh * 512, tc * 1024 + oh * 512 + 512)
                            if e == 0:
                                return v.tensor_scalar(out=dst, in0=psums[bo][:, 0:512],
                                                       scalar1=wcol, scalar2=None, op0=ALU.mult)
                            return v.scalar_tensor_tensor(out=dst, in0=psums[bo][:, 0:512],
                                                          scalar=wcol, in1=dst,
                                                          op0=ALU.mult, op1=ALU.add)
                        return fn
                    acc_last[(tc, oh)] = P.emit("vector", mk_ev(),
                                                wait=[mm, t_win, acc_last.get((tc, oh))],
                                                inc="EV")
                    pb[("2", bo)] = acc_last[(tc, oh)]
                    last_mm_it = mm
            prev2[(en, "free")] = last_mm_it
        if it == 0 and "yaccdbg" in dbg_d:
            P.emit("gpsimd", lambda gp: gp.dma_start(
                out=dbg_d["yaccdbg"], in_=slc("yacc", 0, 1024)),
                wait=[acc_last[(0, 0)], acc_last[(0, 1)]], inc="LG", inc_n=16)
        for tc in range(4):
            tl = it * 4 + tc
            t_g = P.emit("scalar", lambda sc, tc=tc: sc.activation(
                out=slc("yacc", tc * 1024, (tc + 1) * 1024),
                in_=slc("yacc", tc * 1024, (tc + 1) * 1024), func=AF.Silu),
                wait=[acc_last[(tc, 0)], acc_last[(tc, 1)]], inc="AC", hazard=True)
            t_sg = P.emit("vector", lambda v, tc=tc, key=key: v.tensor_tensor(
                out=slc("outb", tc * 1024, (tc + 1) * 1024),
                in0=slc("yacc", tc * 1024, (tc + 1) * 1024),
                in1=slc("sE" + key, tc * 1024, (tc + 1) * 1024), op=ALU.mult),
                wait=[t_g, ld_s[tc], prev2.get(("outd", tc))], inc="EV")
            t_out = P.emit("vector", lambda v, tc=tc, key=key: v.tensor_tensor(
                out=slc("outb", tc * 1024, (tc + 1) * 1024),
                in0=slc("outb", tc * 1024, (tc + 1) * 1024),
                in1=slc("xE" + key, tc * 1024, (tc + 1) * 1024), op=ALU.add),
                wait=[t_sg, ld_x[tc]], inc="EV")
            t_od = P.emit("gpsimd", lambda gp, tc=tc, tl=tl: gp.dma_start(
                out=out_d[tl * 128:(tl + 1) * 128, :],
                in_=slc("outb", tc * 1024, (tc + 1) * 1024)),
                wait=[t_out], inc="OD", inc_n=16)
            prev2[("sE", key, tc)] = t_sg
            prev2[("xE", key, tc)] = t_out
            prev2[("outd", tc)] = t_od
        prev2[xnbt] = last_mm_it

    return _close(nc, P, semh)


def _close(nc, P, semh):
    waits = tuple((s, v) for s, v in P.counts.items() if v > 0
                  and s in ("LD", "LG", "LS", "OD"))
    P.items.append(("gpsimd", lambda gp: gp.nop(), waits, None))
    with nc.Block() as block:
        P.replay(block, semh)
    return nc


def make_in_maps(inputs):
    x = np.ascontiguousarray(inputs["L0_attn_x"], dtype=np.float32)
    ins = {
        "wlnz": inputs["L0_attn_w_ln_z"], "wdt": inputs["L0_attn_w_dt"],
        "wy": inputs["L0_attn_w_y"], "wyact": inputs["L0_attn_w_yact"],
        "wfc": inputs["L0_attn_w_fc"], "wfca": inputs["L0_attn_w_fcact"],
        "wout": inputs["L0_attn_w_out"], "exps": inputs["L0_ffn_experts"],
        "gw": inputs["L0_ffn_gate_w"][0],
        "blnz": inputs["L0_attn_b_ln_z"], "bdt": inputs["L0_attn_b_dt"],
        "by": inputs["L0_attn_b_y"], "byact": inputs["L0_attn_b_yact"],
        "bfc": inputs["L0_attn_b_fc"], "bfca": inputs["L0_attn_b_fcact"],
        "bout": inputs["L0_attn_b_out"],
        "gsio": inputs["L0_attn_g_sio"], "gffn": inputs["L0_attn_g_ffn"],
        "h0": inputs["L0_attn_h0"],
    }
    ins = {k: np.ascontiguousarray(v, dtype=np.float32) for k, v in ins.items()}
    ins["boutr"] = ins["bout"].reshape(1, D).copy()
    ins["ident"] = np.eye(128, dtype=np.float32)
    ins["identr"] = np.eye(128, dtype=np.float32)
    ins["ones1"] = np.ones((1, 128), dtype=np.float32)
    ins["onesc"] = np.ones((128, 1), dtype=np.float32)
    return [dict(ins, x=x[i]) for i in range(NCORE)]


def kernel(**inputs):
    nc = build_program()
    in_maps = make_in_maps(inputs)
    res = run_bass_kernel_spmd(nc, in_maps, core_ids=list(range(NCORE)))
    out = np.stack([np.asarray(res.results[i]["out"]) for i in range(NCORE)], axis=0)
    return out.astype(np.float32)


# version 13
# speedup vs baseline: 1.1769x; 1.0356x over previous
import numpy as np
import concourse.bass as bass
import concourse.mybir as mybir
from concourse.bass_utils import run_bass_kernel_spmd

F32 = mybir.dt.float32
F32R = mybir.dt.float32r
BF16 = mybir.dt.bfloat16
ALU = mybir.AluOpType
AF = mybir.ActivationFunctionType
AX = mybir.AxisListType

T = 4096
D = 1024
FF = 4096
E = 8
G = 8
NCORE = 8
TAU1 = 256
N0 = T // 512
N1 = T // TAU1
N2 = T // 512
NTILE = T // 128

ENGINES = ("sync", "vector", "scalar", "tensor", "gpsimd")
SEMS = ["LD", "LG", "LS", "MM", "EV", "AC", "PL", "OD"]


class Prog:
    def __init__(self):
        self.counts = {s: 0 for s in SEMS}
        self.items = []
        self.tok_info = {}
        self.phase_bar = ()
        self.phase_pending = set()

    def start_phase(self):
        self.phase_bar = tuple((s, v) for s, v in self.counts.items() if v > 0)
        self.phase_pending = set(ENGINES)

    def emit(self, engine, fn, wait=(), inc=None, inc_n=1, hazard=False):
        wl = [w for w in wait if w is not None]
        if engine in self.phase_pending:
            self.phase_pending.discard(engine)
            wl = list(self.phase_bar) + wl
        tok = None
        if inc is not None:
            self.counts[inc] += inc_n
            tok = (inc, self.counts[inc])
            self.tok_info[tok] = (engine, hazard)
        self.items.append((engine, fn, tuple(wl), (inc, inc_n) if inc else None))
        return tok

    def replay(self, block, sems):
        for eng_name in ENGINES:
            items = [(fn, waits, inc) for (e, fn, waits, inc) in self.items if e == eng_name]
            if not items:
                continue

            def body(eng, _items=items, _eng=eng_name):
                hi = {}
                for fn, waits, inc in _items:
                    for (s, v) in waits:
                        if hi.get(s, -1) >= v:
                            continue
                        eng.wait_ge(sems[s], v)
                        hi[s] = v
                    instr = fn(eng)
                    if inc is not None:
                        instr.then_inc(sems[inc[0]], inc[1])
            getattr(block, eng_name)(body)


def build_program(taps=(), upto="all"):
    nc = bass.Bass("TRN2", target_bir_lowering=False)
    P = Prog()

    def kind(name):
        return "ExternalOutput" if name in taps else "Internal"

    x_d = nc.dram_tensor("x", [T, D], F32, kind="ExternalInput").ap()
    out_d = nc.dram_tensor("out", [T, D], F32, kind="ExternalOutput").ap()
    w_d = {}
    for nm, shape in [("wlnz", [D, D]), ("wdt", [D, D]), ("wy", [D, D]),
                      ("wyact", [D, D]), ("wfc", [FF, D]), ("wfca", [FF, D]),
                      ("wout", [D, FF])]:
        w_d[nm] = nc.dram_tensor(nm, shape, F32, kind="ExternalInput").ap()
    exps_d = nc.dram_tensor("exps", [E, D, D], F32R, kind="ExternalInput").ap()
    gw_d = nc.dram_tensor("gw", [D, E], F32R, kind="ExternalInput").ap()
    vec_d = {}
    for nm, n in [("blnz", D), ("bdt", D), ("by", D), ("byact", D), ("bfc", FF),
                  ("bfca", FF), ("bout", D), ("gsio", D), ("gffn", D), ("h0", D)]:
        vec_d[nm] = nc.dram_tensor(nm, [n], F32, kind="ExternalInput").ap()
    boutr_d = nc.dram_tensor("boutr", [1, D], F32R, kind="ExternalInput").ap()
    ident_d = nc.dram_tensor("ident", [128, 128], F32, kind="ExternalInput").ap()
    identr_d = nc.dram_tensor("identr", [128, 128], F32R, kind="ExternalInput").ap()
    ones1_d = nc.dram_tensor("ones1", [1, 128], F32R, kind="ExternalInput").ap()
    onesc_d = nc.dram_tensor("onesc", [128, 1], F32R, kind="ExternalInput").ap()

    xnF_d = nc.dram_tensor("xnF", [G, 128, T], F32R, kind=kind("xnF")).ap()
    xnB_d = nc.dram_tensor("xnB", [G, 128, T], BF16, kind=kind("xnB")).ap()
    x2F_d = nc.dram_tensor("x2F", [G, 128, T], BF16, kind=kind("x2F")).ap()
    xaT_d = nc.dram_tensor("xaT", [NTILE, 128, D], F32, kind=kind("xaT")).ap()
    sT_d = nc.dram_tensor("sT", [NTILE, 128, D], F32, kind=kind("sT")).ap()
    wfcT_d = nc.dram_tensor("wfcT", [G, 128, FF], BF16, kind=kind("wfcT")).ap()
    wfcaT_d = nc.dram_tensor("wfcaT", [G, 128, FF], BF16, kind=kind("wfcaT")).ap()
    woutT_d = nc.dram_tensor("woutT", [FF // 128, 128, D], BF16, kind=kind("woutT")).ap()
    expB_d = nc.dram_tensor("expB", [E, G, 128, D], BF16, kind=kind("expB")).ap()
    dbg_d = {}
    for nm, shape, dt_ in [("gdbg", [8, T], F32), ("windbg", [128, E, NTILE], F32),
                           ("hdbg", [G, 128, T], F32), ("seldbg", [128, E, NTILE], F32),
                           ("taudbg", [128, 8], F32), ("gtokdbg", [128, 256], F32),
                           ("cntdbg", [128, 16], F32), ("yaccdbg", [128, 1024], F32),
                           ("expbdbg", [128, 8192], BF16)]:
        if nm in taps:
            dbg_d[nm] = nc.dram_tensor(nm, shape, dt_, kind="ExternalOutput").ap()

    psums = [nc.psum_tensor(f"PS{i}", [128, 512], F32).__enter__() for i in range(8)]
    semh = {s: nc.semaphore(s).__enter__() for s in SEMS}

    tens = {}
    dead = {}
    seq = [0]

    def sb(name, cols, dtype=F32, rows=128):
        cm = nc.sbuf_tensor("sb_" + name, [rows, cols], dtype)
        h = cm.__enter__()
        seq[0] += 1
        tens[name] = (h, cm, dtype, rows, seq[0])

    def drop(*names):
        for name in sorted(names, key=lambda n: -tens[n][4]):
            rec = tens.pop(name)
            dead[name] = rec
            rec[1].__exit__(None, None, None)

    def sbv(name, rows=None, row0=0):
        h, _, dt_, r, _s = tens[name] if name in tens else dead[name]
        if rows is None and row0 == 0:
            return h[:]
        return h[row0:(row0 + rows) if rows is not None else r, :]

    def slc(name, c0, c1, rows=None, row0=0):
        h, _, dt_, r, _s = tens[name] if name in tens else dead[name]
        r1 = (row0 + rows) if rows is not None else r
        return h[row0:r1, c0:c1]

    sb("ident", 128)
    sb("identr", 128, F32R)
    sb("ones1", 128, F32R, rows=1)
    sb("onesc", 1, F32R)
    sb("ones1B", 128, BF16, rows=1)
    sb("gwsb", 64, F32R)
    colspec = {"blnz_c": ("blnz", 8), "bdt_c": ("bdt", 8), "nbdt_c": (None, 8),
               "by_c": ("by", 8), "byact_c": ("byact", 8), "bfc_c": ("bfc", 32),
               "bfca_c": ("bfca", 32), "gsio_c": ("gsio", 8), "gffn_c": ("gffn", 8),
               "h0_c": ("h0", 8)}
    for nm, (_, k) in colspec.items():
        sb(nm, k)
    sb("win", 256)

    P.emit("sync", lambda s: s.dma_start(out=sbv("ident"), in_=ident_d), inc="LD", inc_n=16)
    P.emit("sync", lambda s: s.dma_start(out=sbv("identr"), in_=identr_d), inc="LD", inc_n=16)
    P.emit("sync", lambda s: s.dma_start(out=sbv("ones1"), in_=ones1_d), inc="LD", inc_n=16)
    P.emit("gpsimd", lambda gp: gp.dma_start(out=sbv("ones1B"), in_=ones1_d),
           wait=[("LD", 48)], inc="LG", inc_n=16)
    P.emit("sync", lambda s: s.dma_start(out=sbv("onesc"), in_=onesc_d), inc="LD", inc_n=16)
    for g in range(G):
        P.emit("sync", lambda s, g=g: s.dma_start(
            out=slc("gwsb", g * 8, g * 8 + 8), in_=gw_d[g * 128:(g + 1) * 128, :]),
            inc="LD", inc_n=16)
    for nm, (src, k) in colspec.items():
        if src is None:
            continue
        for j in range(k):
            P.emit("sync", lambda s, nm=nm, src=src, j=j: s.dma_start(
                out=slc(nm, j, j + 1),
                in_=vec_d[src][j * 128:(j + 1) * 128].rearrange("(p o) -> p o", o=1)),
                inc="LD", inc_n=16)
    tok_consts = ("LD", P.counts["LD"])
    tok_nbdt = P.emit("vector", lambda v: v.tensor_scalar(
        out=sbv("nbdt_c"), in0=sbv("bdt_c"), scalar1=-1.0, scalar2=None, op0=ALU.mult),
        wait=[tok_consts], inc="EV")

    def phW_gen():
        bank_free = {}
        stg_free = {0: None, 1: None}
        for src, dstT in (("wfc", wfcT_d), ("wfca", wfcaT_d)):
            for fb in range(FF // 128):
                par = fb % 2
                tok_stg = P.emit("sync", lambda s, src=src, fb=fb, par=par: s.dma_start(
                    out=slc("wstage", par * 1024, par * 1024 + 1024),
                    in_=w_d[src][fb * 128:(fb + 1) * 128, :]),
                    wait=[stg_free[par], tok_consts], inc="LD", inc_n=16)
                last_tp = None
                for half in range(2):
                    bi = 2 + half
                    bn = "wbnc0" if half == 0 else "wbnc1"
                    tk = None
                    for gh in range(4):
                        g = half * 4 + gh
                        tk = P.emit("tensor", lambda t, g=g, gh=gh, bi=bi, par=par: t.transpose(
                            psums[bi][:, gh * 128:(gh + 1) * 128],
                            slc("wstage", par * 1024 + g * 128, par * 1024 + g * 128 + 128),
                            sbv("ident")),
                            wait=[bank_free.get(bi), tok_stg] if gh == 0 else (),
                            inc="MM" if gh == 3 else None)
                    tok_ev = P.emit("vector", lambda v, bi=bi, bn=bn: v.tensor_copy(
                        sbv(bn), psums[bi][:]), wait=[tk, bank_free.get((bn,))], inc="EV")
                    bank_free[bi] = tok_ev
                    dmt = None
                    for gh in range(4):
                        g = half * 4 + gh
                        dmt = P.emit("gpsimd", lambda gp, g=g, gh=gh, fb=fb, bn=bn, dstT=dstT:
                                     gp.dma_start(out=dstT[g, :, fb * 128:(fb + 1) * 128],
                                                  in_=slc(bn, gh * 128, gh * 128 + 128)),
                                     wait=[tok_ev], inc="LG", inc_n=16)
                    bank_free[(bn,)] = dmt
                    last_tp = tk
                stg_free[par] = last_tp
                yield
        for db in range(D // 128):
            par = db % 2
            tok_stg = P.emit("sync", lambda s, db=db, par=par: s.dma_start(
                out=slc("wstage", par * 4096, par * 4096 + 4096),
                in_=w_d["wout"][db * 128:(db + 1) * 128, :]),
                wait=[stg_free.get(("wo", par))] if ("wo", par) in stg_free
                else [stg_free[0], stg_free[1]], inc="LD", inc_n=16)
            last_tp = None
            for ffc in range(FF // 128):
                half = ffc % 2
                bi = 2 + half
                bn = "wbnc0" if half == 0 else "wbnc1"
                tok_tp = P.emit("tensor", lambda t, db=db, ffc=ffc, bi=bi, par=par: t.transpose(
                    psums[bi][:, 0:128],
                    slc("wstage", par * 4096 + ffc * 128, par * 4096 + ffc * 128 + 128),
                    sbv("ident")),
                    wait=[bank_free.get(bi), tok_stg], inc="MM")
                tok_ev = P.emit("vector", lambda v, bi=bi, bn=bn: v.tensor_copy(
                    slc(bn, 0, 128), psums[bi][:, 0:128]),
                    wait=[tok_tp, bank_free.get((bn,))], inc="EV")
                bank_free[bi] = tok_ev
                dmt = P.emit("gpsimd", lambda gp, ffc=ffc, db=db, bn=bn: gp.dma_start(
                    out=woutT_d[ffc, :, db * 128:(db + 1) * 128], in_=slc(bn, 0, 128)),
                    wait=[tok_ev], inc="LG", inc_n=16)
                bank_free[(bn,)] = dmt
                last_tp = tok_tp
            stg_free[("wo", par)] = last_tp
            yield
        while True:
            yield

    P.start_phase()
    sb("rsioc", 32, F32R)
    sb("gatesT", T, rows=8)
    sb("ewT", T, rows=8)
    sb("rowA", T, rows=1)
    sb("rowB", T, rows=1)
    sb("xtokA", 1024)
    sb("xtokB", 1024)
    sb("bnscrA", 16)
    sb("bnscrB", 16)
    sb("mvA", 2)
    sb("mvB", 2)
    sb("lncolA", 8)
    sb("lncolB", 8)
    sb("xn_sA", 1024)
    sb("xn_sB", 1024)
    sb("xnball", 4096, F32R)
    sb("expstgA", 1024, F32R)
    sb("expstgB", 1024, F32R)
    sb("g_tok", 256)
    sb("skt", 256)
    sb("sks", 128)
    sb("pm8", 128, rows=8)
    sb("mx8", 1, rows=8)
    sb("mxrowR", 8, F32R, rows=1)
    sb("lseR", 8, F32R, rows=1)
    sb("prowR", 8, F32R)
    sb("selm", 256)
    sb("hardm", 256)
    sb("bis", 48)
    sb("bmsk", 8, mybir.dt.int32)
    sb("cntR", 8, F32R)
    sb("crR", 8, F32R, rows=1)
    sb("wstage", 8192)
    sb("wbnc0", 512, F32R)
    sb("wbnc1", 512, F32R)

    def gates_ap(r0=0, r1=8, c0=0, c1=T):
        return slc("gatesT", c0, c1, rows=r1 - r0, row0=r0)

    def ew_ap(r0=0, r1=8, c0=0, c1=T):
        return slc("ewT", c0, c1, rows=r1 - r0, row0=r0)

    exp_chunks = [(e, g) for e in range(E) for g in range(G)]
    exp_state = {"i": 0, "A": None, "B": None}

    def emit_expert_casts(n):
        for _ in range(n):
            i = exp_state["i"]
            if i >= len(exp_chunks):
                return
            e, g = exp_chunks[i]
            key = "A" if i % 2 == 0 else "B"
            stgn = "expstg" + key
            tok_l = P.emit("sync", lambda s, e=e, g=g, stgn=stgn: s.dma_start(
                out=sbv(stgn), in_=exps_d[e, g * 128:(g + 1) * 128, :]),
                wait=[exp_state[key]], inc="LD", inc_n=16)
            tok_c = P.emit("gpsimd", lambda gp, e=e, g=g, stgn=stgn: gp.dma_start(
                out=expB_d[e, g], in_=sbv(stgn)),
                wait=[tok_l], inc="LG", inc_n=16)
            exp_state[key] = tok_c
            exp_state["i"] = i + 1

    xuse = {"A": None, "B": None}
    bfree = {}
    tok_gates_fin = None
    tok_xnF = {}
    wgen = phW_gen()
    for ti in range(N0):
        emit_expert_casts(8)
        evtoks = {}
        for s4 in range(4):
            sub = ti * 4 + s4
            key = "A" if s4 % 2 == 0 else "B"
            xt = "xtok" + key
            bn, mv, lc, xs = "bnscr" + key, "mv" + key, "lncol" + key, "xn_s" + key
            b0, b1 = (0, 1) if s4 % 2 == 0 else (6, 7)
            next(wgen)
            tok_x = P.emit("sync", lambda s, sub=sub, xt=xt: s.dma_start(
                out=sbv(xt), in_=x_d[sub * 128:(sub + 1) * 128, :]),
                wait=[xuse[key]], inc="LD", inc_n=16)
            tb1 = P.emit("vector", lambda v, xt=xt, bn=bn: v.bn_stats(
                out=slc(bn, 0, 6), in_=slc(xt, 0, 512)),
                wait=[tok_x, bfree.get("st" + key)], inc="EV", hazard=True)
            tb2 = P.emit("vector", lambda v, xt=xt, bn=bn: v.bn_stats(
                out=slc(bn, 6, 12), in_=slc(xt, 512, 1024)),
                wait=[tok_x], inc="EV", hazard=True)
            tb3 = P.emit("vector", lambda v, bn=bn, mv=mv: v.bn_aggr(
                out=sbv(mv), in_=slc(bn, 0, 12)), wait=[tb1, tb2], inc="EV", hazard=True)
            t1 = P.emit("vector", lambda v, mv=mv, lc=lc: v.tensor_scalar(
                out=slc(lc, 0, 1), in0=slc(mv, 1, 2), scalar1=1e-5, scalar2=None,
                op0=ALU.add), wait=[tb3], inc="EV", hazard=True)
            t2 = P.emit("scalar", lambda sc, lc=lc: sc.activation(
                out=slc(lc, 1, 2), in_=slc(lc, 0, 1), func=AF.Ln),
                wait=[t1], inc="AC", hazard=True)
            t3 = P.emit("scalar", lambda sc, lc=lc: sc.activation(
                out=slc(lc, 2, 3), in_=slc(lc, 1, 2), func=AF.Exp, scale=-0.5),
                wait=[t2], inc="AC", hazard=True)
            t_xn = P.emit("vector", lambda v, xt=xt, mv=mv, lc=lc, xs=xs: v.tensor_scalar(
                out=sbv(xs), in0=sbv(xt), scalar1=slc(mv, 0, 1),
                scalar2=slc(lc, 2, 3), op0=ALU.subtract, op1=ALU.mult),
                wait=[t3, bfree.get("xn_s" + key)], inc="EV")
            xuse[key] = t_xn
            t4 = P.emit("vector", lambda v, mv=mv, lc=lc: v.tensor_tensor(
                out=slc(lc, 3, 4), in0=slc(mv, 1, 2), in1=slc(lc, 2, 3),
                op=ALU.mult), wait=[t3], inc="EV", hazard=True)
            t5 = P.emit("vector", lambda v, lc=lc: v.tensor_scalar(
                out=slc(lc, 4, 5), in0=slc(lc, 3, 4), scalar1=slc(lc, 2, 3),
                scalar2=1e-6, op0=ALU.mult, op1=ALU.add), wait=[t4], inc="EV", hazard=True)
            t6 = P.emit("scalar", lambda sc, lc=lc: sc.activation(
                out=slc(lc, 5, 6), in_=slc(lc, 4, 5), func=AF.Ln),
                wait=[t5], inc="AC", hazard=True)
            P.emit("scalar", lambda sc, sub=sub, lc=lc: sc.activation(
                out=slc("rsioc", sub, sub + 1), in_=slc(lc, 5, 6), func=AF.Exp,
                scale=-0.5), wait=[t6], inc="AC", hazard=True)
            bfree["st" + key] = t6
            tkb = {}
            for g in range(G):
                bi = b0 if g < 4 else b1
                tkb[bi] = P.emit("tensor", lambda t, g=g, xs=xs, bi=bi: t.transpose(
                    psums[bi][:, (g % 4) * 128:(g % 4) * 128 + 128],
                    slc(xs, g * 128, g * 128 + 128), sbv("ident")),
                    wait=[t_xn] + ([bfree.get(("p0", bi))] if g % 4 == 0 else []),
                    inc="MM" if g % 4 == 3 else None)
            for half, bi in ((0, b0), (1, b1)):
                tok_ev = P.emit("vector", lambda v, half=half, bi=bi, s4=s4: v.tensor_copy(
                    sbv("xnball").rearrange("p (g q) -> p g q", g=8)[:, half * 4:half * 4 + 4,
                                                                    s4 * 128:(s4 + 1) * 128],
                    psums[bi][:].rearrange("p (g q) -> p g q", g=4)),
                    wait=[tkb[bi], bfree.get(("xne", s4, half)), bfree.get("xnd")], inc="EV")
                evtoks[(s4, half)] = tok_ev
                bfree[("p0", bi)] = tok_ev
            bfree["xn_s" + key] = tkb[b1]
        ev_all = [evtoks[(s4, h)] for s4 in range(4) for h in range(2)]
        for g in range(G):
            d1 = P.emit("gpsimd", lambda gp, g=g, ti=ti: gp.dma_start(
                out=xnF_d[g, :, ti * 512:(ti + 1) * 512],
                in_=slc("xnball", g * 512, (g + 1) * 512)),
                wait=ev_all if g == 0 else (), inc="LG", inc_n=16)
            d2 = P.emit("gpsimd", lambda gp, g=g, ti=ti: gp.dma_start(
                out=xnB_d[g, :, ti * 512:(ti + 1) * 512],
                in_=slc("xnball", g * 512, (g + 1) * 512)),
                inc="LG", inc_n=16)
            tok_xnF[(ti, g)] = (d1, d2)
            bfree["xnd"] = d2
        tok_mm = None
        for g in range(G):
            tok_mm = P.emit("tensor", lambda t, g=g: t.matmul(
                psums[0][0:8, :], slc("gwsb", g * 8, g * 8 + 8),
                slc("xnball", g * 512, (g + 1) * 512),
                start=(g == 0), stop=(g == G - 1)),
                wait=ev_all + [tok_consts] if g == 0 else (),
                inc="MM" if g == G - 1 else None)
        tok_gev = P.emit("vector", lambda v, ti=ti: v.tensor_copy(
            gates_ap(0, 8, ti * 512, (ti + 1) * 512), psums[0][0:8, :]),
            wait=[tok_mm], inc="EV")
        bfree[("p0", 0)] = tok_gev
        for s4 in range(4):
            for h in range(2):
                bfree[("xne", s4, h)] = tok_mm
        tok_gates_fin = tok_gev
    for _ in range(80):
        next(wgen)
    drop("wbnc1", "wbnc0", "wstage")
    emit_expert_casts(len(exp_chunks))
    tok_expB = [exp_state["A"], exp_state["B"]]

    if upto == "ph0":
        return _close(nc, P, semh)

    bfree[("gt", 6)] = bfree.get(("p0", 6))
    bfree[("gt", 7)] = bfree.get(("p0", 7))

    def gq(name):
        return sbv(name).rearrange("p (e q) -> p e q", e=8)

    def gqT(name):
        return sbv(name).rearrange("p (e q) -> p q e", e=8)

    t_cl = P.emit("vector", lambda v: v.tensor_scalar(
        out=gates_ap(), in0=gates_ap(), scalar1=1e-6, scalar2=None, op0=ALU.max),
        wait=[tok_gates_fin], inc="EV", hazard=True)
    tok_g0 = P.emit("scalar", lambda sc: sc.activation(
        out=gates_ap(), in_=gates_ap(), func=AF.Ln), wait=[t_cl], inc="AC", hazard=True)
    tts = None
    tok_gtok = {}
    for t8 in range(NTILE):
        bi = 6 + (t8 // 16) % 2
        tts = P.emit("tensor", lambda t, t8=t8, bi=bi: t.transpose(
            psums[bi][:, (t8 % 16) * 8:(t8 % 16) * 8 + 8],
            gates_ap(0, 8, t8 * 128, (t8 + 1) * 128), sbv("ident")[0:8, 0:8]),
            wait=[tok_g0] + ([bfree.get(("gt", bi))] if t8 % 16 == 0 else []),
            inc="MM" if t8 % 16 == 15 else None)
        if t8 % 16 == 15:
            first = t8 - 15
            tok_ge = P.emit("vector", lambda v, bi=bi, first=first: v.tensor_copy(
                sbv("g_tok").rearrange("p (e q) -> p q e", e=8)[:, first:first + 16, :],
                psums[bi][:, 0:128].rearrange("p (q e) -> p q e", e=8)),
                wait=[tts], inc="EV")
            bfree[("gt", bi)] = tok_ge
            tok_gtok[t8 // 16] = tok_ge

    def skslc(c0, c1, rows=None):
        return slc("sks", c0, c1, rows=rows)

    tok_g = None
    for it in range(8):
        w0 = [tok_g] if tok_g else [tok_gtok[0], tok_gtok[1]]
        t_pm = P.emit("vector", lambda v: v.tensor_reduce(
            out=skslc(112, 120), in_=gq("g_tok"), axis=AX.X, op=ALU.max),
            wait=w0, inc="EV", hazard=True)
        t_tp1 = P.emit("tensor", lambda t: t.transpose(
            psums[5][0:8, 0:128], skslc(112, 120), sbv("ident")),
            wait=[t_pm, bfree.get("snk5")], inc="MM")
        t_pm8 = P.emit("vector", lambda v: v.tensor_copy(sbv("pm8"), psums[5][0:8, 0:128]),
                       wait=[t_tp1], inc="EV", hazard=True)
        t_mx8 = P.emit("vector", lambda v: v.tensor_reduce(
            out=sbv("mx8"), in_=sbv("pm8"), axis=AX.X, op=ALU.max),
            wait=[t_pm8], inc="EV", hazard=True)
        t_tp2 = P.emit("tensor", lambda t: t.transpose(
            psums[5][0:1, 128:136], sbv("mx8"), sbv("ident")[0:8, 0:8]),
            wait=[t_mx8], inc="MM")
        t_mxr = P.emit("vector", lambda v: v.tensor_copy(
            sbv("mxrowR"), psums[5][0:1, 128:136]),
            wait=[t_tp2], inc="EV", hazard=True)
        t_bc1 = P.emit("tensor", lambda t: t.matmul(
            psums[5][:, 136:144], sbv("ones1"), sbv("mxrowR"), start=True, stop=True),
            wait=[t_mxr], inc="MM")
        t_mxbc = P.emit("vector", lambda v: v.tensor_copy(skslc(0, 8), psums[5][:, 136:144]),
                        wait=[t_bc1], inc="EV", hazard=True)
        t_sub = P.emit("vector", lambda v: v.tensor_tensor(
            out=gq("skt"), in0=gq("g_tok"),
            in1=skslc(0, 8).unsqueeze(2).broadcast_to((128, 8, 32)), op=ALU.subtract),
            wait=[t_mxbc], inc="EV", hazard=True)
        t_exp = P.emit("scalar", lambda sc: sc.activation(
            out=sbv("skt"), in_=sbv("skt"), func=AF.Exp), wait=[t_sub], inc="AC", hazard=True)

        def _prow_fn(v):
            with nc.allow_low_precision(reason="partial lse sums; f32r rounding ~1e-4"):
                return v.tensor_reduce(out=sbv("prowR"), in_=gq("skt"), axis=AX.X, op=ALU.add)
        t_pr = P.emit("vector", _prow_fn, wait=[t_exp], inc="EV", hazard=True)
        t_sm = P.emit("tensor", lambda t: t.matmul(
            psums[5][0:1, 144:152], sbv("onesc"), sbv("prowR"), start=True, stop=True),
            wait=[t_pr], inc="MM")
        t_ln = P.emit("scalar", lambda sc: sc.activation(
            out=skslc(120, 128, rows=1), in_=psums[5][0:1, 144:152], func=AF.Ln),
            wait=[t_sm], inc="AC", hazard=True)
        t_lse = P.emit("vector", lambda v: v.tensor_tensor(
            out=sbv("lseR"), in0=skslc(120, 128, rows=1), in1=sbv("mxrowR"), op=ALU.add),
            wait=[t_ln], inc="EV", hazard=True)
        t_bc2 = P.emit("tensor", lambda t: t.matmul(
            psums[5][:, 152:160], sbv("ones1"), sbv("lseR"), start=True, stop=True),
            wait=[t_lse], inc="MM")
        t_lsebc = P.emit("vector", lambda v: v.tensor_copy(skslc(8, 16), psums[5][:, 152:160]),
                         wait=[t_bc2], inc="EV", hazard=True)
        bfree["snk5"] = t_lsebc
        t_nn = P.emit("vector", lambda v: v.tensor_tensor(
            out=gq("g_tok"), in0=gq("g_tok"),
            in1=skslc(8, 16).unsqueeze(2).broadcast_to((128, 8, 32)), op=ALU.subtract),
            wait=[t_lsebc], inc="EV", hazard=True)
        t_em = P.emit("vector", lambda v: v.tensor_reduce(
            out=skslc(16, 48), in_=gqT("g_tok"), axis=AX.X, op=ALU.max),
            wait=[t_nn], inc="EV", hazard=True)
        t_es = P.emit("vector", lambda v: v.tensor_tensor(
            out=gqT("skt"), in0=gqT("g_tok"),
            in1=skslc(16, 48).unsqueeze(2).broadcast_to((128, 32, 8)), op=ALU.subtract),
            wait=[t_em], inc="EV", hazard=True)
        t_ee = P.emit("scalar", lambda sc: sc.activation(
            out=sbv("skt"), in_=sbv("skt"), func=AF.Exp), wait=[t_es], inc="AC", hazard=True)
        t_esum = P.emit("vector", lambda v: v.tensor_reduce(
            out=skslc(48, 80), in_=gqT("skt"), axis=AX.X, op=ALU.add),
            wait=[t_ee], inc="EV", hazard=True)
        t_eln = P.emit("scalar", lambda sc: sc.activation(
            out=skslc(80, 112), in_=skslc(48, 80), func=AF.Ln),
            wait=[t_esum], inc="AC", hazard=True)
        t_else = P.emit("vector", lambda v: v.tensor_tensor(
            out=skslc(80, 112), in0=skslc(80, 112), in1=skslc(16, 48), op=ALU.add),
            wait=[t_eln], inc="EV", hazard=True)
        tok_g = P.emit("vector", lambda v: v.tensor_tensor(
            out=gqT("g_tok"), in0=gqT("g_tok"),
            in1=skslc(80, 112).unsqueeze(2).broadcast_to((128, 32, 8)), op=ALU.subtract),
            wait=[t_else], inc="EV", hazard=True)
    tok_gfin = P.emit("scalar", lambda sc: sc.activation(
        out=sbv("g_tok"), in_=sbv("g_tok"), func=AF.Exp), wait=[tok_g], inc="AC", hazard=True)
    tok_gtok = {0: tok_gfin, 1: tok_gfin}

    if "gtokdbg" in dbg_d:
        P.emit("gpsimd", lambda gp: gp.dma_start(out=dbg_d["gtokdbg"], in_=sbv("g_tok")),
               wait=[tok_gtok[0], tok_gtok[1]], inc="LG", inc_n=16)
    t_lo = P.emit("vector", lambda v: v.tensor_scalar(
        out=slc("bis", 0, 8), in0=slc("g_tok", 0, 8), scalar1=0.0, scalar2=None,
        op0=ALU.mult), wait=[tok_gtok[0], tok_gtok[1]], inc="EV", hazard=True)
    t_hi = P.emit("vector", lambda v: v.tensor_scalar(
        out=slc("bis", 8, 16), in0=slc("g_tok", 0, 8), scalar1=0.0, scalar2=1.001,
        op0=ALU.mult, op1=ALU.add), inc="EV", hazard=True)
    prev = [t_lo, t_hi]
    for bit in range(35):
        t_mid = P.emit("vector", lambda v: v.tensor_tensor(
            out=slc("bis", 16, 24), in0=slc("bis", 0, 8), in1=slc("bis", 8, 16), op=ALU.add),
            wait=prev, inc="EV", hazard=True)
        t_mid = P.emit("vector", lambda v: v.tensor_scalar(
            out=slc("bis", 16, 24), in0=slc("bis", 16, 24), scalar1=0.5, scalar2=None,
            op0=ALU.mult), wait=[t_mid], inc="EV", hazard=True)
        if bit == 0 and "cntdbg" in dbg_d:
            t_snap = P.emit("vector", lambda v: v.tensor_copy(
                slc("bis", 40, 48), slc("bis", 16, 24)), wait=[t_mid], inc="EV", hazard=True)
        t_cmp = P.emit("vector", lambda v: v.tensor_tensor(
            out=gq("selm"), in0=gq("g_tok"),
            in1=slc("bis", 16, 24).unsqueeze(2).broadcast_to((128, 8, 32)),
            op=ALU.is_gt), wait=[t_mid], inc="EV", hazard=True)

        def _cnt_fn(v):
            with nc.allow_low_precision(reason="counts near 512 exact in f32r"):
                return v.tensor_reduce(out=sbv("cntR"), in_=gq("selm"),
                                       axis=AX.X, op=ALU.add)
        t_cnt = P.emit("vector", _cnt_fn, wait=[t_cmp], inc="EV", hazard=True)
        t_m1 = P.emit("tensor", lambda t: t.matmul(
            psums[5][0:1, 0:8], sbv("onesc"), sbv("cntR"),
            start=True, stop=True), wait=[t_cnt], inc="MM")
        t_cr = P.emit("vector", lambda v: v.tensor_copy(
            sbv("crR"), psums[5][0:1, 0:8]), wait=[t_m1], inc="EV", hazard=True)
        t_m2 = P.emit("tensor", lambda t: t.matmul(
            psums[5][:, 8:16], sbv("ones1"), sbv("crR"),
            start=True, stop=True), wait=[t_cr], inc="MM")
        if bit == 0 and "cntdbg" in dbg_d:
            tdb = P.emit("vector", lambda v: v.tensor_copy(
                slc("bis", 32, 40), psums[5][:, 8:16]), wait=[t_m2], inc="EV", hazard=True)
            tdb2 = P.emit("gpsimd", lambda gp: gp.dma_start(
                out=dbg_d["cntdbg"][:, 0:8], in_=slc("bis", 32, 40)),
                wait=[tdb], inc="LG", inc_n=16)
            P.emit("gpsimd", lambda gp: gp.dma_start(
                out=dbg_d["cntdbg"][:, 8:16], in_=slc("bis", 40, 48)),
                wait=[tdb2, ("EV", P.counts["EV"])], inc="LG", inc_n=16)
        t_msk = P.emit("vector", lambda v: v.tensor_scalar(
            out=sbv("bmsk"), in0=psums[5][:, 8:16], scalar1=511.5, scalar2=None,
            op0=ALU.is_ge), wait=[t_m2], inc="EV", hazard=True)
        t_lo = P.emit("vector", lambda v: v.select(
            out=slc("bis", 0, 8), mask=sbv("bmsk"), on_true=slc("bis", 16, 24),
            on_false=slc("bis", 0, 8)), wait=[t_msk], inc="EV", hazard=True)
        t_hi = P.emit("vector", lambda v: v.select(
            out=slc("bis", 8, 16), mask=sbv("bmsk"), on_true=slc("bis", 8, 16),
            on_false=slc("bis", 16, 24)), wait=[t_msk], inc="EV", hazard=True)
        prev = [t_lo, t_hi]
    if "taudbg" in dbg_d:
        P.emit("gpsimd", lambda gp: gp.dma_start(out=dbg_d["taudbg"], in_=slc("bis", 0, 8)),
               wait=[t_lo], inc="LG", inc_n=16)
    t_sel = P.emit("vector", lambda v: v.tensor_tensor(
        out=gq("selm"), in0=gq("g_tok"),
        in1=slc("bis", 0, 8).unsqueeze(2).broadcast_to((128, 8, 32)), op=ALU.is_gt),
        wait=[t_lo], inc="EV", hazard=True)
    t_hard = P.emit("vector", lambda v: v.tensor_scalar(
        out=sbv("hardm"), in0=sbv("g_tok"), scalar1=0.5, scalar2=None, op0=ALU.is_gt),
        wait=[tok_gtok[0], tok_gtok[1]], inc="EV", hazard=True)
    t_nsel = P.emit("vector", lambda v: v.tensor_reduce(
        out=slc("win", 0, 32), in_=gqT("selm"), axis=AX.X, op=ALU.add),
        wait=[t_sel], inc="EV", hazard=True)
    t_ind = P.emit("vector", lambda v: v.tensor_scalar(
        out=slc("win", 32, 64), in0=slc("win", 0, 32), scalar1=1.5, scalar2=None,
        op0=ALU.is_lt), wait=[t_nsel], inc="EV", hazard=True)
    t_w1 = P.emit("vector", lambda v: v.tensor_tensor(
        out=sbv("hardm"), in0=sbv("hardm"), in1=sbv("selm"), op=ALU.mult),
        wait=[t_hard, t_sel], inc="EV", hazard=True)
    t_win = P.emit("vector", lambda v: v.tensor_tensor(
        out=gqT("win"), in0=gqT("hardm"),
        in1=slc("win", 32, 64).unsqueeze(2).broadcast_to((128, 32, 8)), op=ALU.mult),
        wait=[t_w1, t_ind], inc="EV", hazard=True)
    if "windbg" in dbg_d:
        P.emit("gpsimd", lambda gp: gp.dma_start(out=dbg_d["windbg"], in_=gq("win")),
               wait=[t_win], inc="LG", inc_n=16)
    if "seldbg" in dbg_d:
        P.emit("gpsimd", lambda gp: gp.dma_start(out=dbg_d["seldbg"], in_=gq("selm")),
               wait=[t_sel, t_w1], inc="LG", inc_n=16)
    drop("g_tok", "skt", "sks", "pm8", "mx8", "mxrowR", "lseR", "prowR",
         "selm", "hardm", "bis", "bmsk", "cntR", "crR",
         "gatesT", "ewT", "rowA", "rowB",
         "xtokA", "xtokB", "bnscrA", "bnscrB", "mvA", "mvB", "lncolA", "lncolB",
         "xn_sA", "xn_sB", "xnball", "expstgA", "expstgB")

    if upto == "sink":
        return _close(nc, P, semh)

    P.start_phase()
    for nm in ("wlnzT", "wdtT", "wyactT", "wyT"):
        sb(nm, G * 1024, F32R)
    sb("wstage2", 8192)
    small = [("wlnzT", "wlnz", "gsio_c"), ("wdtT", "wdt", "gsio_c"),
             ("wyactT", "wyact", "gsio_c"), ("wyT", "wy", None)]
    stage_free2 = None
    bank_free2 = {}
    for nm, src, scale in small:
        tok_stg = P.emit("sync", lambda s, src=src: s.dma_start(
            out=sbv("wstage2").rearrange("p (fb d) -> p fb d", fb=8),
            in_=w_d[src].rearrange("(fb p) d -> p fb d", p=128)),
            wait=[stage_free2, tok_consts], inc="LD", inc_n=16)
        last_tp = None
        for g in range(G):
            for half in range(2):
                bi = half
                tk = None
                for fbh in range(4):
                    fb = half * 4 + fbh
                    tk = P.emit("tensor", lambda t, g=g, fb=fb, fbh=fbh, bi=bi: t.transpose(
                        psums[bi][:, fbh * 128:(fbh + 1) * 128],
                        slc("wstage2", fb * 1024 + g * 128, fb * 1024 + g * 128 + 128),
                        sbv("ident")),
                        wait=[bank_free2.get(bi), tok_stg] if fbh == 0 else (),
                        inc="MM" if fbh == 3 else None)

                def mk_ev(nm=nm, g=g, half=half, bi=bi, scale=scale):
                    def fn(v):
                        dst = slc(nm, g * 1024 + half * 512, g * 1024 + half * 512 + 512)
                        if scale is None:
                            return v.tensor_copy(dst, psums[bi][:])
                        return v.tensor_scalar(out=dst, in0=psums[bi][:],
                                               scalar1=slc(scale, g, g + 1), scalar2=None,
                                               op0=ALU.mult)
                    return fn
                bank_free2[bi] = P.emit("vector", mk_ev(), wait=[tk], inc="EV")
                last_tp = tk
        stage_free2 = last_tp
    drop("wstage2")

    P.start_phase()
    sb("xt1A", 2048, F32R)
    sb("xt1B", 2048, F32R)
    sb("rsrowS", 256, F32R, rows=1)
    sb("rbc", 256)
    sb("zs", 256)
    sb("ds", 256)
    sb("dscr", 1024)
    sb("hcur", 2048)
    sb("hinit", 8)
    sb("hR", 2048, F32R)
    sb("sacts", 2048)
    sb("xa", 2048, F32R)
    sb("sq", 512, F32R)
    sb("x2s", 1024, F32R)
    sb("rfrF", 256, rows=1)
    sb("rfrR", 256, F32R, rows=1)
    sb("xab", 1024)

    def d0s(p):
        return slc("dscr", p * 256, p * 256 + 256)

    def d1s(p):
        return slc("dscr", 512 + p * 256, 512 + p * 256 + 256)

    pb = {}
    prev_xt = {}
    tok_x2 = {}
    tok_xaT = {}
    xa_guard = []
    for it in range(N1):
        key = "A" if it % 2 == 0 else "B"
        xt = "xt1" + key
        sl0 = it * TAU1
        tok_ldt = P.emit("sync", lambda s, xt=xt, sl0=sl0: s.dma_start(
            out=sbv(xt).rearrange("p (g c) -> p g c", g=G),
            in_=xnF_d[:, :, sl0:sl0 + 256].rearrange("g p c -> p g c")),
            wait=[prev_xt.get(key), prev_xt.get(key + "v")]
            + [tok_xnF[(sl0 // 512, g)][0] for g in range(G)],
            inc="LD", inc_n=16)
        ldt = [tok_ldt] * G
        rtp = None
        for s2 in range(2):
            sub = sl0 // 128 + s2
            rtp = P.emit("tensor", lambda t, sub=sub, s2=s2: t.transpose(
                psums[4][0:1, 256 + s2 * 128: 256 + s2 * 128 + 128].bitcast(F32R),
                slc("rsioc", sub, sub + 1), sbv("identr")),
                wait=[pb.get("4r")] if s2 == 0 else (), inc="MM" if s2 == 1 else None)
        t_rsr = P.emit("vector", lambda v: v.tensor_copy(
            sbv("rsrowS"), psums[4][0:1, 256:512].bitcast(F32R)),
            wait=[rtp, pb.get("rsrow")], inc="EV")
        t_rb = P.emit("tensor", lambda t: t.matmul(
            psums[4][:, 0:256], sbv("ones1"), sbv("rsrowS"), start=True, stop=True),
            wait=[t_rsr, pb.get(4)], inc="MM")
        pb["4r"] = t_rb
        t_rbc = P.emit("vector", lambda v: v.tensor_copy(sbv("rbc"), psums[4][:, 0:256]),
                       wait=[t_rb, pb.get("rbc")], inc="EV")
        pb[4] = t_rbc
        pb["rsrow"] = t_rb
        scan_toks = []
        slt = {}
        for fc in range(G):
            dpar = fc % 2
            bz1 = 0 if fc % 2 == 0 else 2
            bd1 = 1 if fc % 2 == 0 else 3
            mmz = None
            for g in range(G):
                mmz = P.emit("tensor", lambda t, g=g, fc=fc, xt=xt, bz1=bz1: t.matmul(
                    psums[bz1][:, 0:256],
                    slc("wlnzT", g * 1024 + fc * 128, g * 1024 + fc * 128 + 128),
                    slc(xt, g * 256, (g + 1) * 256),
                    start=(g == 0), stop=(g == G - 1)),
                    wait=[ldt[g]] + ([pb.get(("a", bz1))] if g == 0 else []),
                    inc="MM" if g == G - 1 else None)
            mmd = None
            for g in range(G):
                mmd = P.emit("tensor", lambda t, g=g, fc=fc, xt=xt, bd1=bd1: t.matmul(
                    psums[bd1][:, 0:256],
                    slc("wdtT", g * 1024 + fc * 128, g * 1024 + fc * 128 + 128),
                    slc(xt, g * 256, (g + 1) * 256),
                    start=(g == 0), stop=(g == G - 1)),
                    wait=[pb.get(("a", bd1))] if g == 0 else (), inc="MM" if g == G - 1 else None)
            t_z = P.emit("vector", lambda v, bz1=bz1: v.tensor_tensor(
                out=sbv("zs"), in0=psums[bz1][:, 0:256], in1=sbv("rbc"), op=ALU.mult),
                wait=[mmz, t_rbc], inc="EV")
            pb[("a", bz1)] = t_z
            t_d = P.emit("vector", lambda v, bd1=bd1: v.tensor_tensor(
                out=sbv("ds"), in0=psums[bd1][:, 0:256], in1=sbv("rbc"), op=ALU.mult),
                wait=[mmd], inc="EV")
            t_sz = P.emit("scalar", lambda sc, fc=fc: sc.activation(
                out=sbv("zs"), in_=sbv("zs"), func=AF.Sigmoid, bias=slc("blnz_c", fc, fc + 1)),
                wait=[t_z], inc="AC")
            t_d0 = P.emit("scalar", lambda sc, fc=fc, dpar=dpar: sc.activation(
                out=d0s(dpar), in_=sbv("ds"), func=AF.Sigmoid,
                scale=-1.0, bias=slc("nbdt_c", fc, fc + 1)),
                wait=[t_d, scan_toks[-2] if len(scan_toks) >= 2 else pb.get(("dscr", dpar))],
                inc="AC")
            t_sd = P.emit("scalar", lambda sc, fc=fc: sc.activation(
                out=sbv("ds"), in_=sbv("ds"), func=AF.Sigmoid, bias=slc("bdt_c", fc, fc + 1)),
                wait=[t_d0], inc="AC", hazard=True)
            pb[("a", bd1)] = t_sd
            t_d1 = P.emit("vector", lambda v, dpar=dpar: v.tensor_tensor(
                out=d1s(dpar), in0=sbv("ds"), in1=sbv("zs"), op=ALU.mult),
                wait=[t_sd, t_sz,
                      scan_toks[-2] if len(scan_toks) >= 2 else pb.get(("dscr1", dpar))],
                inc="EV")

            def mk_scan(fc=fc, it=it, dpar=dpar):
                def fn(v):
                    init = slc("h0_c", fc, fc + 1) if it == 0 else slc("hinit", fc, fc + 1)
                    return v.tensor_tensor_scan(
                        out=slc("hcur", fc * 256, (fc + 1) * 256),
                        data0=d0s(dpar), data1=d1s(dpar),
                        initial=init, op0=ALU.mult, op1=ALU.add)
                return fn
            t_scan = P.emit("vector", mk_scan(), wait=[t_d0, t_d1], inc="EV")
            P.emit("vector", lambda v, fc=fc: v.tensor_copy(
                slc("hinit", fc, fc + 1), slc("hcur", fc * 256 + 255, fc * 256 + 256)),
                wait=[t_scan], inc="EV")
            scan_toks.append(t_scan)
            by1 = 6 if fc % 2 == 0 else 7
            mmy = None
            for g in range(G):
                mmy = P.emit("tensor", lambda t, g=g, fc=fc, xt=xt, by1=by1: t.matmul(
                    psums[by1][:, 0:256],
                    slc("wyactT", g * 1024 + fc * 128, g * 1024 + fc * 128 + 128),
                    slc(xt, g * 256, (g + 1) * 256),
                    start=(g == 0), stop=(g == G - 1)),
                    wait=[pb.get(("ya", by1))] if g == 0 else (),
                    inc="MM" if g == G - 1 else None)
            t_yev = P.emit("vector", lambda v, fc=fc, by1=by1: v.tensor_tensor(
                out=slc("sacts", fc * 256, (fc + 1) * 256), in0=psums[by1][:, 0:256],
                in1=sbv("rbc"), op=ALU.mult),
                wait=[mmy, t_rbc, pb.get(("sacts", fc))], inc="EV")
            pb[("ya", by1)] = t_yev
            t_sl = P.emit("scalar", lambda sc, fc=fc: sc.activation(
                out=slc("sacts", fc * 256, (fc + 1) * 256),
                in_=slc("sacts", fc * 256, (fc + 1) * 256), func=AF.Silu,
                bias=slc("byact_c", fc, fc + 1)),
                wait=[t_yev], inc="AC", hazard=True)
            slt[fc] = t_sl
        t_hR = P.emit("vector", lambda v: v.tensor_copy(sbv("hR"), sbv("hcur")),
                      wait=[scan_toks[-1], pb.get("hRr")], inc="EV")
        if "hdbg" in dbg_d:
            for g in range(G):
                P.emit("gpsimd", lambda gp, g=g, sl0=sl0: gp.dma_start(
                    out=dbg_d["hdbg"][g, :, sl0:sl0 + 256],
                    in_=slc("hcur", g * 256, (g + 1) * 256)),
                    wait=[t_hR], inc="LG", inc_n=16)
        sq_mm = None
        last_mmy = None
        last_mmw = None
        t_y_prev = None
        t_xa_prev = None
        t_sq = None
        for fc in range(G):
            bw1 = 1 if fc % 2 == 0 else 3
            mmw = None
            for g in range(G):
                mmw = P.emit("tensor", lambda t, g=g, fc=fc, bw1=bw1: t.matmul(
                    psums[bw1][:, 0:256],
                    slc("wyT", g * 1024 + fc * 128, g * 1024 + fc * 128 + 128),
                    slc("hR", g * 256, (g + 1) * 256),
                    start=(g == 0), stop=(g == G - 1)),
                    wait=[t_hR] + ([pb.get(("a", bw1))] if g == 0 else []),
                    inc="MM" if g == G - 1 else None)
            t_y = P.emit("vector", lambda v, fc=fc, bw1=bw1: v.scalar_tensor_tensor(
                out=sbv("zs"), in0=psums[bw1][:, 0:256], scalar=slc("by_c", fc, fc + 1),
                in1=slc("sacts", fc * 256, (fc + 1) * 256), op0=ALU.add, op1=ALU.mult),
                wait=[mmw, slt[fc], t_xa_prev], inc="EV")
            pb[("a", bw1)] = t_y
            pb[("sacts", fc)] = t_y
            t_y_prev = t_y
            t_xa = P.emit("vector", lambda v, fc=fc, xt=xt: v.tensor_tensor(
                out=slc("xa", fc * 256, (fc + 1) * 256),
                in0=slc(xt, fc * 256, (fc + 1) * 256), in1=sbv("zs"), op=ALU.add),
                wait=[t_y] + (xa_guard if fc == 0 else []), inc="EV")
            t_xa_prev = t_xa
            spar = fc % 2
            t_sq = P.emit("scalar", lambda sc, fc=fc, spar=spar: sc.activation(
                out=slc("sq", spar * 256, spar * 256 + 256),
                in_=slc("xa", fc * 256, (fc + 1) * 256), func=AF.Square),
                wait=[t_xa, pb.get(("sq", spar))], inc="AC")
            sq_mm = P.emit("tensor", lambda t, fc=fc, spar=spar: t.matmul(
                psums[5][0:1, 0:256], sbv("onesc"),
                slc("sq", spar * 256, spar * 256 + 256),
                start=(fc == 0), stop=(fc == G - 1)),
                wait=[t_sq] + ([pb.get(5)] if fc == 0 else []),
                inc="MM" if fc == G - 1 else None)
            pb[("sq", spar)] = sq_mm
            last_mmy = mmy
            last_mmw = mmw
        pb["rbc"] = t_yev
        pb["hRr"] = last_mmw
        t_rf0 = P.emit("vector", lambda v: v.tensor_scalar(
            out=sbv("rfrF"), in0=psums[5][0:1, 0:256], scalar1=1.0 / D,
            scalar2=1e-6, op0=ALU.mult, op1=ALU.add), wait=[sq_mm, pb.get("rfF")],
            inc="EV", hazard=True)
        pb[5] = t_rf0
        t_rf1 = P.emit("scalar", lambda sc: sc.activation(
            out=sbv("rfrF"), in_=sbv("rfrF"), func=AF.Ln), wait=[t_rf0], inc="AC", hazard=True)
        pb["rfF"] = t_rf1
        t_rf = P.emit("scalar", lambda sc: sc.activation(
            out=sbv("rfrR"), in_=sbv("rfrF"), func=AF.Exp, scale=-0.5),
            wait=[t_rf1, pb.get("rfR")], inc="AC", hazard=True)
        t_rb2 = P.emit("tensor", lambda t: t.matmul(
            psums[4][:, 256:512], sbv("ones1"), sbv("rfrR"),
            start=True, stop=True), wait=[t_rf, pb.get("4b")], inc="MM")
        pb["rfR"] = t_rb2
        t_x2_last = None
        for fc in range(G):
            slot = fc % 4
            t_x2 = P.emit("vector", lambda v, fc=fc, slot=slot: v.scalar_tensor_tensor(
                out=slc("x2s", slot * 256, slot * 256 + 256),
                in0=slc("xa", fc * 256, (fc + 1) * 256),
                scalar=slc("gffn_c", fc, fc + 1), in1=psums[4][:, 256:512],
                op0=ALU.mult, op1=ALU.mult),
                wait=[t_rb2, tok_x2.get((it, fc - 4), tok_x2.get((it - 1, fc + 4)))],
                inc="EV")
            tok_x2[(it, fc)] = P.emit("gpsimd", lambda gp, fc=fc, sl0=sl0, slot=slot: gp.dma_start(
                out=x2F_d[fc, :, sl0:sl0 + 256],
                in_=slc("x2s", slot * 256, slot * 256 + 256)),
                wait=[t_x2], inc="LG", inc_n=16)
            t_x2_last = t_x2
        pb["4b"] = t_x2_last
        pb[("dscr", 0)] = scan_toks[-2]
        pb[("dscr", 1)] = scan_toks[-1]
        pb[("dscr1", 0)] = scan_toks[-2]
        pb[("dscr1", 1)] = scan_toks[-1]
        xa_guard = [t_x2_last, t_sq]
        for s2 in range(2):
            tks = {}
            for g in range(G):
                bi = 6 + (g // 4)
                tks[g] = P.emit("tensor", lambda t, g=g, s2=s2, bi=bi: t.transpose(
                    psums[bi][:, (g % 4) * 128:(g % 4) * 128 + 128].bitcast(F32R),
                    slc("xa", g * 256 + s2 * 128, g * 256 + s2 * 128 + 128),
                    sbv("identr")),
                    wait=[pb.get(("xat", bi)), pb.get(("ya", bi))] if g % 4 == 0 else (),
                    inc="MM" if g % 4 == 3 else None)
            t_e1 = P.emit("vector", lambda v: v.tensor_copy(
                slc("xab", 0, 512), psums[6][:]), wait=[tks[3], pb.get("xabd")], inc="EV")
            t_e2 = P.emit("vector", lambda v: v.tensor_copy(
                slc("xab", 512, 1024), psums[7][:]), wait=[tks[7]], inc="EV")
            pb[("xat", 6)] = t_e1
            pb[("xat", 7)] = t_e2
            pb[("ya", 6)] = t_e1
            pb[("ya", 7)] = t_e2
            tl = it * 2 + s2
            tok_xaT[tl] = P.emit("gpsimd", lambda gp, tl=tl: gp.dma_start(
                out=xaT_d[tl], in_=sbv("xab")), wait=[t_e1, t_e2], inc="LG", inc_n=16)
            pb["xabd"] = tok_xaT[tl]
            xa_guard += [tks[3], tks[7]]
        prev_xt[key] = last_mmy
        prev_xt[key + "v"] = t_xa_prev
    drop("xt1A", "xt1B", "rsrowS", "rbc", "zs", "ds", "dscr", "hcur", "hinit", "hR",
         "sacts", "xa", "sq", "x2s", "rfrF", "rfrR", "xab",
         "wlnzT", "wdtT", "wyactT", "wyT", "rsioc")

    if upto == "ph1a":
        return _close(nc, P, semh)

    P.start_phase()
    sb("x2tA", 2048, BF16)
    sb("x2tB", 2048, BF16)
    sb("xaTtA", 2048)
    sb("xaTtB", 2048)
    sb("wfcbA", 4096, BF16)
    sb("wfcbB", 4096, BF16)
    sb("wfabA", 4096, BF16)
    sb("wfabB", 4096, BF16)
    sb("woutbA", 4096, BF16)
    sb("woutbB", 4096, BF16)
    sb("hid", 1024, BF16)
    sb("sact1b", 512)
    sb("sbnc", 2048)
    sb("boutrS", D, BF16, rows=1)
    tok_bo = P.emit("gpsimd", lambda gp: gp.dma_start(out=sbv("boutrS"), in_=boutr_d),
                    inc="LG", inc_n=16)

    prev_x2t = {}
    prev_w = {}
    prev_s = {}

    def emit_1b_loads(it):
        key = "A" if it % 2 == 0 else "B"
        x2t = "x2t" + key
        sl0 = it * TAU1
        tok_lx2 = P.emit("sync", lambda s, x2t=x2t, sl0=sl0: s.dma_start(
            out=sbv(x2t).rearrange("p (g c) -> p g c", g=G),
            in_=x2F_d[:, :, sl0:sl0 + 256].rearrange("g p c -> p g c")),
            wait=[prev_x2t.get(key)] + [tok_x2.get((it, g)) for g in range(G)],
            inc="LD", inc_n=16)
        lx2 = [tok_lx2] * G
        xak = "xaTt" + key
        tok_lxa = P.emit("sync", lambda s, it=it, xak=xak: s.dma_start(
            out=sbv(xak).rearrange("p (t c) -> p t c", t=2),
            in_=xaT_d[it * 2:(it + 1) * 2].rearrange("t p c -> p t c")),
            wait=[prev_s.get(("sd", key, 0)), prev_s.get(("sd", key, 1)),
                  tok_xaT.get(it * 2), tok_xaT.get(it * 2 + 1)], inc="LD", inc_n=16)
        lxa = [tok_lxa] * 2
        return lx2, lxa

    loads = {0: emit_1b_loads(0)}
    for it in range(N1):
        key = "A" if it % 2 == 0 else "B"
        x2t = "x2t" + key
        sl0 = it * TAU1
        ld_x2, ld_xa = loads.pop(it)
        if it + 1 < N1:
            loads[it + 1] = emit_1b_loads(it + 1)
        tok_offc = None
        hid_guard = {0: None, 1: None}
        for ffc in range(8):
            wk = "A" if ffc % 2 == 0 else "B"
            wfcb, wfab, woutb = "wfcb" + wk, "wfab" + wk, "woutb" + wk
            tok_wfc = P.emit("sync", lambda s, ffc=ffc, wfcb=wfcb: s.dma_start(
                out=sbv(wfcb).rearrange("p (g c) -> p g c", g=G),
                in_=wfcT_d[:, :, ffc * 512:(ffc + 1) * 512].rearrange("g p c -> p g c")),
                wait=[prev_w.get(wfcb)], inc="LD", inc_n=16)
            tok_wfa = P.emit("sync", lambda s, ffc=ffc, wfab=wfab: s.dma_start(
                out=sbv(wfab).rearrange("p (g c) -> p g c", g=G),
                in_=wfcaT_d[:, :, ffc * 512:(ffc + 1) * 512].rearrange("g p c -> p g c")),
                wait=[prev_w.get(wfab)], inc="LD", inc_n=16)
            ld_w = {}
            for g in range(G):
                ld_w[("fc", g)] = tok_wfc
                ld_w[("fa", g)] = tok_wfa
            tok_wo = P.emit("sync", lambda s, ffc=ffc, woutb=woutb: s.dma_start(
                out=sbv(woutb).rearrange("p (k c) -> p k c", k=4),
                in_=woutT_d[ffc * 4:(ffc + 1) * 4].rearrange("k p c -> p k c")),
                wait=[prev_w.get(woutb)], inc="LD", inc_n=16)
            ld_wo = {k4: tok_wo for k4 in range(4)}
            def emit_outs(fs, fpar, t_hid_tok):
                nonlocal tok_offc
                lo = None
                for tc in range(2):
                    for nh in range(2):
                        bo = 4 + tc * 2 + nh
                        is_first = (ffc == 0 and fs == 0)
                        is_tok = (tc == 1 and nh == 1)
                        lo = P.emit(
                            "tensor",
                            lambda t, tc=tc, nh=nh, bo=bo, fpar=fpar, woutb=woutb, k4=fs,
                            is_first=is_first: t.matmul(
                                psums[bo][:, 0:512],
                                slc("hid", fpar * 256 + tc * 128, fpar * 256 + tc * 128 + 128),
                                slc(woutb, k4 * 1024 + nh * 512, k4 * 1024 + nh * 512 + 512),
                                start=is_first, stop=False),
                            wait=[t_hid_tok, ld_wo[fs]]
                            + ([pb.get(("1b", bo))] if is_first else []),
                            inc="MM" if is_tok else None)
                hid_guard[fpar] = lo
                tok_offc = lo

            pend = None
            for fs in range(4):
                fpar = fs % 2
                bz = fpar
                ba = 2 + fpar
                mmf = None
                mma = None
                nrep1b = 2 if (it == 0 and ffc == 0 and fs < 2) else 1
                for _rep in range(nrep1b):
                    for g in range(G):
                        mmf = P.emit("tensor", lambda t, g=g, fs=fs, bz=bz, wfcb=wfcb, x2t=x2t: t.matmul(
                            psums[bz][:, 0:256],
                            slc(wfcb, g * 512 + fs * 128, g * 512 + fs * 128 + 128),
                            slc(x2t, g * 256, (g + 1) * 256),
                            start=(g == 0), stop=(g == G - 1)),
                            wait=[ld_w[("fc", g)], ld_x2[g]] + ([pb.get(("1b", bz))] if g == 0 else []),
                            inc="MM" if g == G - 1 else None)
                    for g in range(G):
                        mma = P.emit("tensor", lambda t, g=g, fs=fs, ba=ba, wfab=wfab, x2t=x2t: t.matmul(
                            psums[ba][:, 0:256],
                            slc(wfab, g * 512 + fs * 128, g * 512 + fs * 128 + 128),
                            slc(x2t, g * 256, (g + 1) * 256),
                            start=(g == 0), stop=(g == G - 1)),
                            wait=[ld_w[("fa", g)]] + ([pb.get(("1b", ba))] if g == 0 else []),
                            inc="MM" if g == G - 1 else None)
                if pend is not None:
                    emit_outs(*pend)
                fcol = ffc * 4 + fs
                t_sl = P.emit("scalar", lambda sc, fcol=fcol, ba=ba, fpar=fpar: sc.activation(
                    out=slc("sact1b", fpar * 256, fpar * 256 + 256),
                    in_=psums[ba][:, 0:256], func=AF.Silu, bias=slc("bfca_c", fcol, fcol + 1)),
                    wait=[mma, pb.get(("sl", fpar))], inc="AC")
                pb[("1b", ba)] = t_sl
                t_hid = P.emit("vector", lambda v, fcol=fcol, bz=bz, fpar=fpar: v.scalar_tensor_tensor(
                    out=slc("hid", fpar * 256, fpar * 256 + 256),
                    in0=psums[bz][:, 0:256], scalar=slc("bfc_c", fcol, fcol + 1),
                    in1=slc("sact1b", fpar * 256, fpar * 256 + 256),
                    op0=ALU.add, op1=ALU.mult),
                    wait=[mmf, t_sl, hid_guard[fpar]], inc="EV")
                pb[("1b", bz)] = t_hid
                pb[("sl", fpar)] = t_hid
                pend = (fs, fpar, t_hid)
            emit_outs(*pend)
            prev_w[wfcb] = tok_offc
            prev_w[wfab] = tok_offc
            prev_w[woutb] = tok_offc
        for tc in range(2):
            for nh in range(2):
                bo = 4 + tc * 2 + nh
                t_b = P.emit("tensor", lambda t, nh=nh, bo=bo: t.matmul(
                    psums[bo][:, 0:512], sbv("ones1B"),
                    slc("boutrS", nh * 512, nh * 512 + 512),
                    start=False, stop=True), wait=[tok_bo], inc="MM")
                t_s = P.emit("vector", lambda v, tc=tc, nh=nh, bo=bo, key=key: v.tensor_tensor(
                    out=slc("sbnc", tc * 1024 + nh * 512, tc * 1024 + nh * 512 + 512),
                    in0=psums[bo][:, 0:512],
                    in1=slc("xaTt" + key, tc * 1024 + nh * 512, tc * 1024 + nh * 512 + 512),
                    op=ALU.add), wait=[t_b, ld_xa[tc]], inc="EV")
                pb[("1b", bo)] = t_s
        for tc in range(2):
            tl = it * 2 + tc
            t_sd = P.emit("gpsimd", lambda gp, tc=tc, tl=tl: gp.dma_start(
                out=sT_d[tl], in_=slc("sbnc", tc * 1024, (tc + 1) * 1024)),
                wait=[pb[("1b", 4 + tc * 2)], pb[("1b", 4 + tc * 2 + 1)]], inc="LG", inc_n=16)
            prev_s[("sd", key, tc)] = t_sd
            prev_s[("sT", tl)] = t_sd
        prev_x2t[key] = tok_offc
    drop("x2tA", "x2tB", "xaTtA", "xaTtB", "wfcbA", "wfcbB", "wfabA", "wfabB",
         "woutbA", "woutbB", "hid", "sact1b", "sbnc", "boutrS")

    if upto == "ph1b":
        return _close(nc, P, semh)

    P.start_phase()
    sb("xnbtA", 4096, BF16)
    sb("xnbtB", 4096, BF16)
    sb("expbA", 8192, BF16)
    sb("expbB", 8192, BF16)
    sb("expbC", 8192, BF16)
    sb("sEA", 4096)
    sb("sEB", 4096)
    sb("xEA", 4096)
    sb("xEB", 4096)
    sb("yacc", 4096)
    sb("outb", 4096)
    prev2 = {}

    def emit_2_loads(it):
        key = "A" if it % 2 == 0 else "B"
        xnbt = "xnbt" + key
        sl0 = it * 512
        tok_lb = P.emit("sync", lambda s, xnbt=xnbt, sl0=sl0: s.dma_start(
            out=sbv(xnbt).rearrange("p (g c) -> p g c", g=G),
            in_=xnB_d[:, :, sl0:sl0 + 512].rearrange("g p c -> p g c")),
            wait=[prev2.get(xnbt)] + [tok_xnF[(sl0 // 512, g)][1] for g in range(G)],
            inc="LD", inc_n=16)
        lb = [tok_lb] * G
        tok_ls = P.emit("sync", lambda s, it=it, key=key: s.dma_start(
            out=sbv("sE" + key).rearrange("p (t c) -> p t c", t=4),
            in_=sT_d[it * 4:(it + 1) * 4].rearrange("t p c -> p t c")),
            wait=[prev2.get(("sE", key, tc)) for tc in range(4)]
            + [prev_s.get(("sT", it * 4 + tc)) for tc in range(4)], inc="LD", inc_n=16)
        ls_ = [tok_ls] * 4
        tok_lx = P.emit("sync", lambda s, it=it, key=key: s.dma_start(
            out=sbv("xE" + key).rearrange("p (t c) -> p t c", t=4),
            in_=x_d[it * 512:(it + 1) * 512, :].rearrange("(t p) c -> p t c", p=128)),
            wait=[prev2.get(("xE", key, tc)) for tc in range(4)], inc="LD", inc_n=16)
        lx_ = [tok_lx] * 4
        return lb, ls_, lx_

    loads2 = {0: emit_2_loads(0)}
    for it in range(N2):
        key = "A" if it % 2 == 0 else "B"
        xnbt = "xnbt" + key
        sl0 = it * 512
        ldb, ld_s, ld_x = loads2.pop(it)
        acc_last = {}
        last_mm_it = None
        for e in range(E):
            en = ("expbA", "expbB", "expbC")[(it * E + e) % 3]
            w = [prev2.get((en, "free"))]
            if it == 0 and e < 3:
                w += tok_expB
            tok_le = P.emit("sync", lambda s, e=e, en=en: s.dma_start(
                out=sbv(en).rearrange("p (g c) -> p g c", g=G),
                in_=expB_d[e].rearrange("g p c -> p g c")),
                wait=w, inc="LD", inc_n=16)
            ld_e = {g: tok_le for g in range(G)}
            if it == 0 and e == 0 and "expbdbg" in dbg_d:
                P.emit("gpsimd", lambda gp, en=en: gp.dma_start(
                    out=dbg_d["expbdbg"], in_=sbv(en)),
                    wait=[ld_e[g2] for g2 in range(G)], inc="LG", inc_n=16)
            for tc in range(4):
                for oh in range(2):
                    bo = (e % 2) * 2 + oh
                    mm = None
                    nrep = 2 if (it == 0 and e == 0) else 1
                    for rep in range(nrep):
                        for g in range(G):
                            mm = P.emit("tensor", lambda t, g=g, e=e, oh=oh, tc=tc, bo=bo,
                                        xnbt=xnbt, en=en: t.matmul(
                                psums[bo][:, 0:512],
                                slc(xnbt, g * 512 + tc * 128, g * 512 + tc * 128 + 128),
                                slc(en, g * 1024 + oh * 512, g * 1024 + oh * 512 + 512),
                                start=(g == 0), stop=(g == G - 1)),
                                wait=[ldb[g], ld_e[g]] + ([pb.get(("2", bo))] if g == 0 else []),
                                inc="MM" if g == G - 1 else None)
                    tl = it * 4 + tc

                    def mk_ev(e=e, oh=oh, tc=tc, bo=bo, tl=tl, mm=mm):
                        def fn(v):
                            wcol = sbv("win")[:, e * 32 + tl:e * 32 + tl + 1]
                            dst = slc("yacc", tc * 1024 + oh * 512, tc * 1024 + oh * 512 + 512)
                            if e == 0:
                                return v.tensor_scalar(out=dst, in0=psums[bo][:, 0:512],
                                                       scalar1=wcol, scalar2=None, op0=ALU.mult)
                            return v.scalar_tensor_tensor(out=dst, in0=psums[bo][:, 0:512],
                                                          scalar=wcol, in1=dst,
                                                          op0=ALU.mult, op1=ALU.add)
                        return fn
                    acc_last[(tc, oh)] = P.emit("vector", mk_ev(),
                                                wait=[mm, t_win, acc_last.get((tc, oh))],
                                                inc="EV")
                    pb[("2", bo)] = acc_last[(tc, oh)]
                    last_mm_it = mm
            prev2[(en, "free")] = last_mm_it
        if it + 1 < N2:
            loads2[it + 1] = emit_2_loads(it + 1)
        if it == 0 and "yaccdbg" in dbg_d:
            P.emit("gpsimd", lambda gp: gp.dma_start(
                out=dbg_d["yaccdbg"], in_=slc("yacc", 0, 1024)),
                wait=[acc_last[(0, 0)], acc_last[(0, 1)]], inc="LG", inc_n=16)
        for tc in range(4):
            tl = it * 4 + tc
            t_g = P.emit("scalar", lambda sc, tc=tc: sc.activation(
                out=slc("yacc", tc * 1024, (tc + 1) * 1024),
                in_=slc("yacc", tc * 1024, (tc + 1) * 1024), func=AF.Silu),
                wait=[acc_last[(tc, 0)], acc_last[(tc, 1)]], inc="AC", hazard=True)
            t_sg = P.emit("vector", lambda v, tc=tc, key=key: v.tensor_tensor(
                out=slc("outb", tc * 1024, (tc + 1) * 1024),
                in0=slc("yacc", tc * 1024, (tc + 1) * 1024),
                in1=slc("sE" + key, tc * 1024, (tc + 1) * 1024), op=ALU.mult),
                wait=[t_g, ld_s[tc], prev2.get(("outd", tc))], inc="EV")
            t_out = P.emit("vector", lambda v, tc=tc, key=key: v.tensor_tensor(
                out=slc("outb", tc * 1024, (tc + 1) * 1024),
                in0=slc("outb", tc * 1024, (tc + 1) * 1024),
                in1=slc("xE" + key, tc * 1024, (tc + 1) * 1024), op=ALU.add),
                wait=[t_sg, ld_x[tc]], inc="EV")
            t_od = P.emit("gpsimd", lambda gp, tc=tc, tl=tl: gp.dma_start(
                out=out_d[tl * 128:(tl + 1) * 128, :],
                in_=slc("outb", tc * 1024, (tc + 1) * 1024)),
                wait=[t_out], inc="OD", inc_n=16)
            prev2[("sE", key, tc)] = t_sg
            prev2[("xE", key, tc)] = t_out
            prev2[("outd", tc)] = t_od
        prev2[xnbt] = last_mm_it

    return _close(nc, P, semh)


def _close(nc, P, semh):
    waits = tuple((s, v) for s, v in P.counts.items() if v > 0
                  and s in ("LD", "LG", "LS", "OD"))
    P.items.append(("gpsimd", lambda gp: gp.nop(), waits, None))
    with nc.Block() as block:
        P.replay(block, semh)
    return nc


def make_in_maps(inputs):
    x = np.ascontiguousarray(inputs["L0_attn_x"], dtype=np.float32)
    ins = {
        "wlnz": inputs["L0_attn_w_ln_z"], "wdt": inputs["L0_attn_w_dt"],
        "wy": inputs["L0_attn_w_y"], "wyact": inputs["L0_attn_w_yact"],
        "wfc": inputs["L0_attn_w_fc"], "wfca": inputs["L0_attn_w_fcact"],
        "wout": inputs["L0_attn_w_out"], "exps": inputs["L0_ffn_experts"],
        "gw": inputs["L0_ffn_gate_w"][0],
        "blnz": inputs["L0_attn_b_ln_z"], "bdt": inputs["L0_attn_b_dt"],
        "by": inputs["L0_attn_b_y"], "byact": inputs["L0_attn_b_yact"],
        "bfc": inputs["L0_attn_b_fc"], "bfca": inputs["L0_attn_b_fcact"],
        "bout": inputs["L0_attn_b_out"],
        "gsio": inputs["L0_attn_g_sio"], "gffn": inputs["L0_attn_g_ffn"],
        "h0": inputs["L0_attn_h0"],
    }
    ins = {k: np.ascontiguousarray(v, dtype=np.float32) for k, v in ins.items()}
    ins["boutr"] = ins["bout"].reshape(1, D).copy()
    ins["ident"] = np.eye(128, dtype=np.float32)
    ins["identr"] = np.eye(128, dtype=np.float32)
    ins["ones1"] = np.ones((1, 128), dtype=np.float32)
    ins["onesc"] = np.ones((128, 1), dtype=np.float32)
    return [dict(ins, x=x[i]) for i in range(NCORE)]


def kernel(**inputs):
    nc = build_program()
    in_maps = make_in_maps(inputs)
    res = run_bass_kernel_spmd(nc, in_maps, core_ids=list(range(NCORE)))
    out = np.stack([np.asarray(res.results[i]["out"]) for i in range(NCORE)], axis=0)
    return out.astype(np.float32)
